# version 02
# baseline (speedup 1.0000x reference)
import os
import numpy as np
import ml_dtypes
import concourse.bass as bass
import concourse.mybir as mybir
from concourse.bass_utils import run_bass_kernel_spmd
from concourse.hw_specs import get_activation_tables
from contextlib import ExitStack

F32 = mybir.dt.float32
BF16 = mybir.dt.bfloat16
AF = mybir.ActivationFunctionType
ALU = mybir.AluOpType
AX = mybir.AxisListType.X

NCORES = 8
BPC = 2
T = 2048
D = 2048
H = 32
DH = 64
TOK = BPC * T
NT = TOK // 128
CH = 128
NCH = T // CH
E = 8

DEBUG = bool(os.environ.get("MK_DEBUG"))
DEVCORES = int(os.environ.get("MK_CORES", "0"))
PHASES = int(os.environ.get("MK_PHASES", "6"))


def _strip_order():
    sio = np.zeros(D, np.int64)
    for d in range(16):
        h0, h1 = 2 * d, 2 * d + 1
        rows = np.concatenate([
            h0 * 64 + np.arange(0, 64, 2), h1 * 64 + np.arange(0, 64, 2),
            h0 * 64 + 1 + np.arange(0, 64, 2), h1 * 64 + 1 + np.arange(0, 64, 2)])
        sio[d * 128:(d + 1) * 128] = rows
    return sio


def _yarn_tables():
    ROPE_BASE, SCALING, BETA_FAST, BETA_SLOW = 10000.0, 32.0, 1.0, 32.0
    half = (np.arange(0, DH, 2, dtype=np.float32) / DH).astype(np.float32)
    inv_freq = (1.0 / ROPE_BASE ** half).astype(np.float32)
    wavelen = (2.0 * np.pi / inv_freq).astype(np.float32)
    ramp = np.clip((wavelen - BETA_FAST) / (BETA_SLOW - BETA_FAST), 0.0, 1.0).astype(np.float32)
    scale = (1.0 + (SCALING - 1.0) * ramp).astype(np.float32)
    t = np.arange(T, dtype=np.float32)
    freqs = (t[:, None] / scale[None, :] * inv_freq[None, :]).astype(np.float32)
    idx = (2 * np.arange(DH // 2)) % (DH // 2)
    ctab = np.cos(freqs[:, idx]).T.astype(np.float32)
    stab = np.sin(freqs[:, idx]).T.astype(np.float32)
    return ctab, stab


def host_prepare(inp):
    f32 = np.float32
    bfl = ml_dtypes.bfloat16
    sio = _strip_order()
    Wq = np.asarray(inp["L0_attn_Wq"], f32)
    Wk = np.asarray(inp["L0_attn_Wk"], f32)
    Wv = np.asarray(inp["L0_attn_Wv"], f32)
    Wg = np.asarray(inp["L0_attn_Wg"], f32)
    Wo = np.asarray(inp["L0_attn_Wo"], f32)
    d = {}

    def blkT(W, order=None):
        WT = W.T if order is None else np.ascontiguousarray(W.T[:, order])
        return np.ascontiguousarray(WT.reshape(D, 16, 128).transpose(1, 0, 2))

    d["wqT"] = blkT(Wq, sio)
    d["wkT"] = blkT(Wk, sio)
    d["wvT"] = blkT(Wv)

    def blk512(WT):
        return np.ascontiguousarray(WT.reshape(16, 128, 4, 512).transpose(2, 0, 1, 3))

    d["wgT"] = blk512(np.ascontiguousarray(Wg.T))
    d["woT"] = blk512(np.ascontiguousarray(Wo.T))
    Wf = np.asarray(inp["L0_ffn_W"], f32)
    d["wffn"] = np.ascontiguousarray(Wf.reshape(8, 16, 128, 4, 512).transpose(0, 3, 1, 2, 4))

    def small(WT):
        return np.ascontiguousarray(np.asarray(WT, f32).reshape(16, 128, -1))

    d["wbT"] = small(np.asarray(inp["L0_attn_Wb"], f32).T)
    d["wgkT"] = small(np.asarray(inp["L0_attn_Wgk"], f32).T)
    d["wgf"] = small(np.asarray(inp["L0_ffn_Wg"], f32))
    d["fb"] = np.ascontiguousarray(np.asarray(inp["L0_ffn_b"], f32))
    for nm, order in (("q", sio), ("k", sio), ("v", None)):
        w = np.asarray(inp[f"L0_attn_w{nm}_conv"], f32)
        b = np.asarray(inp[f"L0_attn_b{nm}_conv"], f32)
        if order is not None:
            w, b = w[order], b[order]
        d[f"cw_{nm}"] = np.ascontiguousarray(w.reshape(16, 128, 4).transpose(1, 0, 2))
        d[f"cb_{nm}"] = np.ascontiguousarray(b.reshape(16, 128).T)

    def rep(v):
        v = np.asarray(v, f32).ravel()
        return np.ascontiguousarray(np.broadcast_to(v[None, :], (128, v.size)))

    d["bb_rep"] = rep(inp["L0_attn_bb"])
    d["bgk_rep"] = rep(inp["L0_attn_bgk"])
    d["dt_rep"] = rep(inp["L0_attn_dt_bias"])
    d["alog_rep"] = rep(inp["L0_attn_A_log"])
    d["dD_rep"] = rep(inp["L0_attn_D"])
    d["onorm_rep"] = np.ascontiguousarray(np.broadcast_to(
        np.tile(np.asarray(inp["L0_attn_onorm_w"], f32), H)[None, :], (128, D)))
    ctab, stab = _yarn_tables()
    d["cs_c"] = np.ascontiguousarray(np.tile(ctab, (4, 1)))
    d["cs_s"] = np.ascontiguousarray(np.tile(stab, (4, 1)))
    I = np.eye(128, dtype=f32)
    d["id_bf"] = I.astype(bfl)
    d["id_f32"] = I
    d["i4_bf"] = np.ascontiguousarray(np.tile(I, (1, 4)).astype(bfl))
    d["negi4_bf"] = np.ascontiguousarray(np.tile(-I, (1, 4)).astype(bfl))
    d["umask_bf"] = np.triu(np.full((128, 128), 3.3e38, f32)).astype(bfl)
    d["ones_f32"] = np.ones((1, 128), f32)
    obd = np.zeros((128, 2), f32)
    for r in range(128):
        obd[r, (r // 32) % 2] = 1.0
    d["obd_bf"] = obd.astype(bfl)
    d["obdT_bf"] = np.ascontiguousarray(obd.T.astype(bfl))
    obd64 = np.zeros((128, 2), f32)
    for r in range(128):
        obd64[r, r // 64] = 1.0
    d["obd64_bf"] = obd64.astype(bfl)
    d["obd64T_bf"] = np.ascontiguousarray(obd64.T.astype(bfl))
    d["onc"] = np.ascontiguousarray(
        (np.tile(np.asarray(inp["L0_attn_onorm_w"], f32), 2) * 8.0)[:, None])
    return d


class Sem:
    __slots__ = ("h", "n")

    def __init__(self, h):
        self.h = h
        self.n = 0


class Em:
    ENGS = ("sync", "tensor", "vector", "scalar", "gpsimd")

    def __init__(self, nc, stack):
        self.nc = nc
        self.stack = stack
        self.cmds = {e: [] for e in self.ENGS}
        self.sems = {}

    def sem(self, name):
        if name not in self.sems:
            self.sems[name] = Sem(self.stack.enter_context(self.nc.semaphore(name)))
        return self.sems[name]

    def add(self, eng, fn, inc=None):
        if inc is not None:
            sem, k = inc
            sem.n += k
            self.cmds[eng].append(lambda e, fn=fn, h=sem.h, k=k: fn(e).then_inc(h, k))
        else:
            self.cmds[eng].append(fn)

    def wait(self, eng, sem, val=None):
        v = sem.n if val is None else val
        if v > 0:
            self.cmds[eng].append(lambda e, h=sem.h, v=v: e.wait_ge(h, v))

    def relay(self, eng, waitsem, val, incsem):
        self.wait(eng, waitsem, val)
        incsem.n += 1
        self.cmds[eng].append(lambda e, h=incsem.h: e.sem_inc(h, 1))
        return incsem.n

    def dma(self, eng, out, in_, sem, transpose=False):
        if transpose:
            self.add(eng, lambda e, o=out, i=in_: e.dma_start_transpose(o, i), inc=(sem, 16))
        else:
            self.add(eng, lambda e, o=out, i=in_: e.dma_start(out=o, in_=i), inc=(sem, 16))
        return sem.n

    def play(self, name):
        with self.nc.Block(name) as block:
            for eng in self.ENGS:
                dec = getattr(block, eng)
                cl = self.cmds[eng]

                def body(e, cl=cl):
                    for fn in cl:
                        fn(e)
                dec(body)
        self.cmds = {e: [] for e in self.ENGS}


def MM(out, lhsT, rhs, start=True, stop=True, tile_position=None):
    return lambda e: e.matmul(out, lhsT=lhsT, rhs=rhs, start=start, stop=stop,
                              skip_group_check=True, tile_position=tile_position)


def TR(out, in_, ident):
    return lambda e: e.transpose(out, in_, ident)


def CP(out, in_):
    return lambda e: e.tensor_copy(out, in_)


def TT(out, a, b, op):
    return lambda e: e.tensor_tensor(out=out, in0=a, in1=b, op=op)


def TS(out, a, s1, s2, op0, op1=ALU.bypass):
    return lambda e: e.tensor_scalar(out=out, in0=a, scalar1=s1, scalar2=s2, op0=op0, op1=op1)


def STT(out, a, s, b, op0, op1):
    return lambda e: e.scalar_tensor_tensor(out=out, in0=a, scalar=s, in1=b, op0=op0, op1=op1)


def ACT(out, in_, func, bias=None, scale=None):
    kw = {}
    if bias is not None:
        kw["bias"] = bias
    if scale is not None:
        kw["scale"] = scale
    return lambda e: e.activation(out=out, in_=in_, func=func, **kw)


def MEMSET(ap, v):
    return lambda e: e.memset(ap, v)


def RED(out, in_, op, negate=False):
    return lambda e: e.tensor_reduce(out=out, in_=in_, axis=AX, op=op, negate=negate)


def SCAN(out, d0, d1, init, op0, op1):
    return lambda e: e.tensor_tensor_scan(out=out, data0=d0, data1=d1, initial=init,
                                          op0=op0, op1=op1)


def build_program():
    nc = bass.Bass("TRN2", target_bir_lowering=False)
    stack = ExitStack()
    em = Em(nc, stack)
    f, bf = F32, BF16

    din = lambda name, shape, dt: nc.dram_tensor(name, shape, dt, kind="ExternalInput").ap()
    dout_ = lambda name, shape, dt: nc.dram_tensor(name, shape, dt, kind="ExternalOutput").ap()
    dint = lambda name, shape, dt: nc.dram_tensor(
        name, shape, dt, kind=("ExternalOutput" if DEBUG else "Internal")).ap()

    x_in = din("x", [TOK, D], f)
    wqT = din("wqT", [16, D, 128], f)
    wkT = din("wkT", [16, D, 128], f)
    wvT = din("wvT", [16, D, 128], f)
    wgT = din("wgT", [4, 16, 128, 512], f)
    woT = din("woT", [4, 16, 128, 512], f)
    wffn = din("wffn", [8, 4, 16, 128, 512], f)
    wbT = din("wbT", [16, 128, 32], f)
    wgkT = din("wgkT", [16, 128, 32], f)
    wgf = din("wgf", [16, 128, 8], f)
    fb = din("fb", [8, 2048], f)
    cw = {p: din(f"cw_{p}", [128, 16, 4], f) for p in "qkv"}
    cb = {p: din(f"cb_{p}", [128, 16], f) for p in "qkv"}
    bb_rep = din("bb_rep", [128, 32], f)
    bgk_rep = din("bgk_rep", [128, 32], f)
    dt_rep = din("dt_rep", [128, 32], f)
    alog_rep = din("alog_rep", [128, 32], f)
    dD_rep = din("dD_rep", [128, 32], f)
    onorm_rep = din("onorm_rep", [128, D], f)
    cs_c = din("cs_c", [128, T], f)
    cs_s = din("cs_s", [128, T], f)
    id_bf = din("id_bf", [128, 128], bf)
    id_f32 = din("id_f32", [128, 128], f)
    i4_bf = din("i4_bf", [128, 512], bf)
    negi4_bf = din("negi4_bf", [128, 512], bf)
    umask_bf = din("umask_bf", [128, 128], bf)
    ones_f32 = din("ones_f32", [1, 128], f)
    obd_bf = din("obd_bf", [128, 2], bf)
    obdT_bf = din("obdT_bf", [2, 128], bf)
    obd64_bf = din("obd64_bf", [128, 2], bf)
    obd64T_bf = din("obd64T_bf", [2, 128], bf)
    onc = din("onc", [128, 1], f)

    out_d = dout_("out", [TOK, D], f)
    xnT_d = dint("xnT_d", [128, 16, TOK], bf)
    qT_d = dint("qT_d", [D, TOK], bf)
    kT_d = dint("kT_d", [D, TOK], bf)
    v_d = dint("v_d", [TOK, D], bf)
    o_d = dint("o_d", [TOK, D], bf)
    s_d = dint("s_d", [TOK, D], f)
    dbg_d = dint("dbg_d", [128, NT * 72], f)
    wq_bf = dint("wq_bf", [16, D, 128], bf)
    wk_bf = dint("wk_bf", [16, D, 128], bf)
    wv_bf = dint("wv_bf", [16, D, 128], bf)
    wg_bf = dint("wg_bf", [4, 16, 128, 512], bf)
    wo_bf = dint("wo_bf", [4, 16, 128, 512], bf)
    wf_bf = dint("wf_bf", [8, 4, 16, 128, 512], bf)
    bounce_cg = dint("bounce_cg", [2, 32, 128], f)
    bounce_bl = dint("bounce_bl", [2, 1, 32], f)

    dS = em.sem("dS")
    dG = em.sem("dG")
    dConst = em.sem("dConst")
    dX = [em.sem("dX0"), em.sem("dX1")]
    dSpX = [em.sem("dSpX0"), em.sem("dSpX1")]
    dCqkv = em.sem("dCqkv")
    dCgw = em.sem("dCgw")
    dCff = em.sem("dCff")
    dXns = em.sem("dXns")
    dW = [em.sem("dW0"), em.sem("dW1")]
    dFin = em.sem("dFin")
    dVsp = em.sem("dVsp")
    dT = [em.sem("dT0"), em.sem("dT1")]
    dPr = [em.sem("dPr0"), em.sem("dPr1")]
    dOsp = [em.sem("dOsp0"), em.sem("dOsp1")]
    sTE = em.sem("sTE")
    rlyA = em.sem("rlyA")
    rlyG = em.sem("rlyG")
    sDV = em.sem("sDV")
    sAC = em.sem("sAC")
    sGP = em.sem("sGP")

    sb = lambda name, shape, dt: stack.enter_context(nc.sbuf_tensor(name, shape, dt))
    stack2 = ExitStack()
    sb2 = lambda name, shape, dt: stack2.enter_context(nc.sbuf_tensor(name, shape, dt))
    c_dD = sb("c_dD", [128, 32], f)
    c_idbf = sb("c_idbf", [128, 128], bf)
    c_idf = sb("c_idf", [128, 128], f)
    c_i4 = sb("c_i4", [128, 512], bf)
    c_negi4 = sb("c_negi4", [128, 512], bf)
    c_umask = sb("c_umask", [128, 128], bf)
    c_ones = sb("c_ones", [1, 128], f)
    c_obd = sb("c_obd", [128, 2], bf)
    c_obdT = sb("c_obdT", [2, 128], bf)
    c_eps5 = sb("c_eps5", [128, 1], f)
    c_eps6 = sb("c_eps6", [128, 1], f)
    c_eps6r = sb("c_eps6r", [128, 1], f)
    c_one = sb("c_one", [128, 1], f)
    c_cw = {p: sb(f"c_cw_{p}", [128, 16, 4], f) for p in "qkv"}
    c_cb = {p: sb(f"c_cb_{p}", [128, 16], f) for p in "qkv"}
    p_comb = sb("p_comb", [128, NT * 8], f)
    c_obd64 = sb("c_obd64", [128, 2], bf)
    c_obd64T = sb("c_obd64T", [2, 128], bf)
    c_onc = sb("c_onc", [128, 1], f)
    p_beta = sb2("p_beta", [128, NT * 32], f)
    p_negb = sb2("p_negb", [128, NT * 32], f)
    p_g = sb2("p_g", [128, NT * 32], f)
    p_S = [[sb2(f"p_S_{b}_{pp}", [128, 16 * 64], f) for pp in range(2)] for b in range(BPC)]
    p_Sbf = [sb2(f"p_Sbf_{b}", [128, 16 * 64], bf) for b in range(BPC)]
    c_csc = sb2("c_csc", [128, T], f)
    c_css = sb2("c_css", [128, T], f)

    names = list(get_activation_tables(nc.m.arch).keys())
    set_id = names.index("natural_log_exp_and_others")

    ph = ExitStack()
    sbl = lambda name, shape, dt: ph.enter_context(nc.sbuf_tensor(name, shape, dt))
    psl = lambda name, shape, dt: ph.enter_context(nc.psum_tensor(name, shape, dt))

    c_wbT = sbl("c_wbT", [128, 16, 32], f)
    c_wgkT = sbl("c_wgkT", [128, 16, 32], f)
    c_wgf = sbl("c_wgf", [128, 16, 8], f)
    c_bb = sbl("c_bb", [128, 32], f)
    c_bgkdt = sbl("c_bgkdt", [128, 32], f)
    c_nexpA = sbl("c_nexpA", [128, 32], f)
    s_bgk = sbl("s_bgk", [128, 32], f)
    s_dt = sbl("s_dt", [128, 32], f)
    s_alog = sbl("s_alog", [128, 32], f)

    for dst, src in [(c_wbT[:], wbT.rearrange("kb p n -> p kb n")),
                     (c_wgkT[:], wgkT.rearrange("kb p n -> p kb n")),
                     (c_wgf[:], wgf.rearrange("kb p n -> p kb n")),
                     (c_bb[:], bb_rep), (c_dD[:], dD_rep),
                     (c_csc[:], cs_c), (c_css[:], cs_s),
                     (c_idbf[:], id_bf), (c_idf[:], id_f32), (c_i4[:], i4_bf),
                     (c_negi4[:], negi4_bf), (c_umask[:], umask_bf), (c_ones[:], ones_f32),
                     (c_obd[:], obd_bf), (c_obdT[:], obdT_bf),
                     (c_obd64[:], obd64_bf), (c_obd64T[:], obd64T_bf), (c_onc[:], onc),
                     (s_bgk[:], bgk_rep), (s_dt[:], dt_rep), (s_alog[:], alog_rep)]:
        em.dma("sync", dst, src, dConst)
    for p in "qkv":
        em.dma("sync", c_cw[p][:], cw[p], dConst)
        em.dma("sync", c_cb[p][:], cb[p], dConst)
    n_const = dConst.n

    for dst, src in [(wq_bf, wqT), (wk_bf, wkT), (wv_bf, wvT)]:
        em.dma("gpsimd", dst, src, dCqkv)
    cast_qkv = dCqkv.n

    def _preload(e, sid=set_id):
        return e.add_instruction(mybir.InstLoadActFuncSet(
            name=nc.get_next_instruction_name(), act_func_set_id=sid, ins=[], outs=[]))
    em.add("scalar", _preload)

    em.add("vector", MEMSET(c_eps5[:], 1e-5))
    em.add("vector", MEMSET(c_eps6[:], 1e-6))
    em.add("vector", MEMSET(c_eps6r[:], 64e-6))
    em.add("vector", MEMSET(c_one[:], 1.0), inc=(sDV, 1))
    em.wait("vector", dConst, n_const)
    em.add("vector", TT(c_bgkdt[:], s_bgk[:], s_dt[:], ALU.add), inc=(sDV, 1))
    bgkdt_done = sDV.n
    em.wait("scalar", dConst, n_const)
    em.add("scalar", ACT(c_nexpA[:], s_alog[:], AF.Exp))


    s_x = [sbl(f"s_x{i}", [128, D], f) for i in range(2)]
    s_xnbf = [sbl(f"s_xnbf{i}", [128, D], bf) for i in range(2)]
    s_xnf = [sbl(f"s_xnf{i}", [128, D], f) for i in range(2)]
    s_xTf = [sbl(f"s_xTf{i}", [128, 16 * 128], f) for i in range(2)]
    s_hi = [sbl(f"s_hi{i}", [128, 16, 1024], bf) for i in range(2)]
    s_bnst = [sbl(f"s_bnst{i}", [128, 24], f) for i in range(2)]
    s_mv = [sbl(f"s_mv{i}", [128, 2], f) for i in range(2)]
    s_rstd = [sbl(f"s_rstd{i}", [128, 1], f) for i in range(2)]
    s_mb = [sbl(f"s_mb{i}", [128, 1], f) for i in range(2)]
    s_sm = sbl("s_sm", [128, 8], f)
    s_e8 = sbl("s_e8", [128, 8], f)
    s_msk = sbl("s_msk", [128, 8], f)
    s_t32 = sbl("s_t32", [128, 32], f)
    s_t32b = sbl("s_t32b", [128, 32], f)
    s_t32c = sbl("s_t32c", [128, 32], f)
    ps_tbf = [psl(f"ps_tbf{i}", [128, 512], bf) for i in range(2)]
    ps_tf = [psl(f"ps_tf{i}", [128, 512], f) for i in range(2)]
    ps_small = psl("ps_small", [128, 512], f)

    x_ld = [0] * NT
    xn_done = [0] * NT
    tile_ev = [0] * NT
    small_done = [0] * NT
    epi_done = [0] * NT
    spill = {}

    for i in range(NT):
        par = i % 2
        seg = i // 8
        if i >= 2:
            em.wait("sync", sAC, xn_done[i - 2])
        x_ld[i] = em.dma("sync", s_x[par][:], x_in[i * 128:(i + 1) * 128, :], dX[par])
        em.wait("vector", dX[par], x_ld[i])
        for j in range(4):
            em.add("vector", lambda e, j=j, par=par: e.bn_stats(
                s_bnst[par][:, j * 6:(j + 1) * 6], s_x[par][:, j * 512:(j + 1) * 512]),
                   inc=(sDV, 1) if j == 3 else None)
        stats_dv = sDV.n
        rv = em.relay("scalar", sDV, stats_dv, rlyA)
        em.wait("vector", rlyA, rv)
        em.add("vector", lambda e, par=par: e.bn_aggr(s_mv[par][:], s_bnst[par][:]),
               inc=(sDV, 1))
        bn_done = sDV.n
        em.wait("scalar", sDV, bn_done)
        em.add("scalar", ACT(s_rstd[par][:], s_mv[par][:, 1:2], AF.Ln, bias=c_eps5[:, 0:1]))
        em.add("scalar", lambda e, par=par: e.activation(
            out=s_rstd[par][:], in_=s_rstd[par][:], func=AF.Exp, scale=-0.5), inc=(sAC, 1))
        rstd_done = sAC.n
        em.wait("vector", sAC, rstd_done)
        em.add("vector", lambda e, par=par: e.scalar_tensor_tensor(
            out=s_mb[par][:], in0=s_mv[par][:, 0:1], scalar=-1.0, in1=s_rstd[par][:],
            op0=ALU.mult, op1=ALU.mult), inc=(sDV, 1))
        mb_done = sDV.n
        em.wait("scalar", sDV, mb_done)
        if i >= 2:
            em.wait("scalar", sTE, small_done[i - 2])
        em.add("scalar", lambda e, par=par: e.activation(
            out=s_xnbf[par][:], in_=s_x[par][:], func=AF.Identity,
            bias=s_mb[par][:, 0:1], scale=s_rstd[par][:, 0:1]))
        em.add("scalar", lambda e, par=par: e.activation(
            out=s_xnf[par][:], in_=s_x[par][:], func=AF.Identity,
            bias=s_mb[par][:, 0:1], scale=s_rstd[par][:, 0:1]), inc=(sAC, 1))
        xn_done[i] = sAC.n
        em.wait("tensor", sAC, xn_done[i])
        if i >= 1:
            em.wait("tensor", sDV, tile_ev[i - 1])
        segcol = (i % 8) * 128
        if i % 8 == 0 and seg >= 2:
            em.wait("vector", dSpX[seg % 2], spill[seg - 2])
        if i >= 2:
            em.wait("vector", sTE, small_done[i - 2])
        for half in range(2):
            for j8 in range(8):
                j = half * 8 + j8
                bank = ps_tbf[j8 // 4]
                em.add("tensor", TR(bank[:, (j8 % 4) * 128:(j8 % 4 + 1) * 128],
                                    s_xnbf[par][:, j * 128:(j + 1) * 128], c_idbf[:]),
                       inc=(sTE, 1) if j8 == 7 else None)
            tb = sTE.n
            for j8 in range(8):
                j = half * 8 + j8
                bank = ps_tf[j8 // 4]
                em.add("tensor", TR(bank[:, (j8 % 4) * 128:(j8 % 4 + 1) * 128],
                                    s_xnf[par][:, j * 128:(j + 1) * 128], c_idf[:]),
                       inc=(sTE, 1) if j8 == 7 else None)
            tf_ = sTE.n
            em.wait("vector", sTE, tb)
            for q in range(2):
                j0 = half * 8 + q * 4
                em.add("vector", lambda e, q=q, j0=j0, seg=seg, segcol=segcol: e.tensor_copy(
                    s_hi[seg % 2][:, j0:j0 + 4, segcol:segcol + 128], ps_tbf[q][:]))
            em.wait("vector", sTE, tf_)
            for q in range(2):
                j0 = half * 8 + q * 4
                em.add("vector", lambda e, q=q, j0=j0, par=par: e.tensor_copy(
                    s_xTf[par][:, j0 * 128:(j0 + 4) * 128], ps_tf[q][:]),
                       inc=(sDV, 1) if q == 1 else None)
            if half == 0:
                em.wait("tensor", sDV, sDV.n)
        tile_ev[i] = sDV.n
        if i % 8 == 7:
            em.wait("sync", sDV, tile_ev[i])
            spill[seg] = em.dma("sync", xnT_d[:, :, seg * 1024:(seg + 1) * 1024],
                                s_hi[seg % 2][:], dSpX[seg % 2])
        em.wait("tensor", sDV, tile_ev[i])
        if i >= 1:
            em.wait("tensor", sDV, epi_done[i - 1])
        for j in range(16):
            em.add("tensor", MM(ps_small[:, 0:32], s_xTf[par][:, j * 128:(j + 1) * 128],
                                c_wbT[:, j, :], start=(j == 0), stop=(j == 15)))
        for j in range(16):
            em.add("tensor", MM(ps_small[:, 32:64], s_xTf[par][:, j * 128:(j + 1) * 128],
                                c_wgkT[:, j, :], start=False, stop=(j == 15)))
        for j in range(16):
            em.add("tensor", MM(ps_small[:, 64:72], s_xTf[par][:, j * 128:(j + 1) * 128],
                                c_wgf[:, j, :], start=False, stop=(j == 15)),
                   inc=(sTE, 1) if j == 15 else None)
        small_done[i] = sTE.n
        em.wait("vector", sTE, small_done[i])
        em.wait("vector", sDV, bgkdt_done)
        em.add("vector", TT(s_t32[:], ps_small[:, 0:32], c_bb[:], ALU.add))
        em.add("vector", TT(s_t32b[:], ps_small[:, 32:64], c_bgkdt[:], ALU.add),
               inc=(sDV, 1))
        raw_done = sDV.n
        em.wait("scalar", sDV, raw_done)
        em.add("scalar", ACT(s_t32c[:], s_t32[:], AF.Exp, scale=-1.0))
        em.add("scalar", ACT(s_t32c[:], s_t32c[:], AF.Ln, bias=c_one[:, 0:1]))
        em.add("scalar", ACT(p_beta[:, i * 32:(i + 1) * 32], s_t32c[:], AF.Exp, scale=-1.0),
               inc=(sAC, 1))
        beta_done = sAC.n
        em.add("scalar", ACT(s_t32[:], s_t32b[:], AF.Abs))
        em.add("scalar", ACT(s_t32[:], s_t32[:], AF.Exp, scale=-1.0))
        em.add("scalar", ACT(s_t32[:], s_t32[:], AF.Ln, bias=c_one[:, 0:1]), inc=(sAC, 1))
        spl_done = sAC.n
        em.wait("vector", sAC, spl_done)
        em.add("vector", TS(s_t32c[:], s_t32b[:], 0.0, 0.0, ALU.max))
        em.add("vector", RED(s_sm[:, 0:1], ps_small[:, 64:72], ALU.max, negate=True))
        em.add("vector", TS(p_negb[:, i * 32:(i + 1) * 32], p_beta[:, i * 32:(i + 1) * 32],
                            -1.0, 0.0, ALU.mult), inc=(sDV, 1))
        red_dv = sDV.n
        rv = em.relay("scalar", sDV, red_dv, rlyA)
        em.wait("vector", rlyA, rv)
        em.add("vector", TT(p_g[:, i * 32:(i + 1) * 32], s_t32[:], s_t32c[:], ALU.add))
        em.add("vector", TS(s_msk[:], ps_small[:, 64:72], s_sm[:, 0:1], 0.0,
                            ALU.add, ALU.is_ge), inc=(sDV, 1))
        nm_done = sDV.n
        em.wait("scalar", sDV, nm_done)
        em.add("scalar", ACT(s_e8[:], ps_small[:, 64:72], AF.Exp, bias=s_sm[:, 0:1]),
               inc=(sAC, 1))
        e8_done = sAC.n
        em.wait("vector", sAC, e8_done)
        em.add("vector", RED(s_sm[:, 1:2], s_e8[:], ALU.add), inc=(sDV, 1))
        ssum_done = sDV.n
        em.wait("scalar", sDV, ssum_done)
        em.add("scalar", ACT(s_sm[:, 2:3], s_sm[:, 1:2], AF.Ln))
        em.add("scalar", ACT(s_sm[:, 3:4], s_sm[:, 2:3], AF.Exp, scale=-1.0), inc=(sAC, 1))
        rs_done = sAC.n
        em.wait("vector", sAC, rs_done)
        em.add("vector", STT(p_comb[:, i * 8:(i + 1) * 8], s_e8[:], s_sm[:, 3:4], s_msk[:],
                             ALU.mult, ALU.mult), inc=(sDV, 1))
        epi_done[i] = sDV.n

    em.wait("gpsimd", sAC, xn_done[8])
    for dst, src in [(wg_bf, wgT), (wo_bf, woT)]:
        em.dma("gpsimd", dst, src, dCgw)
    for e_ in range(E):
        em.dma("gpsimd", wf_bf[e_], wffn[e_], dCff)
    cast_gwo = dCgw.n
    cast_all = dCff.n

    em.add("scalar", ACT(c_nexpA[:], s_alog[:], AF.Exp))
    em.add("scalar", lambda e: e.activation(out=c_nexpA[:], in_=c_nexpA[:],
                                            func=AF.Identity, scale=-1.0), inc=(sAC, 1))
    nexpA_done = sAC.n
    em.wait("vector", sAC, nexpA_done)
    for i in range(NT):
        em.add("vector", TT(p_g[:, i * 32:(i + 1) * 32], p_g[:, i * 32:(i + 1) * 32],
                            c_nexpA[:], ALU.mult), inc=(sDV, 1) if i == NT - 1 else None)
    if DEBUG:
        em.wait("sync", sDV, sDV.n)
        em.wait("sync", sAC, sAC.n)
        for nm, src, w in (("beta", p_beta, 32), ("g", p_g, 32), ("comb", p_comb, 8)):
            off = {"beta": 0, "g": NT * 32, "comb": NT * 64}[nm]
            em.dma("sync", dbg_d[:, off:off + NT * w], src[:], dS)
    em.wait("sync", dS, dS.n)
    for sm_ in (dConst, dX[0], dX[1], dSpX[0], dSpX[1]):
        em.wait("sync", sm_, sm_.n)
    em.wait("sync", sDV, sDV.n)
    em.wait("sync", sAC, sAC.n)
    em.wait("sync", sTE, sTE.n)
    em.play("p1")
    ph.close()

    if PHASES >= 3:
        ph = ExitStack()
        sbl = lambda name, shape, dt: ph.enter_context(nc.sbuf_tensor(name, shape, dt))
        psl = lambda name, shape, dt: ph.enter_context(nc.psum_tensor(name, shape, dt))
        HT = T // 2
        s_xns = sbl("s_xns", [128, 16, HT], bf)
        s_wt = [sbl(f"s_wt{i}", [128, 16, 128], bf) for i in range(2)]
        s_strip = sbl("s_strip", [128, 3 + HT], f)
        s_hist = sbl("s_hist", [128, 48, 3], f)
        s_conv = [sbl(f"s_conv{i}", [128, HT], f) for i in range(2)]
        s_act = sbl("s_act", [128, HT], bf)
        s_u = sbl("s_u", [128, HT], f)
        s_v = sbl("s_v", [128, HT], f)
        s_rope = sbl("s_rope", [128, HT], bf)
        s_sq = sbl("s_sq", [128, HT], bf)
        s_ssq = sbl("s_ssq", [2, HT], f)
        s_rsq = sbl("s_rsq", [2, HT], f)
        s_rsqb = sbl("s_rsqb", [2, HT], bf)
        s_fin = sbl("s_fin", [128, HT], bf)
        s_vt = sbl("s_vt", [128, 8, 128], bf)
        ps_pr = [psl(f"ps_pr{i}", [128, 512], f) for i in range(4)]
        ps_ss = ps_pr[0]
        ps_rb = [ps_pr[2], ps_pr[3]]
        ps_vt = [psl(f"ps_vt{i}", [128, 512], bf) for i in range(2)]

        em.wait("gpsimd", dCqkv, cast_qkv)
        em.wait("sync", dCqkv, cast_qkv)
        prev_sg_done = 0
        seq_ld = 0
        prev_fin_dma = 0
        prev_vdma = 0
        w_ld = [0, 0]
        w_ld_te = [0, 0]
        strip_ev = 0
        strip_idx = 0
        for sq_ in range(2):
            for hf in range(2):
                col0 = sq_ * T + hf * HT
                em.wait("sync", sTE, sTE.n)
                for qq in range(4):
                    seq_ld = em.dma("sync", s_xns[:, qq * 4:(qq + 1) * 4, :],
                                    xnT_d[:, qq * 4:(qq + 1) * 4, col0:col0 + HT], dXns)
                for ip, p in enumerate("qkv"):
                    wbf = {"q": wq_bf, "k": wk_bf, "v": wv_bf}[p]
                    for d2 in range(16):
                        hidx = ip * 16 + d2
                        wpar = strip_idx % 2
                        if strip_idx >= 2:
                            em.wait("sync", sTE, w_ld_te[wpar])
                        w_ld[wpar] = em.dma("sync", s_wt[wpar][:],
                                            wbf[d2].rearrange("(kb p) m -> p kb m", kb=16, p=128),
                                            dW[wpar])
                        em.wait("tensor", dW[wpar], w_ld[wpar])
                        em.wait("tensor", dXns, seq_ld)
                        if strip_idx >= 1:
                            em.wait("tensor", sDV, strip_ev)
                        for tn in range(2):
                            for kb in range(16):
                                em.add("tensor", MM(ps_pr[tn][:],
                                                    s_wt[wpar][:, kb, :],
                                                    s_xns[:, kb, tn * 512:(tn + 1) * 512],
                                                    start=(kb == 0), stop=(kb == 15)),
                                       inc=(sTE, 1) if kb == 15 else None)
                        proj_done = sTE.n
                        w_ld_te[wpar] = proj_done
                        em.wait("vector", sTE, proj_done)
                        em.wait("vector", sAC, prev_sg_done)
                        if hf == 0:
                            em.add("vector", MEMSET(s_strip[:, 0:3], 0.0))
                        else:
                            em.add("vector", CP(s_strip[:, 0:3], s_hist[:, hidx, :]))
                        for tn in range(2):
                            em.add("vector", CP(s_strip[:, 3 + tn * 512:3 + (tn + 1) * 512],
                                                ps_pr[tn][:]),
                                   inc=(sDV, 1) if tn == 1 else None)
                        strip_ev2 = sDV.n
                        rv = em.relay("scalar", sDV, strip_ev2, rlyA)
                        em.wait("vector", rlyA, rv)
                        cwp, cbp = c_cw[p], c_cb[p]
                        em.add("vector", TS(s_conv[0][:], s_strip[:, 0:HT],
                                            cwp[:, d2, 0:1], cbp[:, d2:d2 + 1], ALU.mult, ALU.add))
                        em.add("vector", STT(s_conv[1][:], s_strip[:, 1:1 + HT], cwp[:, d2, 1:2],
                                             s_conv[0][:], ALU.mult, ALU.add))
                        em.add("vector", STT(s_conv[0][:], s_strip[:, 2:2 + HT], cwp[:, d2, 2:3],
                                             s_conv[1][:], ALU.mult, ALU.add))
                        em.add("vector", STT(s_conv[1][:], s_strip[:, 3:3 + HT], cwp[:, d2, 3:4],
                                             s_conv[0][:], ALU.mult, ALU.add), inc=(sDV, 1))
                        conv_done = sDV.n
                        if hf == 0:
                            em.add("vector", CP(s_hist[:, hidx, :], s_strip[:, HT:HT + 3]))
                        em.wait("scalar", sDV, conv_done)
                        em.add("scalar", ACT(s_conv[0][:], s_conv[1][:], AF.Exp, scale=-1.0))
                        em.add("scalar", ACT(s_strip[:, 0:HT], s_conv[0][:], AF.Ln,
                                             bias=c_one[:, 0:1]))
                        em.add("scalar", ACT(s_conv[0][:], s_strip[:, 0:HT], AF.Exp,
                                             scale=-1.0), inc=(sAC, 1))
                        sg_done = sAC.n
                        em.wait("vector", sAC, sg_done)
                        em.add("vector", TT(s_act[:], s_conv[1][:], s_conv[0][:], ALU.mult),
                               inc=(sDV, 1))
                        act_done = sDV.n
                        if p in "qk":
                            csl = slice(hf * HT, (hf + 1) * HT)
                            em.add("vector", TT(s_u[:], s_act[:], c_csc[:, csl], ALU.mult))
                            em.add("vector", TT(s_v[0:64, :], s_act[64:128, :],
                                                c_css[64:128, csl], ALU.mult))
                            em.add("vector", TT(s_v[64:128, :], s_act[0:64, :],
                                                c_css[0:64, csl], ALU.mult), inc=(sDV, 1))
                            uv_dv = sDV.n
                            rv = em.relay("scalar", sDV, uv_dv, rlyA)
                            em.wait("vector", rlyA, rv)
                            em.add("vector", TT(s_rope[0:64, :], s_u[0:64, :], s_v[0:64, :],
                                                ALU.subtract))
                            em.add("vector", TT(s_rope[64:128, :], s_u[64:128, :], s_v[64:128, :],
                                                ALU.add), inc=(sDV, 1))
                            rope_done = sDV.n
                            em.wait("scalar", sDV, rope_done)
                            em.add("scalar", ACT(s_sq[:], s_rope[:], AF.Square), inc=(sAC, 1))
                            sq_done = sAC.n
                            em.wait("tensor", sAC, sq_done)
                            for tn in range(2):
                                em.add("tensor", MM(ps_ss[0:2, 0:512],
                                                    c_obd[:], s_sq[:, tn * 512:(tn + 1) * 512]),
                                       inc=(sTE, 1))
                                ss_te = sTE.n
                                em.wait("vector", sTE, ss_te)
                                em.add("vector", CP(s_ssq[:, tn * 512:(tn + 1) * 512],
                                                    ps_ss[0:2, 0:512]), inc=(sDV, 1))
                                if tn < 1:
                                    em.wait("tensor", sDV, sDV.n)
                            ssq_done = sDV.n
                            em.wait("scalar", sDV, ssq_done)
                            em.add("scalar", ACT(s_rsq[:], s_ssq[:], AF.Ln,
                                                 bias=c_eps6[0:2, 0:1]))
                            em.add("scalar", ACT(s_rsq[:], s_rsq[:], AF.Exp, scale=-0.5),
                                   inc=(sAC, 1))
                            rsq_done = sAC.n
                            em.wait("vector", sAC, rsq_done)
                            em.add("vector", CP(s_rsqb[:], s_rsq[:]), inc=(sDV, 1))
                            rsqb_done = sDV.n
                            em.wait("tensor", sDV, rsqb_done)
                            em.wait("vector", dFin, prev_fin_dma)
                            rb_ev = [0, 0]
                            for tn in range(2):
                                em.add("tensor", MM(ps_rb[tn % 2][:], c_obdT[:],
                                                    s_rsqb[:, tn * 512:(tn + 1) * 512]),
                                       inc=(sTE, 1))
                                rb_te = sTE.n
                                em.wait("vector", sTE, rb_te)
                                em.add("vector", STT(s_fin[:, tn * 512:(tn + 1) * 512],
                                                     ps_rb[tn % 2][:], 1.0,
                                                     s_rope[:, tn * 512:(tn + 1) * 512],
                                                     ALU.mult, ALU.mult),
                                       inc=(sDV, 1))
                                rb_ev[tn % 2] = sDV.n
                            fin_ev = sDV.n
                            em.wait("sync", sDV, fin_ev)
                            tgt = (qT_d if p == "q" else kT_d)
                            dst = tgt.rearrange("(d b a r) t -> d a b r t",
                                                d=16, b=2, a=2, r=32)
                            em.dma("sync", dst[d2, 0, :, :, col0:col0 + HT],
                                   s_fin[0:64, :], dFin)
                            prev_fin_dma = em.dma(
                                "sync", dst[d2, 1, :, :, col0:col0 + HT],
                                s_fin[64:128, :], dFin)
                        else:
                            em.wait("tensor", sDV, act_done)
                            em.wait("vector", dVsp, prev_vdma)
                            for j8 in range(8):
                                bank = ps_vt[j8 // 4]
                                em.add("tensor", TR(bank[:, (j8 % 4) * 128:(j8 % 4 + 1) * 128],
                                                    s_act[:, j8 * 128:(j8 + 1) * 128],
                                                    c_idbf[:]),
                                       inc=(sTE, 1) if j8 == 7 else None)
                            vt_te = sTE.n
                            em.wait("vector", sTE, vt_te)
                            for q in range(2):
                                em.add("vector", lambda e, q=q: e.tensor_copy(
                                    s_vt[:, q * 4:q * 4 + 4, :], ps_vt[q][:]),
                                       inc=(sDV, 1) if q == 1 else None)
                            vt_ev = sDV.n
                            em.wait("gpsimd", sDV, vt_ev)
                            prev_vdma = em.dma(
                                "gpsimd",
                                v_d.rearrange("(u j p) c -> u p j c", u=4, j=8, p=128)[
                                    sq_ * 2 + hf, :, :, d2 * 128:(d2 + 1) * 128],
                                s_vt[:], dVsp)
                        strip_ev = sDV.n
                        strip_idx += 1
        for sm_ in (dXns, dW[0], dW[1], dFin):
            em.wait("sync", sm_, sm_.n)
        em.wait("sync", sDV, sDV.n)
        em.wait("gpsimd", dVsp, dVsp.n)
        em.play("p3")
        ph.close()

    if PHASES >= 4:
        ph = ExitStack()
        sbl = lambda name, shape, dt: ph.enter_context(nc.sbuf_tensor(name, shape, dt))
        psl = lambda name, shape, dt: ph.enter_context(nc.psum_tensor(name, shape, dt))
        B0 = psl("p4_B0", [128, 512], f)
        B1 = psl("p4_B1", [128, 512], f)
        B2 = psl("p4_B2", [128, 512], f)
        B3 = psl("p4_B3", [128, 512], bf)
        B4 = psl("p4_B4", [128, 512], f)
        B5 = psl("p4_B5", [128, 512], f)
        B6 = psl("p4_B6", [128, 512], f)
        B7 = psl("p4_B7", [128, 512], f)
        s_qc = [sbl(f"s_qc{i}", [128, 16, 128], bf) for i in range(2)]
        s_kc = [sbl(f"s_kc{i}", [128, 16, 128], bf) for i in range(2)]
        s_kt = [sbl(f"s_kt{i}", [128, 2048], bf) for i in range(2)]
        s_vk = [sbl(f"s_vk{i}", [128, 2048], bf) for i in range(2)]
        s_oa = [sbl(f"s_oa{i}", [128, 2048], bf) for i in range(2)]
        s_gT = [sbl(f"s_gT{i}", [32, 128], f) for i in range(2)]
        s_cgT = [sbl(f"s_cgT{i}", [32, 128], f) for i in range(2)]
        s_cgtok = [sbl(f"s_cgtok{i}", [128, 32], f) for i in range(2)]
        s_negcg = [sbl(f"s_negcg{i}", [128, 32], f) for i in range(2)]
        s_ecg = [sbl(f"s_ecg{i}", [128, 32], f) for i in range(2)]
        s_cgf = [sbl(f"s_cgf{i}", [1, 4096], f) for i in range(2)]
        s_blr = [sbl(f"s_blr{i}", [1, 32], f) for i in range(2)]
        s_BL = [sbl(f"s_BL{i}", [128, 32], f) for i in range(2)]
        s_kkr = [sbl(f"s_kkr{i}", [128, 512], bf) for i in range(2)]
        s_qkr = [sbl(f"s_qkr{i}", [128, 512], bf) for i in range(2)]
        s_dtr = [sbl(f"s_dtr{i}", [128, 512], bf) for i in range(2)]
        s_dt4 = [sbl(f"s_dt4{i}", [128, 512], bf) for i in range(2)]
        s_D4 = [sbl(f"s_D4{i}", [128, 512], bf) for i in range(2)]
        s_bn4 = [sbl(f"s_bn4{i}", [128, 512], bf) for i in range(2)]
        s_bt4 = [sbl(f"s_bt4{i}", [128, 512], bf) for i in range(2)]
        s_pw = [[sbl(f"s_pw{i}_{j}", [128, 512], bf) for j in range(2)] for i in range(2)]
        s_tp = [[sbl(f"s_tp{i}_{j}", [128, 512], bf) for j in range(2)] for i in range(2)]
        s_hh = [[sbl(f"s_hh{i}_{j}", [128, 512], bf) for j in range(2)] for i in range(2)]
        s_U4 = [sbl(f"s_U4{i}", [128, 256], bf) for i in range(2)]
        s_Ub4 = [sbl(f"s_Ub4{i}", [128, 256], bf) for i in range(2)]
        s_R4 = [sbl(f"s_R4{i}", [128, 256], bf) for i in range(2)]
        s_r14 = [sbl(f"s_r14{i}", [128, 256], f) for i in range(2)]
        s_o14 = [sbl(f"s_o14{i}", [128, 256], f) for i in range(2)]
        s_w4 = [sbl(f"s_w4_{i}", [128, 4], f) for i in range(2)]
        s_o24 = [sbl(f"s_o24{i}", [128, 256], f) for i in range(2)]

        for b_ in range(BPC):
            em.add("vector", MEMSET(p_S[b_][0][:], 0.0))
            em.add("vector", MEMSET(p_S[b_][1][:], 0.0))
            em.add("vector", MEMSET(p_Sbf[b_][:], 0.0), inc=(sDV, 1))
        s_zero_dv = sDV.n

        NCHUNKS = int(os.environ.get("MK_P4_CHUNKS", 2 * NCH))
        sched = [(ci % 2, ci // 2) for ci in range(NCHUNKS)]
        prep_dma = [0] * NCHUNKS
        prep_pr = [0] * NCHUNKS
        prep_cgf = [0] * NCHUNKS
        prep_dv = [0] * NCHUNKS
        prep_ac = [0] * NCHUNKS
        ch_lastuse_te = [0] * NCHUNKS
        ch_o_spill = [0] * NCHUNKS
        ch_blend_ac = [0] * NCHUNKS
        ch_o_dv = [0] * NCHUNKS
        st = {"rb_free_ac": 0, "kq_free_dv": 0, "B2_free": 0, "B1s_free_ac": 0,
              "B3_free": 0, "B4_free": 0, "B5_free": 0, "b6_free": 0,
              "scr_free": [0, 0]}

        def emit_prep(ci):
            b, c = sched[ci]
            par = ci % 2
            i_tile = b * NCH + c
            row0 = b * T + c * CH
            if ci >= 2:
                em.wait("sync", sTE, ch_lastuse_te[ci - 2])
                em.wait("sync", sDV, ch_o_dv[ci - 2])
            em.dma("sync", s_qc[par][:],
                   qT_d.rearrange("(hp p) t -> p hp t", hp=16, p=128)[:, :, row0:row0 + 128],
                   dT[par])
            em.dma("sync", s_kc[par][:],
                   kT_d.rearrange("(hp p) t -> p hp t", hp=16, p=128)[:, :, row0:row0 + 128],
                   dT[par])
            em.dma("sync", s_kt[par][:], kT_d[:, row0:row0 + 128], dT[par], transpose=True)
            d1 = em.dma("sync", s_vk[par][:], v_d[row0:row0 + 128, :], dT[par])
            em.wait("tensor", sDV, st["b6_free"])
            em.add("tensor", TR(B6[0:32, 0:128], p_g[:, i_tile * 32:(i_tile + 1) * 32],
                                c_idf[:]), inc=(sTE, 1))
            gT_te = sTE.n
            em.wait("vector", sTE, gT_te)
            em.add("vector", CP(s_gT[par][:], B6[0:32, 0:128]))
            em.add("vector", MEMSET(s_cgT[par][:], 0.0))
            em.add("vector", SCAN(s_cgT[par][:], s_gT[par][:], s_gT[par][:], 0.0,
                                  ALU.add, ALU.bypass), inc=(sDV, 1))
            scan_dv = sDV.n
            em.wait("sync", sDV, scan_dv)
            em.dma("sync", bounce_cg[par], s_cgT[par][:], dPr[par])
            em.wait("sync", dPr[par], dPr[par].n)
            em.dma("sync", s_cgf[par][0:1, :],
                   bounce_cg[par].rearrange("h t -> (h t)"), dPr[par])
            cgf_dma = dPr[par].n
            em.wait("tensor", sDV, scan_dv)
            em.add("tensor", TR(B6[0:128, 128:160], s_cgT[par][:], c_idf[0:32, 0:32]), inc=(sTE, 1))
            cgtok_te = sTE.n
            em.wait("vector", sTE, cgtok_te)
            em.add("vector", CP(s_cgtok[par][:], B6[:, 128:160]))
            em.add("vector", TS(s_negcg[par][:], B6[:, 128:160], -1.0, 0.0, ALU.mult),
                   inc=(sDV, 1))
            cg_dv = sDV.n
            em.wait("scalar", sDV, cg_dv)
            em.add("scalar", ACT(s_ecg[par][:], s_cgtok[par][:], AF.Exp), inc=(sAC, 1))
            ecg_ac = sAC.n
            em.wait("sync", sAC, ecg_ac)
            em.dma("sync", bounce_bl[par], s_ecg[par][127:128, 0:32], dPr[par])
            em.wait("sync", dPr[par], dPr[par].n)
            em.dma("sync", s_blr[par][0:1, :], bounce_bl[par], dPr[par])
            blr_dma = dPr[par].n
            em.wait("tensor", dPr[par], blr_dma)
            em.wait("tensor", sDV, cg_dv)
            em.add("tensor", MM(B6[:, 160:192], c_ones[0:1, :], s_blr[par][0:1, :]),
                   inc=(sTE, 1))
            bl_te = sTE.n
            em.wait("vector", sTE, bl_te)
            em.add("vector", CP(s_BL[par][:], B6[:, 160:192]), inc=(sDV, 1))
            prep_dv[ci] = sDV.n
            prep_ac[ci] = ecg_ac
            prep_dma[ci] = d1
            prep_pr[ci] = blr_dma
            prep_cgf[ci] = cgf_dma
            st["b6_free"] = sDV.n

        gidx = 0
        for ci in range(NCHUNKS):
            if ci == 0:
                emit_prep(0)
                if NCHUNKS > 1:
                    emit_prep(1)
            b, c = sched[ci]
            par = ci % 2
            i_tile = b * NCH + c
            row0 = b * T + c * CH
            Scur = p_S[b][c % 2]
            Snext = p_S[b][(c + 1) % 2]
            Sbf = p_Sbf[b]
            STAGE = int(os.environ.get("MK_P4_STAGE", 99))
            for grp in range(int(os.environ.get("MK_P4_GROUPS", 8))):
                gpar = gidx % 2
                o_dv = bl_dv = kq_dv = dtm_dv = r_dv = exp_ac = sDV.n
                heads = [4 * grp + j for j in range(4)]

                def hsl(buf, h):
                    return buf[64 * (h % 2):64 * (h % 2) + 64, h // 2, :]

                def stile(buf, h):
                    return buf[64 * (h % 2):64 * (h % 2) + 64,
                               (h // 2) * 64:(h // 2) * 64 + 64]

                em.wait("tensor", sAC, st["rb_free_ac"])
                em.wait("tensor", dT[par], prep_dma[ci])
                em.wait("tensor", dPr[par], prep_cgf[ci])
                em.add("tensor", MM(B7[:], c_ones[0:1, :],
                                    s_cgf[par][0:1, grp * 512:(grp + 1) * 512]), inc=(sTE, 1))
                rb_te = sTE.n
                em.wait("tensor", sDV, st["kq_free_dv"])
                em.wait("tensor", sDV, s_zero_dv)
                em.wait("tensor", sDV, st["B4_free"])
                if ci >= 2:
                    em.wait("tensor", sAC, ch_blend_ac[ci - 2])
                for j, h in enumerate(heads):
                    em.add("tensor", MM(([B0, B1][j % 2])[:, (j // 2) * 128:(j // 2) * 128 + 128],
                                        hsl(s_kc[par], h), hsl(s_kc[par], h), start=(j < 2)))
                for j, h in enumerate(heads):
                    em.add("tensor", MM(([B0, B1][j % 2])[:, 256 + (j // 2) * 128:256 + (j // 2) * 128 + 128],
                                        hsl(s_kc[par], h), hsl(s_qc[par], h), start=False))
                for j, h in enumerate(heads):
                    em.add("tensor", MM(([B4, B5][j % 2])[:, (j // 2) * 64:(j // 2) * 64 + 64],
                                        hsl(s_kc[par], h), stile(Sbf, h), start=(j < 2)))
                for j, h in enumerate(heads):
                    em.add("tensor", MM(([B4, B5][j % 2])[:, 128 + (j // 2) * 64:128 + (j // 2) * 64 + 64],
                                        hsl(s_qc[par], h), stile(Sbf, h), start=False),
                           inc=(sTE, 1) if j == 3 else None)
                l0_te = sTE.n
                if STAGE >= 2:
                    em.wait("scalar", sTE, rb_te)
                    em.wait("scalar", sDV, prep_dv[ci])
                    em.wait("scalar", sDV, st["scr_free"][gpar])
                    for j, h in enumerate(heads):
                        em.add("scalar", ACT(s_dtr[gpar][:, j * 128:(j + 1) * 128],
                                             B7[:, j * 128:(j + 1) * 128], AF.Exp,
                                             bias=s_negcg[par][:, h:h + 1]),
                               inc=(sAC, 1) if j == 3 else None)
                    exp_ac = sAC.n
                    st["rb_free_ac"] = exp_ac
                if STAGE >= 3:
                    em.wait("vector", sAC, exp_ac)
                    em.wait("vector", sDV, st["scr_free"][gpar])
                    for j in range(4):
                        em.add("vector", TT(s_dt4[gpar][:, j * 128:(j + 1) * 128],
                                            s_dtr[gpar][:, j * 128:(j + 1) * 128],
                                            c_umask[:], ALU.min),
                               inc=(sDV, 1) if j == 3 else None)
                    dtm_dv = sDV.n
                    rv = em.relay("gpsimd", sDV, dtm_dv, rlyG)
                if STAGE >= 4:
                    em.wait("vector", sTE, l0_te)
                    em.add("vector", CP(s_kkr[gpar][:], B0[:]))
                    em.add("vector", CP(s_qkr[gpar][:], B1[:]), inc=(sDV, 1))
                    kq_dv = sDV.n
                    em.wait("vector", rlyG, rv)
                    for j in range(4):
                        em.add("vector", CP(s_w4[gpar][:, j:j + 1],
                                            s_dt4[gpar][:, j * 128 + 127:j * 128 + 128]))
                    st["kq_free_dv"] = kq_dv
                if STAGE >= 5:
                    em.wait("vector", sDV, prep_dv[ci])
                    em.wait("vector", sAC, prep_ac[ci])
                    for j, h in enumerate(heads):
                        em.add("vector", STT(s_r14[gpar][:, j * 64:(j + 1) * 64],
                                             [B4, B5][j % 2][:, (j // 2) * 64:(j // 2) * 64 + 64],
                                             s_ecg[par][:, h:h + 1],
                                             s_vk[par][:, h * 64:(h + 1) * 64],
                                             ALU.mult, ALU.subtract),
                               inc=(sDV, 1) if j == 3 else None)
                    r1_dv = sDV.n
                    rv = em.relay("gpsimd", sDV, r1_dv, rlyG)
                    em.wait("vector", rlyG, rv)
                    for j, h in enumerate(heads):
                        em.add("vector", TS(s_R4[gpar][:, j * 64:(j + 1) * 64],
                                            s_r14[gpar][:, j * 64:(j + 1) * 64],
                                            p_negb[:, i_tile * 32 + h:i_tile * 32 + h + 1], 0.0,
                                            ALU.mult), inc=(sDV, 1) if j == 3 else None)
                    r_dv = sDV.n
                    for j, h in enumerate(heads):
                        em.add("vector", TS(s_o14[gpar][:, j * 64:(j + 1) * 64],
                                            [B4, B5][j % 2][:, 128 + (j // 2) * 64:
                                                            128 + (j // 2) * 64 + 64],
                                            s_ecg[par][:, h:h + 1], 0.0, ALU.mult),
                               inc=(sDV, 1) if j == 3 else None)
                    t1_dv = sDV.n
                if STAGE >= 6:
                    em.wait("tensor", sDV, dtm_dv)
                    em.wait("tensor", sDV, kq_dv)
                    em.wait("tensor", sDV, st["B3_free"])
                    for j in range(4):
                        em.add("tensor", TR(B3[:, j * 128:(j + 1) * 128],
                                            s_dt4[gpar][:, j * 128:(j + 1) * 128], c_idbf[:]),
                               inc=(sTE, 1) if j == 3 else None)
                    dtT_te = sTE.n
                    em.wait("vector", sTE, dtT_te)
                    em.add("vector", TT(s_D4[gpar][:], B3[:], c_negi4[:], ALU.add),
                           inc=(sDV, 1))
                    d_dv = sDV.n
                    rv = em.relay("gpsimd", sDV, d_dv, rlyG)
                    em.wait("vector", rlyG, rv)
                    for j, h in enumerate(heads):
                        em.add("vector", STT(s_bn4[gpar][:, j * 128:(j + 1) * 128],
                                             [s_kkr, s_qkr][j % 2][gpar]
                                             [:, (j // 2) * 128:(j // 2) * 128 + 128],
                                             p_negb[:, i_tile * 32 + h:i_tile * 32 + h + 1],
                                             s_D4[gpar][:, j * 128:(j + 1) * 128],
                                             ALU.mult, ALU.mult),
                               inc=(sDV, 1) if j == 3 else None)
                    bn_dv = sDV.n
                    em.wait("gpsimd", sDV, kq_dv)
                    em.wait("gpsimd", sDV, dtm_dv)
                    for j in range(4):
                        em.add("gpsimd", TT(s_dtr[gpar][:, j * 128:(j + 1) * 128],
                                            [s_kkr, s_qkr][j % 2][gpar]
                                            [:, 256 + (j // 2) * 128:256 + (j // 2) * 128 + 128],
                                            s_dt4[gpar][:, j * 128:(j + 1) * 128], ALU.mult),
                               inc=(sGP, 1) if j == 3 else None)
                    p2_gp = sGP.n
                if STAGE >= 7:
                    em.wait("tensor", sDV, bn_dv)
                    for j in range(4):
                        em.add("tensor", TR(B3[:, j * 128:(j + 1) * 128],
                                            s_bn4[gpar][:, j * 128:(j + 1) * 128], c_idbf[:]),
                               inc=(sTE, 1) if j == 3 else None)
                    bnT_te = sTE.n
                    em.wait("vector", sTE, bnT_te)
                    em.add("vector", CP(s_bt4[gpar][:], B3[:]))
                    em.add("vector", TT(s_hh[gpar][0][:], B3[:], c_i4[:], ALU.add),
                           inc=(sDV, 1))
                    bt_dv = sDV.n
                    st["B3_free"] = bt_dv
                if STAGE >= 8:
                    tpow_prev, pow_prev = s_bt4[gpar], s_bn4[gpar]
                    h_prev = s_hh[gpar][0]
                    ev_prev = bt_dv
                    tp_ac = 0
                    h_ev = 0
                    for lvl in range(1, 7):
                        lp = lvl % 2
                        em.wait("tensor", sDV, ev_prev)
                        em.wait("tensor", sDV, st["B2_free"])
                        if tp_ac:
                            em.wait("tensor", sAC, tp_ac)
                        if lvl < 6:
                            em.wait("tensor", sAC, st["B1s_free_ac"])
                            for j in range(4):
                                em.add("tensor", MM(B1[:, j * 128:(j + 1) * 128],
                                                    pow_prev[:, j * 128:(j + 1) * 128],
                                                    tpow_prev[:, j * 128:(j + 1) * 128],
                                                    start=(j == 0)))
                        for j in range(4):
                            em.add("tensor", MM(B2[:, j * 128:(j + 1) * 128],
                                                tpow_prev[:, j * 128:(j + 1) * 128],
                                                pow_prev[:, j * 128:(j + 1) * 128],
                                                start=(j == 0)),
                                   inc=(sTE, 1) if j == 3 else None)
                        pw_te = sTE.n
                        em.wait("vector", sTE, pw_te)
                        em.add("vector", CP(s_pw[gpar][lp][:], B2[:]), inc=(sDV, 1))
                        pow_dv = sDV.n
                        st["B2_free"] = pow_dv
                        if lvl < 6:
                            em.wait("scalar", sTE, pw_te)
                            em.add("scalar", lambda e, a=s_tp[gpar][lp], b2=B1: e.copy(
                                a[:], b2[:]), inc=(sAC, 1))
                            tp_ac = sAC.n
                            st["B1s_free_ac"] = tp_ac
                        em.wait("tensor", sDV, pow_dv)
                        if h_ev:
                            em.wait("tensor", sDV, h_ev)
                        for j in range(4):
                            em.add("tensor", MM(B2[:, j * 128:(j + 1) * 128], c_idbf[:],
                                                h_prev[:, j * 128:(j + 1) * 128],
                                                start=(j == 0), stop=False))
                            em.add("tensor", MM(B2[:, j * 128:(j + 1) * 128],
                                                s_pw[gpar][lp][:, j * 128:(j + 1) * 128],
                                                h_prev[:, j * 128:(j + 1) * 128],
                                                start=False, stop=True),
                                   inc=(sTE, 1) if j == 3 else None)
                        h_te = sTE.n
                        em.wait("vector", sTE, h_te)
                        em.add("vector", CP(s_hh[gpar][lp][:], B2[:]), inc=(sDV, 1))
                        h_ev = sDV.n
                        st["B2_free"] = h_ev
                        h_prev = s_hh[gpar][lp]
                        tpow_prev, pow_prev = s_tp[gpar][lp], s_pw[gpar][lp]
                        ev_prev = h_ev
                if STAGE >= 9:
                    em.wait("tensor", sDV, r_dv)
                    em.wait("tensor", sDV, st["B2_free"])
                    for j in range(4):
                        em.add("tensor", MM(B2[:, j * 64:(j + 1) * 64],
                                            h_prev[:, j * 128:(j + 1) * 128],
                                            s_R4[gpar][:, j * 64:(j + 1) * 64],
                                            start=(j == 0)),
                               inc=(sTE, 1) if j == 3 else None)
                    u_te = sTE.n
                    em.wait("vector", sTE, u_te)
                    em.add("vector", CP(s_U4[gpar][:], B2[:, 0:256]))
                    for j, h in enumerate(heads):
                        em.add("vector", TS(s_Ub4[gpar][:, j * 64:(j + 1) * 64],
                                            B2[:, j * 64:(j + 1) * 64],
                                            s_w4[gpar][:, j:j + 1], 0.0,
                                            ALU.mult), inc=(sDV, 1) if j == 3 else None)
                    u_dv = sDV.n
                if STAGE >= 10:
                    em.wait("tensor", sGP, p2_gp)
                    em.wait("tensor", sDV, u_dv)
                    for j in range(4):
                        em.add("tensor", MM(B2[:, 256 + j * 64:256 + (j + 1) * 64],
                                            s_dtr[gpar][:, j * 128:(j + 1) * 128],
                                            s_U4[gpar][:, j * 64:(j + 1) * 64],
                                            start=False))
                    em.wait("tensor", sDV, t1_dv)
                    for j, h in enumerate(heads):
                        em.add("tensor", MM(B4[64 * (h % 2):64 * (h % 2) + 64,
                                               256 + j * 64:256 + (j + 1) * 64],
                                            s_kt[par][:, h * 64:(h + 1) * 64],
                                            s_Ub4[gpar][:, j * 64:(j + 1) * 64],
                                            start=(j == 0),
                                            tile_position=(0, 64 * (h % 2))),
                               inc=(sTE, 1) if j == 3 else None)
                    oi_te = sTE.n
                    ch_lastuse_te[ci] = oi_te
                if STAGE >= 11:
                    em.wait("vector", sTE, oi_te)
                    if ci >= 2:
                        em.wait("vector", dOsp[par], ch_o_spill[ci - 2])
                    for j, h in enumerate(heads):
                        em.add("vector", STT(s_o24[gpar][:, j * 64:(j + 1) * 64],
                                             B2[:, 256 + j * 64:256 + (j + 1) * 64], 1.0,
                                             s_o14[gpar][:, j * 64:(j + 1) * 64],
                                             ALU.mult, ALU.add),
                               inc=(sDV, 1) if j == 3 else None)
                    o1_dv = sDV.n
                    rv = em.relay("gpsimd", sDV, o1_dv, rlyG)
                    em.wait("vector", rlyG, rv)
                    for j, h in enumerate(heads):
                        em.add("vector", STT(s_oa[par][:, h * 64:(h + 1) * 64],
                                             s_vk[par][:, h * 64:(h + 1) * 64],
                                             c_dD[:, h:h + 1],
                                             s_o24[gpar][:, j * 64:(j + 1) * 64],
                                             ALU.mult, ALU.add),
                               inc=(sDV, 1) if j == 3 else None)
                    o_dv = sDV.n
                    st["B2_free"] = o1_dv
                    st["scr_free"][gpar] = o_dv
                if STAGE >= 12:
                    for j, h in enumerate(heads):
                        em.add("vector", STT(stile(Snext, h), stile(Scur, h),
                                             s_BL[par][64 * (h % 2):64 * (h % 2) + 64, h:h + 1],
                                             B4[64 * (h % 2):64 * (h % 2) + 64,
                                                256 + j * 64:256 + (j + 1) * 64],
                                             ALU.mult, ALU.add),
                               inc=(sDV, 1) if j == 3 else None)
                    bl_dv = sDV.n
                    st["B4_free"] = bl_dv
                    em.wait("scalar", sDV, bl_dv)
                    for j, h in enumerate(heads):
                        em.add("scalar",
                               lambda e, a=stile(Sbf, h), bb2=stile(Snext, h): e.copy(a, bb2),
                               inc=(sAC, 1) if j == 3 else None)
                if STAGE < 12:
                    em.add("vector", MEMSET(s_oa[par][:, 0:64], 0.0), inc=(sDV, 1))
                    o_dv = bl_dv = sDV.n
                ch_blend_ac[ci] = sAC.n
                ch_o_dv[ci] = o_dv
                gidx += 1
            em.wait("sync", sDV, ch_o_dv[ci])
            ch_o_spill[ci] = em.dma("sync", o_d[row0:row0 + 128, :], s_oa[par][:], dOsp[par])
            if ci + 2 < NCHUNKS:
                emit_prep(ci + 2)
        for sm_ in (dT[0], dT[1], dPr[0], dPr[1], dOsp[0], dOsp[1]):
            em.wait("sync", sm_, sm_.n)
        em.wait("sync", sDV, sDV.n)
        em.wait("sync", sAC, sAC.n)
        em.wait("sync", sGP, sGP.n)
        em.play("p4")
        ph.close()
    stack2.close()

    if PHASES >= 5:
        ph = ExitStack()
        sbl = lambda name, shape, dt: ph.enter_context(nc.sbuf_tensor(name, shape, dt))
        psl = lambda name, shape, dt: ph.enter_context(nc.psum_tensor(name, shape, dt))
        SP5 = 1024
        NSP5 = TOK // SP5
        TPS5 = SP5 // 128
        PG = [[psl(f"p5_G{i}{j}", [128, 512], f) for j in range(2)] for i in range(2)]
        PSS = psl("p5_SS", [128, 512], f)
        PRB = psl("p5_RB", [128, 512], f)
        PW = [psl(f"p5_W{i}", [128, 512], f) for i in range(2)]
        s_x5 = sbl("s_x5", [128, 16, SP5], bf)
        s_oT5 = sbl("s_oT5", [128, 16, SP5], bf)
        s_o55 = sbl("s_o55", [128, 16, SP5], bf)
        s_sq5 = sbl("s_sq5", [128, SP5], bf)
        s_r2b = sbl("s_r2b", [2, 16, SP5], bf)
        s_r25 = sbl("s_r25", [2, 512], f)
        s_wg5 = [sbl(f"s_wg5_{i}", [128, 16, 128], bf) for i in range(2)]
        s_wo5 = [sbl(f"s_wo5_{i}", [128, 16, 512], bf) for i in range(2)]
        s_a5 = sbl("s_a5", [128, SP5], f)
        s_sg5 = sbl("s_sg5", [128, SP5], f)
        s_t25 = sbl("s_t25", [128, 512], f)
        s_ev5 = [sbl(f"s_ev5_{i}", [128, 512], f) for i in range(2)]
        dO5 = [em.sem("dO5a"), em.sem("dO5b")]
        dWg5 = [em.sem("dWg5a"), em.sem("dWg5b")]
        dWoQ = [em.sem("dWoQa"), em.sem("dWoQb")]
        dX5 = em.sem("dX5")
        dSd = [em.sem("dSda"), em.sem("dSdb")]
        em.wait("sync", dCgw, cast_gwo)

        sq_ac = [0] * 16
        ss_te = [[0, 0] for _ in range(16)]
        rstd_ac = [[0, 0] for _ in range(16)]
        gate_te = [[0, 0] for _ in range(16)]
        rb_te5 = [[0, 0] for _ in range(16)]
        a_dv = [[0, 0] for _ in range(16)]
        o5_dv = [[0, 0] for _ in range(16)]
        prev_rbte = [0] * 16
        prev_adv = [0] * 16
        prev_gate_all = 0
        wo_q_te = [0, 0, 0, 0]
        prev_wo_q_te = [0, 0, 0, 0]
        wg_ld5 = [0, 0]
        o_ld5 = [0, 0]
        woq_ld = [0, 0, 0, 0]
        ev_dv5 = [0, 0]
        sd_dma = [0, 0]
        for sp in range(NSP5):
            rows0 = sp * SP5
            em.wait("sync", sTE, prev_gate_all)
            x5_ld = em.dma("sync", s_x5[:], xnT_d[:, :, rows0:rows0 + SP5], dX5)
            for it in range(18):
                d2 = it
                e2 = it - 2
                par = d2 % 2
                ep = e2 % 2
                if d2 < 16:
                    if d2 >= 2:
                        em.wait("sync", sAC, sq_ac[d2 - 2])
                    if sp > 0:
                        em.wait("sync", sDV, prev_adv[d2])
                    o_ld5[par] = em.dma("sync", s_oT5[:, d2, :],
                                        o_d[rows0:rows0 + SP5, d2 * 128:(d2 + 1) * 128],
                                        dO5[par], transpose=True)
                if 0 <= e2 < 16:
                    if e2 >= 2:
                        em.wait("sync", sTE, gate_te[e2 - 2][1])
                    q5, c0 = e2 // 4, (e2 % 4) * 128
                    wg_ld5[ep] = em.dma(
                        "sync", s_wg5[ep][:],
                        wg_bf[q5].rearrange("kb p c -> p kb c")[:, :, c0:c0 + 128],
                        dWg5[ep])
                if d2 in (8, 12):
                    qq = (d2 - 8) // 4
                    em.wait("sync", sTE, prev_wo_q_te[qq + 2])
                    woq_ld[qq] = em.dma("sync", s_wo5[qq % 2][:],
                                        wo_bf[qq].rearrange("kb p c -> p kb c"),
                                        dWoQ[qq % 2])
                if 1 <= d2 <= 16:
                    dd = d2 - 1
                    em.wait("scalar", sTE, ss_te[dd][1])
                    em.wait("scalar", sTE, prev_rbte[dd])
                    em.add("scalar", ACT(s_r25[:], PSS[0:2, 0:512], AF.Ln,
                                         bias=c_eps6r[0:2, 0:1]))
                    em.add("scalar", lambda e, dd=dd: e.activation(
                        out=s_r2b[:, dd, 512:1024], in_=s_r25[:], func=AF.Exp,
                        scale=-0.5), inc=(sAC, 1))
                    rstd_ac[dd][1] = sAC.n
                if d2 < 16:
                    em.wait("scalar", dO5[par], o_ld5[par])
                    if d2 >= 1:
                        em.wait("scalar", sTE, ss_te[d2 - 1][1])
                    em.add("scalar", ACT(s_sq5[:], s_oT5[:, d2, :], AF.Square),
                           inc=(sAC, 1))
                    sq_ac[d2] = sAC.n
                if 0 <= e2 < 16:
                    em.wait("tensor", sAC, rstd_ac[e2][0])
                    em.wait("tensor", sDV, a_dv[e2 - 1][1] if e2 >= 1 else prev_adv[15])
                    em.add("tensor", MM(PRB[:], c_obd64T[:], s_r2b[:, e2, 0:512]),
                           inc=(sTE, 1))
                    rb_te5[e2][0] = sTE.n
                    em.wait("tensor", dX5, x5_ld)
                    em.wait("tensor", dWg5[ep], wg_ld5[ep])
                    em.wait("tensor", sDV, o5_dv[e2 - 2][0])
                    for kb in range(16):
                        em.add("tensor", MM(PG[ep][0][:], s_wg5[ep][:, kb, :],
                                            s_x5[:, kb, 0:512],
                                            start=(kb == 0), stop=(kb == 15)),
                               inc=(sTE, 1) if kb == 15 else None)
                    gate_te[e2][0] = sTE.n
                if d2 < 16:
                    em.wait("tensor", sAC, sq_ac[d2])
                    em.wait("tensor", sAC, rstd_ac[d2 - 1][1])
                    em.add("tensor", MM(PSS[0:2, 0:512], c_obd64[:], s_sq5[:, 0:512]),
                           inc=(sTE, 1))
                    ss_te[d2][0] = sTE.n
                if 0 <= e2 < 16:
                    em.wait("vector", sTE, rb_te5[e2][0])
                    em.add("vector", STT(s_a5[:, 0:512], PRB[:], c_onc[:, 0:1],
                                         s_oT5[:, e2, 0:512], ALU.mult, ALU.mult),
                           inc=(sDV, 1))
                    a_dv[e2][0] = sDV.n
                if 0 <= e2 < 16:
                    em.wait("scalar", sDV, a_dv[e2][0])
                    em.add("scalar", ACT(s_sg5[:, 0:512], s_a5[:, 0:512], AF.Exp,
                                         scale=-1.0))
                    em.add("scalar", ACT(s_sg5[:, 0:512], s_sg5[:, 0:512], AF.Ln,
                                         bias=c_one[:, 0:1]))
                    em.add("scalar", ACT(s_sg5[:, 0:512], s_sg5[:, 0:512], AF.Exp,
                                         scale=-1.0), inc=(sAC, 1))
                    sg0_ac = sAC.n
                if d2 < 16:
                    em.wait("scalar", sTE, ss_te[d2][0])
                    em.add("scalar", ACT(s_r25[:], PSS[0:2, 0:512], AF.Ln,
                                         bias=c_eps6r[0:2, 0:1]))
                    em.add("scalar", lambda e, d2=d2: e.activation(
                        out=s_r2b[:, d2, 0:512], in_=s_r25[:], func=AF.Exp,
                        scale=-0.5), inc=(sAC, 1))
                    rstd_ac[d2][0] = sAC.n
                if 0 <= e2 < 16:
                    em.wait("tensor", sAC, rstd_ac[e2][1])
                    em.wait("tensor", sDV, a_dv[e2][0])
                    em.add("tensor", MM(PRB[:], c_obd64T[:], s_r2b[:, e2, 512:1024]),
                           inc=(sTE, 1))
                    rb_te5[e2][1] = sTE.n
                    em.wait("tensor", sDV, o5_dv[e2 - 2][1])
                    for kb in range(16):
                        em.add("tensor", MM(PG[ep][1][:], s_wg5[ep][:, kb, :],
                                            s_x5[:, kb, 512:1024],
                                            start=(kb == 0), stop=(kb == 15)),
                               inc=(sTE, 1) if kb == 15 else None)
                    gate_te[e2][1] = sTE.n
                if d2 < 16:
                    em.wait("tensor", sAC, rstd_ac[d2][0])
                    em.add("tensor", MM(PSS[0:2, 0:512], c_obd64[:], s_sq5[:, 512:1024]),
                           inc=(sTE, 1))
                    ss_te[d2][1] = sTE.n
                if 0 <= e2 < 16:
                    em.wait("vector", sAC, sg0_ac)
                    em.add("vector", TT(s_t25[:], s_a5[:, 0:512], s_sg5[:, 0:512],
                                        ALU.mult))
                    em.wait("vector", sTE, gate_te[e2][0])
                    em.add("vector", TT(s_o55[:, e2, 0:512], PG[ep][0][:], s_t25[:],
                                        ALU.mult), inc=(sDV, 1))
                    o5_dv[e2][0] = sDV.n
                    em.wait("vector", sTE, rb_te5[e2][1])
                    em.add("vector", STT(s_a5[:, 512:1024], PRB[:], c_onc[:, 0:1],
                                         s_oT5[:, e2, 512:1024], ALU.mult, ALU.mult),
                           inc=(sDV, 1))
                    a_dv[e2][1] = sDV.n
                    em.wait("scalar", sDV, a_dv[e2][1])
                    em.add("scalar", ACT(s_sg5[:, 512:1024], s_a5[:, 512:1024], AF.Exp,
                                         scale=-1.0))
                    em.add("scalar", ACT(s_sg5[:, 512:1024], s_sg5[:, 512:1024], AF.Ln,
                                         bias=c_one[:, 0:1]))
                    em.add("scalar", ACT(s_sg5[:, 512:1024], s_sg5[:, 512:1024], AF.Exp,
                                         scale=-1.0), inc=(sAC, 1))
                    sg1_ac = sAC.n
                    em.wait("vector", sAC, sg1_ac)
                    em.add("vector", TT(s_t25[:], s_a5[:, 512:1024], s_sg5[:, 512:1024],
                                        ALU.mult))
                    em.wait("vector", sTE, gate_te[e2][1])
                    em.add("vector", TT(s_o55[:, e2, 512:1024], PG[ep][1][:], s_t25[:],
                                        ALU.mult), inc=(sDV, 1))
                    o5_dv[e2][1] = sDV.n
            prev_gate_all = sTE.n
            prev_rbte = [rb_te5[i][1] for i in range(16)]
            prev_adv = [a_dv[i][1] for i in range(16)]
            em.wait("tensor", sDV, o5_dv[15][1])
            for q in range(4):
                qp = q % 2
                em.wait("tensor", dWoQ[qp], woq_ld[q])
                for tt in range(TPS5):
                    bank = PW[tt % 2]
                    em.wait("tensor", sDV, ev_dv5[tt % 2])
                    for d2 in range(16):
                        em.add("tensor", MM(bank[:], s_o55[:, d2, tt * 128:(tt + 1) * 128],
                                            s_wo5[qp][:, d2, :],
                                            start=(d2 == 0), stop=(d2 == 15)),
                               inc=(sTE, 1) if d2 == 15 else None)
                    rte = sTE.n
                    em.wait("vector", sTE, rte)
                    em.wait("vector", dSd[tt % 2], sd_dma[tt % 2])
                    em.add("vector", CP(s_ev5[tt % 2][:], bank[:]), inc=(sDV, 1))
                    ev_dv5[tt % 2] = sDV.n
                    em.wait("sync", sDV, ev_dv5[tt % 2])
                    sd_dma[tt % 2] = em.dma(
                        "sync", s_d[rows0 + tt * 128:rows0 + (tt + 1) * 128,
                                    q * 512:(q + 1) * 512],
                        s_ev5[tt % 2][:], dSd[tt % 2])
                wo_q_te[q] = sTE.n
                if q + 2 < 4:
                    em.wait("sync", sTE, wo_q_te[q])
                    woq_ld[q + 2] = em.dma("sync", s_wo5[q % 2][:],
                                           wo_bf[q + 2].rearrange("kb p c -> p kb c"),
                                           dWoQ[q % 2])
            prev_wo_q_te = list(wo_q_te)
        for sm_ in (dO5[0], dO5[1], dWg5[0], dWg5[1], dWoQ[0], dWoQ[1], dX5,
                    dSd[0], dSd[1]):
            em.wait("sync", sm_, sm_.n)
        em.wait("sync", sDV, sDV.n)
        em.wait("sync", sAC, sAC.n)
        em.wait("sync", sTE, sTE.n)
        em.play("p5")
        ph.close()

    if PHASES >= 6:
        ph = ExitStack()
        sbl = lambda name, shape, dt: ph.enter_context(nc.sbuf_tensor(name, shape, dt))
        psl = lambda name, shape, dt: ph.enter_context(nc.psum_tensor(name, shape, dt))
        SP6 = 1024
        NSP6 = TOK // SP6
        TPS6 = SP6 // 128
        PB = [psl(f"p6_B{i}", [128, 512], f) for i in range(8)]
        s_x6 = sbl("s_x6", [128, 16, SP6], bf)
        s_w6 = [sbl(f"s_w6_{i}", [128, 16, 512], bf) for i in range(2)]
        s_acc = sbl("s_acc", [128, TPS6 * 2048], f)
        s_cbT = sbl("s_cbT", [8, TPS6 * 128], f)
        c_fb6 = sbl("c_fb6", [8, 2048], f)
        s_sg6 = sbl("s_sg6", [128, 2048], f)
        s_xr6 = [sbl(f"s_xr6_{i}", [128, 2048], f) for i in range(2)]
        s_sr6 = [sbl(f"s_sr6_{i}", [128, 2048], f) for i in range(2)]
        s_ob6 = [sbl(f"s_ob6_{i}", [128, 2048], f) for i in range(2)]
        dX6 = em.sem("dX6")
        dW6 = [em.sem("dW6a"), em.sem("dW6b")]
        dRes = [em.sem("dResa"), em.sem("dResb")]
        dOut = [em.sem("dOuta"), em.sem("dOutb")]
        em.wait("sync", dCff, cast_all)
        fb_ld = em.dma("sync", c_fb6[:], fb, dX6)

        def accv(tt, q):
            return s_acc[:, tt * 2048 + q * 512:tt * 2048 + (q + 1) * 512]

        x6war_te = 0
        prev_ev_all = 0
        w6_ld = [0, 0]
        round_te = [0] * 32
        res_ld = [0, 0]
        out_dma6 = [0, 0]
        res_war = [0, 0]
        gtt_dv = 0
        for sp in range(NSP6):
            rows0 = sp * SP6
            i0 = sp * TPS6
            em.wait("sync", sTE, x6war_te)
            x6_ld = em.dma("sync", s_x6[:], xnT_d[:, :, rows0:rows0 + SP6], dX6)
            for r0 in range(2):
                e0, q0 = r0 // 4, r0 % 4
                w6_ld[r0 % 2] = em.dma("sync", s_w6[r0 % 2][:],
                                       wf_bf[e0, q0].rearrange("kb p c -> p kb c"),
                                       dW6[r0 % 2])
            em.wait("tensor", sDV, prev_ev_all)
            for tt in range(TPS6):
                em.add("tensor", TR(PB[tt][0:8, 0:128],
                                    p_comb[:, (i0 + tt) * 8:(i0 + tt + 1) * 8], c_idf[:]),
                       inc=(sTE, 1) if tt == TPS6 - 1 else None)
            cbt_te = sTE.n
            em.wait("vector", sTE, cbt_te)
            for tt in range(TPS6):
                em.add("vector", CP(s_cbT[:, tt * 128:(tt + 1) * 128], PB[tt][0:8, 0:128]),
                       inc=(sDV, 1) if tt == TPS6 - 1 else None)
            cbt_dv = sDV.n
            em.wait("tensor", sDV, cbt_dv)
            em.wait("tensor", dX6, fb_ld)
            seed_dv = 0
            for q in range(4):
                if seed_dv:
                    em.wait("tensor", sDV, seed_dv)
                for tt in range(TPS6):
                    em.add("tensor", MM(PB[tt][:], s_cbT[:, tt * 128:(tt + 1) * 128],
                                        c_fb6[:, q * 512:(q + 1) * 512]),
                           inc=(sTE, 1) if tt == TPS6 - 1 else None)
                seed_te = sTE.n
                em.wait("vector", sTE, seed_te)
                for tt in range(TPS6):
                    em.add("vector", CP(accv(tt, q), PB[tt][:]),
                           inc=(sDV, 1) if tt == TPS6 - 1 else None)
                seed_dv = sDV.n
            ev_half = [seed_dv, seed_dv]
            for e in range(E):
                for q in range(4):
                    r = e * 4 + q
                    wp = r % 2
                    em.wait("tensor", dW6[wp], w6_ld[wp])
                    em.wait("tensor", dX6, x6_ld)
                    for half in range(2):
                        em.wait("tensor", sDV, ev_half[half])
                        for kb in range(16):
                            for tt in range(4 * half, 4 * half + 4):
                                em.add("tensor",
                                       MM(PB[tt][:], s_x6[:, kb, tt * 128:(tt + 1) * 128],
                                          s_w6[wp][:, kb, :],
                                          start=(kb == 0), stop=(kb == 15)),
                                       inc=(sTE, 1) if (kb == 15 and tt == 4 * half + 3)
                                       else None)
                        hte = sTE.n
                        em.wait("vector", sTE, hte)
                        for tt in range(4 * half, 4 * half + 4):
                            em.add("vector",
                                   STT(accv(tt, q), PB[tt][:],
                                       p_comb[:, (i0 + tt) * 8 + e:(i0 + tt) * 8 + e + 1],
                                       accv(tt, q), ALU.mult, ALU.add),
                                   inc=(sDV, 1) if tt == 4 * half + 3 else None)
                        ev_half[half] = sDV.n
                    round_te[r] = sTE.n
                    if r + 2 < 32:
                        e2_, q2_ = (r + 2) // 4, (r + 2) % 4
                        em.wait("sync", sTE, round_te[r])
                        w6_ld[wp] = em.dma("sync", s_w6[wp][:],
                                           wf_bf[e2_, q2_].rearrange("kb p c -> p kb c"),
                                           dW6[wp])
            prev_ev_all = ev_half[1]
            x6war_te = round_te[31]
            prev_ob_dv = 0
            for tt in range(TPS6):
                par = tt % 2
                acct = s_acc[:, tt * 2048:(tt + 1) * 2048]
                em.wait("scalar", sDV, ev_half[1] if tt == 0 else prev_ob_dv)
                em.add("scalar", ACT(s_sg6[:], acct, AF.Exp, scale=-1.0))
                em.add("scalar", ACT(s_sg6[:], s_sg6[:], AF.Ln, bias=c_one[:, 0:1]))
                em.add("scalar", ACT(s_sg6[:], s_sg6[:], AF.Exp, scale=-1.0),
                       inc=(sAC, 1))
                sg_ac6 = sAC.n
                em.wait("sync", sDV, res_war[par])
                em.dma("sync", s_xr6[par][:], x_in[rows0 + tt * 128:rows0 + (tt + 1) * 128, :],
                       dRes[par])
                res_ld[par] = em.dma(
                    "sync", s_sr6[par][:],
                    s_d[rows0 + tt * 128:rows0 + (tt + 1) * 128, :], dRes[par])
                em.wait("vector", sAC, sg_ac6)
                em.add("vector", TT(s_sg6[:], acct, s_sg6[:], ALU.mult), inc=(sDV, 1))
                gtt_dv = sDV.n
                em.wait("vector", dRes[par], res_ld[par])
                em.wait("vector", dOut[par], out_dma6[par])
                em.add("vector", TT(s_ob6[par][:], s_sg6[:], s_sr6[par][:], ALU.mult))
                em.add("vector", TT(s_ob6[par][:], s_ob6[par][:], s_xr6[par][:], ALU.add),
                       inc=(sDV, 1))
                ob_dv = sDV.n
                res_war[par] = ob_dv
                prev_ob_dv = ob_dv
                em.wait("sync", sDV, ob_dv)
                out_dma6[par] = em.dma(
                    "sync", out_d[rows0 + tt * 128:rows0 + (tt + 1) * 128, :],
                    s_ob6[par][:], dOut[par])
        for sm_ in (dX6, dW6[0], dW6[1], dRes[0], dRes[1], dOut[0], dOut[1]):
            em.wait("sync", sm_, sm_.n)
        em.wait("sync", sDV, sDV.n)
        em.wait("sync", sAC, sAC.n)
        em.wait("sync", sTE, sTE.n)
        em.wait("sync", sGP, sGP.n)
        em.play("p6")
        ph.close()

    if PHASES < 6:
        ph = ExitStack()
        s_z = ph.enter_context(nc.sbuf_tensor("s_z", [128, D], F32))
        em.add("vector", MEMSET(s_z[:], 0.0), inc=(sDV, 1))
        em.wait("sync", sDV, sDV.n)
        for i in range(NT):
            em.dma("sync", out_d[i * 128:(i + 1) * 128, :], s_z[:], dS)
        em.wait("sync", dS, dS.n)
        em.play("stub")
        ph.close()

    stack.close()
    return nc


def kernel(**inputs):
    x = np.asarray(inputs["L0_attn_x"], np.float32)
    shared = host_prepare(inputs)
    nc = build_program()
    ncores = DEVCORES or NCORES
    in_maps = []
    for c in range(ncores):
        m = dict(shared)
        m["x"] = np.ascontiguousarray(x[c * BPC:(c + 1) * BPC].reshape(TOK, D))
        in_maps.append(m)
    res = run_bass_kernel_spmd(nc, in_maps, core_ids=list(range(ncores)))
    out = np.concatenate([np.asarray(r["out"]).reshape(BPC, T, D) for r in res.results], axis=0)
    if DEVCORES:
        out = np.concatenate([out] + [out[:BPC]] * (NCORES - DEVCORES), axis=0)
    if DEBUG:
        kernel.last_results = res.results
    return out


if __name__ == "__main__":
    nc = build_program()
    print("BUILD OK")


# version 03
# speedup vs baseline: 1.3013x; 1.3013x over previous
import os
import numpy as np
import ml_dtypes
import concourse.bass as bass
import concourse.mybir as mybir
from concourse.bass_utils import run_bass_kernel_spmd
from concourse.hw_specs import get_activation_tables
from contextlib import ExitStack

F32 = mybir.dt.float32
BF16 = mybir.dt.bfloat16
AF = mybir.ActivationFunctionType
ALU = mybir.AluOpType
AX = mybir.AxisListType.X

NCORES = 8
BPC = 2
T = 2048
D = 2048
H = 32
DH = 64
TOK = BPC * T
NT = TOK // 128
CH = 128
NCH = T // CH
E = 8

DEBUG = bool(os.environ.get("MK_DEBUG"))
DEVCORES = int(os.environ.get("MK_CORES", "0"))
PHASES = int(os.environ.get("MK_PHASES", "6"))


def _strip_order():
    sio = np.zeros(D, np.int64)
    for d in range(16):
        h0, h1 = 2 * d, 2 * d + 1
        rows = np.concatenate([
            h0 * 64 + np.arange(0, 64, 2), h1 * 64 + np.arange(0, 64, 2),
            h0 * 64 + 1 + np.arange(0, 64, 2), h1 * 64 + 1 + np.arange(0, 64, 2)])
        sio[d * 128:(d + 1) * 128] = rows
    return sio


def _yarn_tables():
    ROPE_BASE, SCALING, BETA_FAST, BETA_SLOW = 10000.0, 32.0, 1.0, 32.0
    half = (np.arange(0, DH, 2, dtype=np.float32) / DH).astype(np.float32)
    inv_freq = (1.0 / ROPE_BASE ** half).astype(np.float32)
    wavelen = (2.0 * np.pi / inv_freq).astype(np.float32)
    ramp = np.clip((wavelen - BETA_FAST) / (BETA_SLOW - BETA_FAST), 0.0, 1.0).astype(np.float32)
    scale = (1.0 + (SCALING - 1.0) * ramp).astype(np.float32)
    t = np.arange(T, dtype=np.float32)
    freqs = (t[:, None] / scale[None, :] * inv_freq[None, :]).astype(np.float32)
    idx = (2 * np.arange(DH // 2)) % (DH // 2)
    ctab = np.cos(freqs[:, idx]).T.astype(np.float32)
    stab = np.sin(freqs[:, idx]).T.astype(np.float32)
    return ctab, stab


def host_prepare(inp):
    f32 = np.float32
    bfl = ml_dtypes.bfloat16
    sio = _strip_order()
    Wq = np.asarray(inp["L0_attn_Wq"], f32)
    Wk = np.asarray(inp["L0_attn_Wk"], f32)
    Wv = np.asarray(inp["L0_attn_Wv"], f32)
    Wg = np.asarray(inp["L0_attn_Wg"], f32)
    Wo = np.asarray(inp["L0_attn_Wo"], f32)
    d = {}

    def blkT(W, order=None):
        WT = W.T if order is None else np.ascontiguousarray(W.T[:, order])
        return np.ascontiguousarray(WT.reshape(D, 16, 128).transpose(1, 0, 2))

    d["wqT"] = blkT(Wq, sio)
    d["wkT"] = blkT(Wk, sio)
    d["wvT"] = blkT(Wv)

    def blk512(WT):
        return np.ascontiguousarray(WT.reshape(16, 128, 4, 512).transpose(2, 0, 1, 3))

    d["wgT"] = blk512(np.ascontiguousarray(Wg.T))
    d["woT"] = blk512(np.ascontiguousarray(Wo.T))
    Wf = np.asarray(inp["L0_ffn_W"], f32)
    d["wffn"] = np.ascontiguousarray(Wf.reshape(8, 16, 128, 4, 512).transpose(0, 3, 1, 2, 4))

    def small(WT):
        return np.ascontiguousarray(np.asarray(WT, f32).reshape(16, 128, -1))

    d["wbT"] = small(np.asarray(inp["L0_attn_Wb"], f32).T)
    d["wgkT"] = small(np.asarray(inp["L0_attn_Wgk"], f32).T)
    d["wgf"] = small(np.asarray(inp["L0_ffn_Wg"], f32))
    d["fb"] = np.ascontiguousarray(np.asarray(inp["L0_ffn_b"], f32))
    for nm, order in (("q", sio), ("k", sio), ("v", None)):
        w = np.asarray(inp[f"L0_attn_w{nm}_conv"], f32)
        b = np.asarray(inp[f"L0_attn_b{nm}_conv"], f32)
        if order is not None:
            w, b = w[order], b[order]
        d[f"cw_{nm}"] = np.ascontiguousarray(w.reshape(16, 128, 4).transpose(1, 0, 2))
        d[f"cb_{nm}"] = np.ascontiguousarray(b.reshape(16, 128).T)

    def rep(v):
        v = np.asarray(v, f32).ravel()
        return np.ascontiguousarray(np.broadcast_to(v[None, :], (128, v.size)))

    d["bb_rep"] = rep(inp["L0_attn_bb"])
    d["bgk_rep"] = rep(inp["L0_attn_bgk"])
    d["dt_rep"] = rep(inp["L0_attn_dt_bias"])
    d["alog_rep"] = rep(inp["L0_attn_A_log"])
    d["dD_rep"] = rep(inp["L0_attn_D"])
    d["onorm_rep"] = np.ascontiguousarray(np.broadcast_to(
        np.tile(np.asarray(inp["L0_attn_onorm_w"], f32), H)[None, :], (128, D)))
    ctab, stab = _yarn_tables()
    d["cs_c"] = np.ascontiguousarray(np.tile(ctab, (4, 1)))
    d["cs_s"] = np.ascontiguousarray(np.tile(stab, (4, 1)))
    I = np.eye(128, dtype=f32)
    d["id_bf"] = I.astype(bfl)
    d["id_f32"] = I
    d["i4_bf"] = np.ascontiguousarray(np.tile(I, (1, 4)).astype(bfl))
    d["negi4_bf"] = np.ascontiguousarray(np.tile(-I, (1, 4)).astype(bfl))
    d["umask_bf"] = np.triu(np.full((128, 128), 3.3e38, f32)).astype(bfl)
    d["ones_f32"] = np.ones((1, 128), f32)
    obd = np.zeros((128, 2), f32)
    for r in range(128):
        obd[r, (r // 32) % 2] = 1.0
    d["obd_bf"] = obd.astype(bfl)
    d["obdT_bf"] = np.ascontiguousarray(obd.T.astype(bfl))
    obd64 = np.zeros((128, 2), f32)
    for r in range(128):
        obd64[r, r // 64] = 1.0
    d["obd64_bf"] = obd64.astype(bfl)
    d["obd64T_bf"] = np.ascontiguousarray(obd64.T.astype(bfl))
    d["onc"] = np.ascontiguousarray(
        (np.tile(np.asarray(inp["L0_attn_onorm_w"], f32), 2) * 8.0)[:, None])
    return d


class Sem:
    __slots__ = ("h", "n")

    def __init__(self, h):
        self.h = h
        self.n = 0


class Em:
    ENGS = ("sync", "tensor", "vector", "scalar", "gpsimd")

    def __init__(self, nc, stack):
        self.nc = nc
        self.stack = stack
        self.cmds = {e: [] for e in self.ENGS}
        self.sems = {}

    def sem(self, name):
        if name not in self.sems:
            self.sems[name] = Sem(self.stack.enter_context(self.nc.semaphore(name)))
        return self.sems[name]

    def add(self, eng, fn, inc=None):
        if inc is not None:
            sem, k = inc
            sem.n += k
            self.cmds[eng].append(lambda e, fn=fn, h=sem.h, k=k: fn(e).then_inc(h, k))
        else:
            self.cmds[eng].append(fn)

    def wait(self, eng, sem, val=None):
        v = sem.n if val is None else val
        if v > 0:
            self.cmds[eng].append(lambda e, h=sem.h, v=v: e.wait_ge(h, v))

    def relay(self, eng, waitsem, val, incsem):
        self.wait(eng, waitsem, val)
        incsem.n += 1
        self.cmds[eng].append(lambda e, h=incsem.h: e.sem_inc(h, 1))
        return incsem.n

    def dma(self, eng, out, in_, sem, transpose=False):
        if transpose:
            self.add(eng, lambda e, o=out, i=in_: e.dma_start_transpose(o, i), inc=(sem, 16))
        else:
            self.add(eng, lambda e, o=out, i=in_: e.dma_start(out=o, in_=i), inc=(sem, 16))
        return sem.n

    def play(self, name):
        with self.nc.Block(name) as block:
            for eng in self.ENGS:
                dec = getattr(block, eng)
                cl = self.cmds[eng]

                def body(e, cl=cl):
                    for fn in cl:
                        fn(e)
                dec(body)
        self.cmds = {e: [] for e in self.ENGS}


def MM(out, lhsT, rhs, start=True, stop=True, tile_position=None):
    return lambda e: e.matmul(out, lhsT=lhsT, rhs=rhs, start=start, stop=stop,
                              skip_group_check=True, tile_position=tile_position)


def TR(out, in_, ident):
    return lambda e: e.transpose(out, in_, ident)


def CP(out, in_):
    return lambda e: e.tensor_copy(out, in_)


def TT(out, a, b, op):
    return lambda e: e.tensor_tensor(out=out, in0=a, in1=b, op=op)


def TS(out, a, s1, s2, op0, op1=ALU.bypass):
    return lambda e: e.tensor_scalar(out=out, in0=a, scalar1=s1, scalar2=s2, op0=op0, op1=op1)


def STT(out, a, s, b, op0, op1):
    return lambda e: e.scalar_tensor_tensor(out=out, in0=a, scalar=s, in1=b, op0=op0, op1=op1)


def ACT(out, in_, func, bias=None, scale=None):
    kw = {}
    if bias is not None:
        kw["bias"] = bias
    if scale is not None:
        kw["scale"] = scale
    return lambda e: e.activation(out=out, in_=in_, func=func, **kw)


def MEMSET(ap, v):
    return lambda e: e.memset(ap, v)


def RED(out, in_, op, negate=False):
    return lambda e: e.tensor_reduce(out=out, in_=in_, axis=AX, op=op, negate=negate)


def SCAN(out, d0, d1, init, op0, op1):
    return lambda e: e.tensor_tensor_scan(out=out, data0=d0, data1=d1, initial=init,
                                          op0=op0, op1=op1)


def build_program():
    nc = bass.Bass("TRN2", target_bir_lowering=False)
    stack = ExitStack()
    em = Em(nc, stack)
    f, bf = F32, BF16

    din = lambda name, shape, dt: nc.dram_tensor(name, shape, dt, kind="ExternalInput").ap()
    dout_ = lambda name, shape, dt: nc.dram_tensor(name, shape, dt, kind="ExternalOutput").ap()
    dint = lambda name, shape, dt: nc.dram_tensor(
        name, shape, dt, kind=("ExternalOutput" if DEBUG else "Internal")).ap()

    x_in = din("x", [TOK, D], f)
    wqT = din("wqT", [16, D, 128], f)
    wkT = din("wkT", [16, D, 128], f)
    wvT = din("wvT", [16, D, 128], f)
    wgT = din("wgT", [4, 16, 128, 512], f)
    woT = din("woT", [4, 16, 128, 512], f)
    wffn = din("wffn", [8, 4, 16, 128, 512], f)
    wbT = din("wbT", [16, 128, 32], f)
    wgkT = din("wgkT", [16, 128, 32], f)
    wgf = din("wgf", [16, 128, 8], f)
    fb = din("fb", [8, 2048], f)
    cw = {p: din(f"cw_{p}", [128, 16, 4], f) for p in "qkv"}
    cb = {p: din(f"cb_{p}", [128, 16], f) for p in "qkv"}
    bb_rep = din("bb_rep", [128, 32], f)
    bgk_rep = din("bgk_rep", [128, 32], f)
    dt_rep = din("dt_rep", [128, 32], f)
    alog_rep = din("alog_rep", [128, 32], f)
    dD_rep = din("dD_rep", [128, 32], f)
    onorm_rep = din("onorm_rep", [128, D], f)
    cs_c = din("cs_c", [128, T], f)
    cs_s = din("cs_s", [128, T], f)
    id_bf = din("id_bf", [128, 128], bf)
    id_f32 = din("id_f32", [128, 128], f)
    i4_bf = din("i4_bf", [128, 512], bf)
    negi4_bf = din("negi4_bf", [128, 512], bf)
    umask_bf = din("umask_bf", [128, 128], bf)
    ones_f32 = din("ones_f32", [1, 128], f)
    obd_bf = din("obd_bf", [128, 2], bf)
    obdT_bf = din("obdT_bf", [2, 128], bf)
    obd64_bf = din("obd64_bf", [128, 2], bf)
    obd64T_bf = din("obd64T_bf", [2, 128], bf)
    onc = din("onc", [128, 1], f)

    out_d = dout_("out", [TOK, D], f)
    xnT_d = dint("xnT_d", [128, 16, TOK], bf)
    qT_d = dint("qT_d", [D, TOK], bf)
    kT_d = dint("kT_d", [D, TOK], bf)
    v_d = dint("v_d", [TOK, D], bf)
    o_d = dint("o_d", [TOK, D], bf)
    s_d = dint("s_d", [TOK, D], f)
    dbg_d = dint("dbg_d", [128, NT * 72], f)
    wq_bf = dint("wq_bf", [16, D, 128], bf)
    wk_bf = dint("wk_bf", [16, D, 128], bf)
    wv_bf = dint("wv_bf", [16, D, 128], bf)
    wg_bf = dint("wg_bf", [4, 16, 128, 512], bf)
    wo_bf = dint("wo_bf", [4, 16, 128, 512], bf)
    wf_bf = dint("wf_bf", [8, 4, 16, 128, 512], bf)
    bounce_cg = dint("bounce_cg", [2, 32, 128], f)
    bounce_bl = dint("bounce_bl", [2, 1, 32], f)

    dS = em.sem("dS")
    dG = em.sem("dG")
    dConst = em.sem("dConst")
    dX = [em.sem("dX0"), em.sem("dX1")]
    dSpX = [em.sem("dSpX0"), em.sem("dSpX1")]
    dCqkv = em.sem("dCqkv")
    dCgw = em.sem("dCgw")
    dCff = em.sem("dCff")
    dXns = em.sem("dXns")
    dW = [em.sem("dW0"), em.sem("dW1")]
    dFin = em.sem("dFin")
    dVsp = em.sem("dVsp")
    dT = [em.sem("dT0"), em.sem("dT1")]
    dPr = [em.sem("dPr0"), em.sem("dPr1")]
    dOsp = [em.sem("dOsp0"), em.sem("dOsp1")]
    sTE = em.sem("sTE")
    rlyA = em.sem("rlyA")
    rlyG = em.sem("rlyG")
    sDV = em.sem("sDV")
    sAC = em.sem("sAC")
    sGP = em.sem("sGP")

    sb = lambda name, shape, dt: stack.enter_context(nc.sbuf_tensor(name, shape, dt))
    stack2 = ExitStack()
    sb2 = lambda name, shape, dt: stack2.enter_context(nc.sbuf_tensor(name, shape, dt))
    c_dD = sb("c_dD", [128, 32], f)
    c_idbf = sb("c_idbf", [128, 128], bf)
    c_idf = sb("c_idf", [128, 128], f)
    c_i4 = sb("c_i4", [128, 512], bf)
    c_negi4 = sb("c_negi4", [128, 512], bf)
    c_umask = sb("c_umask", [128, 128], bf)
    c_ones = sb("c_ones", [1, 128], f)
    c_obd = sb("c_obd", [128, 2], bf)
    c_obdT = sb("c_obdT", [2, 128], bf)
    c_eps5 = sb("c_eps5", [128, 1], f)
    c_eps6 = sb("c_eps6", [128, 1], f)
    c_eps6r = sb("c_eps6r", [128, 1], f)
    c_one = sb("c_one", [128, 1], f)
    c_cw = {p: sb(f"c_cw_{p}", [128, 16, 4], f) for p in "qkv"}
    c_cb = {p: sb(f"c_cb_{p}", [128, 16], f) for p in "qkv"}
    p_comb = sb("p_comb", [128, NT * 8], f)
    c_obd64 = sb("c_obd64", [128, 2], bf)
    c_obd64T = sb("c_obd64T", [2, 128], bf)
    c_onc = sb("c_onc", [128, 1], f)
    p_beta = sb2("p_beta", [128, NT * 32], f)
    p_negb = sb2("p_negb", [128, NT * 32], f)
    p_g = sb2("p_g", [128, NT * 32], f)
    p_S = [[sb2(f"p_S_{b}_{pp}", [128, 16 * 64], f) for pp in range(2)] for b in range(BPC)]
    p_Sbf = [sb2(f"p_Sbf_{b}", [128, 16 * 64], bf) for b in range(BPC)]
    c_csc = sb2("c_csc", [128, T], f)
    c_css = sb2("c_css", [128, T], f)

    names = list(get_activation_tables(nc.m.arch).keys())
    set_id = names.index("natural_log_exp_and_others")

    ph = ExitStack()
    sbl = lambda name, shape, dt: ph.enter_context(nc.sbuf_tensor(name, shape, dt))
    psl = lambda name, shape, dt: ph.enter_context(nc.psum_tensor(name, shape, dt))

    c_wbT = sbl("c_wbT", [128, 16, 32], f)
    c_wgkT = sbl("c_wgkT", [128, 16, 32], f)
    c_wgf = sbl("c_wgf", [128, 16, 8], f)
    c_bb = sbl("c_bb", [128, 32], f)
    c_bgkdt = sbl("c_bgkdt", [128, 32], f)
    c_nexpA = sbl("c_nexpA", [128, 32], f)
    s_bgk = sbl("s_bgk", [128, 32], f)
    s_dt = sbl("s_dt", [128, 32], f)
    s_alog = sbl("s_alog", [128, 32], f)
    s_x = [sbl(f"s_x{i}", [128, D], f) for i in range(2)]

    x_ld = [0] * NT
    x_ld[0] = em.dma("sync", s_x[0][:], x_in[0:128, :], dX[0])
    x_ld[1] = em.dma("sync", s_x[1][:], x_in[128:256, :], dX[1])
    for dst, src in [(c_idbf[:], id_bf), (c_idf[:], id_f32),
                     (c_wbT[:], wbT.rearrange("kb p n -> p kb n")),
                     (c_wgkT[:], wgkT.rearrange("kb p n -> p kb n")),
                     (c_wgf[:], wgf.rearrange("kb p n -> p kb n")),
                     (c_bb[:], bb_rep), (c_dD[:], dD_rep),
                     (c_i4[:], i4_bf),
                     (c_negi4[:], negi4_bf), (c_umask[:], umask_bf), (c_ones[:], ones_f32),
                     (c_obd[:], obd_bf), (c_obdT[:], obdT_bf),
                     (c_obd64[:], obd64_bf), (c_obd64T[:], obd64T_bf), (c_onc[:], onc),
                     (s_bgk[:], bgk_rep), (s_dt[:], dt_rep), (s_alog[:], alog_rep),
                     (c_csc[:], cs_c), (c_css[:], cs_s)]:
        em.dma("sync", dst, src, dConst)
    for p in "qkv":
        em.dma("sync", c_cw[p][:], cw[p], dConst)
        em.dma("sync", c_cb[p][:], cb[p], dConst)
    n_const = dConst.n

    for dst, src in [(wq_bf, wqT), (wk_bf, wkT), (wv_bf, wvT)]:
        em.dma("gpsimd", dst, src, dCqkv)
    cast_qkv = dCqkv.n
    cast_gwo = cast_all = 0

    def _preload(e, sid=set_id):
        return e.add_instruction(mybir.InstLoadActFuncSet(
            name=nc.get_next_instruction_name(), act_func_set_id=sid, ins=[], outs=[]))
    em.add("scalar", _preload)

    em.add("vector", MEMSET(c_eps5[:], 1e-5))
    em.add("vector", MEMSET(c_eps6[:], 1e-6))
    em.add("vector", MEMSET(c_eps6r[:], 64e-6))
    em.add("vector", MEMSET(c_one[:], 1.0), inc=(sDV, 1))
    em.wait("vector", dConst, n_const)
    em.add("vector", TT(c_bgkdt[:], s_bgk[:], s_dt[:], ALU.add), inc=(sDV, 1))
    bgkdt_done = sDV.n
    em.wait("scalar", dConst, n_const)
    em.add("scalar", ACT(c_nexpA[:], s_alog[:], AF.Exp))


    s_xnbf = [sbl(f"s_xnbf{i}", [128, D], bf) for i in range(2)]
    s_xnf = [sbl(f"s_xnf{i}", [128, D], f) for i in range(2)]
    s_xTf = [sbl(f"s_xTf{i}", [128, 16 * 128], f) for i in range(2)]
    s_hi = [sbl(f"s_hi{i}", [128, 16, 1024], bf) for i in range(2)]
    s_bnst = [sbl(f"s_bnst{i}", [128, 24], f) for i in range(2)]
    s_mv = [sbl(f"s_mv{i}", [128, 2], f) for i in range(2)]
    s_rstd = [sbl(f"s_rstd{i}", [128, 1], f) for i in range(2)]
    s_mb = [sbl(f"s_mb{i}", [128, 1], f) for i in range(2)]
    s_sm = sbl("s_sm", [128, 8], f)
    s_e8 = sbl("s_e8", [128, 8], f)
    s_msk = sbl("s_msk", [128, 8], f)
    s_t32 = sbl("s_t32", [128, 32], f)
    s_t32b = sbl("s_t32b", [128, 32], f)
    s_t32c = sbl("s_t32c", [128, 32], f)
    ps_tbf = [psl(f"ps_tbf{i}", [128, 512], bf) for i in range(2)]
    ps_tf = [psl(f"ps_tf{i}", [128, 512], f) for i in range(2)]
    ps_small = psl("ps_small", [128, 512], f)

    xn_done = [0] * NT
    tile_ev = [0] * NT
    small_done = [0] * NT
    epi_done = [0] * NT
    spill = {}

    for i in range(NT):
        par = i % 2
        seg = i // 8
        if i >= 2:
            em.wait("sync", sAC, xn_done[i - 2])
            x_ld[i] = em.dma("sync", s_x[par][:], x_in[i * 128:(i + 1) * 128, :], dX[par])
        em.wait("vector", dX[par], x_ld[i])
        for j in range(4):
            em.add("vector", lambda e, j=j, par=par: e.bn_stats(
                s_bnst[par][:, j * 6:(j + 1) * 6], s_x[par][:, j * 512:(j + 1) * 512]),
                   inc=(sDV, 1) if j == 3 else None)
        stats_dv = sDV.n
        rv = em.relay("scalar", sDV, stats_dv, rlyA)
        em.wait("vector", rlyA, rv)
        em.add("vector", lambda e, par=par: e.bn_aggr(s_mv[par][:], s_bnst[par][:]),
               inc=(sDV, 1))
        bn_done = sDV.n
        em.wait("scalar", sDV, bn_done)
        em.add("scalar", ACT(s_rstd[par][:], s_mv[par][:, 1:2], AF.Ln, bias=c_eps5[:, 0:1]))
        em.add("scalar", lambda e, par=par: e.activation(
            out=s_rstd[par][:], in_=s_rstd[par][:], func=AF.Exp, scale=-0.5), inc=(sAC, 1))
        rstd_done = sAC.n
        em.wait("vector", sAC, rstd_done)
        em.add("vector", lambda e, par=par: e.scalar_tensor_tensor(
            out=s_mb[par][:], in0=s_mv[par][:, 0:1], scalar=-1.0, in1=s_rstd[par][:],
            op0=ALU.mult, op1=ALU.mult), inc=(sDV, 1))
        mb_done = sDV.n
        em.wait("scalar", sDV, mb_done)
        if i >= 2:
            em.wait("scalar", sTE, small_done[i - 2])
        em.add("scalar", lambda e, par=par: e.activation(
            out=s_xnbf[par][:], in_=s_x[par][:], func=AF.Identity,
            bias=s_mb[par][:, 0:1], scale=s_rstd[par][:, 0:1]))
        em.add("scalar", lambda e, par=par: e.activation(
            out=s_xnf[par][:], in_=s_x[par][:], func=AF.Identity,
            bias=s_mb[par][:, 0:1], scale=s_rstd[par][:, 0:1]), inc=(sAC, 1))
        xn_done[i] = sAC.n
        em.wait("tensor", sAC, xn_done[i])
        if i >= 1:
            em.wait("tensor", sDV, tile_ev[i - 1])
        segcol = (i % 8) * 128
        if i % 8 == 0 and seg >= 2:
            em.wait("vector", dSpX[seg % 2], spill[seg - 2])
        if i >= 2:
            em.wait("vector", sTE, small_done[i - 2])
        for half in range(2):
            for j8 in range(8):
                j = half * 8 + j8
                bank = ps_tbf[j8 // 4]
                em.add("tensor", TR(bank[:, (j8 % 4) * 128:(j8 % 4 + 1) * 128],
                                    s_xnbf[par][:, j * 128:(j + 1) * 128], c_idbf[:]),
                       inc=(sTE, 1) if j8 == 7 else None)
            tb = sTE.n
            for j8 in range(8):
                j = half * 8 + j8
                bank = ps_tf[j8 // 4]
                em.add("tensor", TR(bank[:, (j8 % 4) * 128:(j8 % 4 + 1) * 128],
                                    s_xnf[par][:, j * 128:(j + 1) * 128], c_idf[:]),
                       inc=(sTE, 1) if j8 == 7 else None)
            tf_ = sTE.n
            em.wait("vector", sTE, tb)
            for q in range(2):
                j0 = half * 8 + q * 4
                em.add("vector", lambda e, q=q, j0=j0, seg=seg, segcol=segcol: e.tensor_copy(
                    s_hi[seg % 2][:, j0:j0 + 4, segcol:segcol + 128], ps_tbf[q][:]))
            em.wait("vector", sTE, tf_)
            for q in range(2):
                j0 = half * 8 + q * 4
                em.add("vector", lambda e, q=q, j0=j0, par=par: e.tensor_copy(
                    s_xTf[par][:, j0 * 128:(j0 + 4) * 128], ps_tf[q][:]),
                       inc=(sDV, 1) if q == 1 else None)
            if half == 0:
                em.wait("tensor", sDV, sDV.n)
        tile_ev[i] = sDV.n
        if i % 8 == 7:
            em.wait("sync", sDV, tile_ev[i])
            spill[seg] = em.dma("sync", xnT_d[:, :, seg * 1024:(seg + 1) * 1024],
                                s_hi[seg % 2][:], dSpX[seg % 2])
        em.wait("tensor", sDV, tile_ev[i])
        if i >= 1:
            em.wait("tensor", sDV, epi_done[i - 1])
        for j in range(16):
            em.add("tensor", MM(ps_small[:, 0:32], s_xTf[par][:, j * 128:(j + 1) * 128],
                                c_wbT[:, j, :], start=(j == 0), stop=(j == 15)))
        for j in range(16):
            em.add("tensor", MM(ps_small[:, 32:64], s_xTf[par][:, j * 128:(j + 1) * 128],
                                c_wgkT[:, j, :], start=False, stop=(j == 15)))
        for j in range(16):
            em.add("tensor", MM(ps_small[:, 64:72], s_xTf[par][:, j * 128:(j + 1) * 128],
                                c_wgf[:, j, :], start=False, stop=(j == 15)),
                   inc=(sTE, 1) if j == 15 else None)
        small_done[i] = sTE.n
        em.wait("vector", sTE, small_done[i])
        em.wait("vector", sDV, bgkdt_done)
        em.add("vector", TT(s_t32[:], ps_small[:, 0:32], c_bb[:], ALU.add))
        em.add("vector", TT(s_t32b[:], ps_small[:, 32:64], c_bgkdt[:], ALU.add),
               inc=(sDV, 1))
        raw_done = sDV.n
        em.wait("scalar", sDV, raw_done)
        em.add("scalar", ACT(s_t32c[:], s_t32[:], AF.Exp, scale=-1.0))
        em.add("scalar", ACT(s_t32c[:], s_t32c[:], AF.Ln, bias=c_one[:, 0:1]))
        em.add("scalar", ACT(p_beta[:, i * 32:(i + 1) * 32], s_t32c[:], AF.Exp, scale=-1.0),
               inc=(sAC, 1))
        beta_done = sAC.n
        em.add("scalar", ACT(s_t32[:], s_t32b[:], AF.Abs))
        em.add("scalar", ACT(s_t32[:], s_t32[:], AF.Exp, scale=-1.0))
        em.add("scalar", ACT(s_t32[:], s_t32[:], AF.Ln, bias=c_one[:, 0:1]), inc=(sAC, 1))
        spl_done = sAC.n
        em.wait("vector", sAC, spl_done)
        em.add("vector", TS(s_t32c[:], s_t32b[:], 0.0, 0.0, ALU.max))
        em.add("vector", RED(s_sm[:, 0:1], ps_small[:, 64:72], ALU.max, negate=True))
        em.add("vector", TS(p_negb[:, i * 32:(i + 1) * 32], p_beta[:, i * 32:(i + 1) * 32],
                            -1.0, 0.0, ALU.mult), inc=(sDV, 1))
        red_dv = sDV.n
        rv = em.relay("scalar", sDV, red_dv, rlyA)
        em.wait("vector", rlyA, rv)
        em.add("vector", TT(p_g[:, i * 32:(i + 1) * 32], s_t32[:], s_t32c[:], ALU.add))
        em.add("vector", TS(s_msk[:], ps_small[:, 64:72], s_sm[:, 0:1], 0.0,
                            ALU.add, ALU.is_ge), inc=(sDV, 1))
        nm_done = sDV.n
        em.wait("scalar", sDV, nm_done)
        em.add("scalar", ACT(s_e8[:], ps_small[:, 64:72], AF.Exp, bias=s_sm[:, 0:1]),
               inc=(sAC, 1))
        e8_done = sAC.n
        em.wait("vector", sAC, e8_done)
        em.add("vector", RED(s_sm[:, 1:2], s_e8[:], ALU.add), inc=(sDV, 1))
        ssum_done = sDV.n
        em.wait("scalar", sDV, ssum_done)
        em.add("scalar", ACT(s_sm[:, 2:3], s_sm[:, 1:2], AF.Ln))
        em.add("scalar", ACT(s_sm[:, 3:4], s_sm[:, 2:3], AF.Exp, scale=-1.0), inc=(sAC, 1))
        rs_done = sAC.n
        em.wait("vector", sAC, rs_done)
        em.add("vector", STT(p_comb[:, i * 8:(i + 1) * 8], s_e8[:], s_sm[:, 3:4], s_msk[:],
                             ALU.mult, ALU.mult), inc=(sDV, 1))
        epi_done[i] = sDV.n

    em.add("scalar", ACT(c_nexpA[:], s_alog[:], AF.Exp))
    em.add("scalar", lambda e: e.activation(out=c_nexpA[:], in_=c_nexpA[:],
                                            func=AF.Identity, scale=-1.0), inc=(sAC, 1))
    nexpA_done = sAC.n
    em.wait("vector", sAC, nexpA_done)
    for i in range(NT):
        em.add("vector", TT(p_g[:, i * 32:(i + 1) * 32], p_g[:, i * 32:(i + 1) * 32],
                            c_nexpA[:], ALU.mult), inc=(sDV, 1) if i == NT - 1 else None)
    if DEBUG:
        em.wait("sync", sDV, sDV.n)
        em.wait("sync", sAC, sAC.n)
        for nm, src, w in (("beta", p_beta, 32), ("g", p_g, 32), ("comb", p_comb, 8)):
            off = {"beta": 0, "g": NT * 32, "comb": NT * 64}[nm]
            em.dma("sync", dbg_d[:, off:off + NT * w], src[:], dS)
    em.wait("sync", dS, dS.n)
    for sm_ in (dConst, dX[0], dX[1], dSpX[0], dSpX[1]):
        em.wait("sync", sm_, sm_.n)
    em.wait("sync", sDV, sDV.n)
    em.wait("sync", sAC, sAC.n)
    em.wait("sync", sTE, sTE.n)
    em.play("p1")
    ph.close()

    if PHASES >= 3:
        ph = ExitStack()
        sbl = lambda name, shape, dt: ph.enter_context(nc.sbuf_tensor(name, shape, dt))
        psl = lambda name, shape, dt: ph.enter_context(nc.psum_tensor(name, shape, dt))
        HT = T // 2
        s_xns = sbl("s_xns", [128, 16, HT], bf)
        s_wt = [sbl(f"s_wt{i}", [128, 16, 128], bf) for i in range(2)]
        s_strip = [sbl(f"s_strip{i}", [128, 3 + HT], f) for i in range(2)]
        s_hist = sbl("s_hist", [128, 48, 3], f)
        s_conv = [[sbl(f"s_conv{i}_{j}", [128, HT], f) for j in range(2)] for i in range(2)]
        s_act = [sbl(f"s_act{i}", [128, HT], bf) for i in range(2)]
        s_u = [sbl(f"s_u{i}", [128, HT], f) for i in range(2)]
        s_v = [sbl(f"s_v{i}", [128, HT], f) for i in range(2)]
        s_rope = [sbl(f"s_rope{i}", [128, HT], bf) for i in range(2)]
        s_sq = [sbl(f"s_sq{i}", [128, HT], bf) for i in range(2)]
        s_rsq = sbl("s_rsq", [2, 512], f)
        s_rsqb = [sbl(f"s_rsqb{i}", [2, HT], bf) for i in range(2)]
        s_fin = [sbl(f"s_fin{i}", [128, HT], bf) for i in range(2)]
        s_vt = [sbl(f"s_vt{i}", [128, 8, 128], bf) for i in range(2)]
        ps_pr = [psl(f"ps_pr{i}", [128, 512], f) for i in range(4)]
        ps_ss = psl("ps_ss", [128, 512], f)
        ps_rb = [psl(f"ps_rb{i}", [128, 512], f) for i in range(2)]
        ps_vt = psl("ps_vt", [128, 512], bf)
        dFin2 = [dFin, em.sem("dFin2")]
        dVsp2 = [dVsp, em.sem("dVsp2")]

        em.wait("gpsimd", dCqkv, cast_qkv)
        em.wait("sync", dCqkv, cast_qkv)
        strips = []
        for sq_ in range(2):
            for hf in range(2):
                for ip, p in enumerate("qkv"):
                    for d2 in range(16):
                        strips.append((sq_, hf, p, ip * 16 + d2, d2))
        NS3 = len(strips)
        proj_te = [0] * NS3
        evict_dv = [0] * NS3
        sg_ac = [0] * NS3
        act_gp = [0] * NS3
        sq_ac3 = [0] * NS3
        cte = [0] * NS3
        rsqb_ac = [0] * NS3
        fin_dv = [0] * NS3
        w_ld3 = [0, 0]
        seq_ld = 0
        fin_dma = [0, 0]
        v_dma = [0, 0]
        last_ss_ac = 0
        last_vt_dv = 0
        last_rb_fin = 0
        for it in range(NS3 + 3):
            if it < NS3:
                i = it
                sq_, hf, p, hidx, d2 = strips[i]
                par = i % 2
                col0 = sq_ * T + hf * HT
                if i % 48 == 0:
                    if i:
                        em.wait("sync", sTE, proj_te[i - 1])
                    for qq in range(4):
                        seq_ld = em.dma("sync", s_xns[:, qq * 4:(qq + 1) * 4, :],
                                        xnT_d[:, qq * 4:(qq + 1) * 4, col0:col0 + HT],
                                        dXns)
                wbf = {"q": wq_bf, "k": wk_bf, "v": wv_bf}[p]
                if i >= 2:
                    em.wait("sync", sTE, proj_te[i - 2])
                w_ld3[par] = em.dma("sync", s_wt[par][:],
                                    wbf[d2].rearrange("(kb p) m -> p kb m", kb=16, p=128),
                                    dW[par])
                em.wait("tensor", dW[par], w_ld3[par])
                em.wait("tensor", dXns, seq_ld)
                if i >= 2:
                    em.wait("tensor", sDV, evict_dv[i - 2])
                for tn in range(2):
                    bank = ps_pr[2 * par + tn]
                    for kb in range(16):
                        em.add("tensor", MM(bank[:], s_wt[par][:, kb, :],
                                            s_xns[:, kb, tn * 512:(tn + 1) * 512],
                                            start=(kb == 0), stop=(kb == 15)),
                               inc=(sTE, 1) if (tn == 1 and kb == 15) else None)
                proj_te[i] = sTE.n
            if it >= 2 and it - 2 < NS3:
                i = it - 2
                sq_, hf, p, hidx, d2 = strips[i]
                par = i % 2
                if p in "qk":
                    for tn in range(2):
                        em.wait("tensor", sAC, last_ss_ac)
                        if tn == 0:
                            em.wait("tensor", sAC, sq_ac3[i])
                        em.add("tensor", MM(ps_ss[0:2, 0:512], c_obd[:],
                                            s_sq[par][:, tn * 512:(tn + 1) * 512]),
                               inc=(sTE, 1))
                        te3 = sTE.n
                        em.wait("scalar", sTE, te3)
                        em.add("scalar", ACT(s_rsq[:], ps_ss[0:2, 0:512], AF.Ln,
                                             bias=c_eps6[0:2, 0:1]))
                        em.add("scalar", lambda e, par=par, tn=tn: e.activation(
                            out=s_rsqb[par][:, tn * 512:(tn + 1) * 512], in_=s_rsq[:],
                            func=AF.Exp, scale=-0.5), inc=(sAC, 1))
                        last_ss_ac = sAC.n
                    rsqb_ac[i] = sAC.n
                    cte[i] = sTE.n
                else:
                    em.wait("tensor", sGP, act_gp[i])
                    for rnd in range(2):
                        em.wait("tensor", sDV, last_vt_dv)
                        for j4 in range(4):
                            j8 = rnd * 4 + j4
                            em.add("tensor", TR(ps_vt[:, j4 * 128:(j4 + 1) * 128],
                                                s_act[par][:, j8 * 128:(j8 + 1) * 128],
                                                c_idbf[:]),
                                   inc=(sTE, 1) if j4 == 3 else None)
                        te3 = sTE.n
                        em.wait("vector", sTE, te3)
                        if rnd == 0:
                            em.wait("vector", dVsp2[par], v_dma[par])
                        em.add("vector", lambda e, par=par, rnd=rnd: e.tensor_copy(
                            s_vt[par][:, rnd * 4:rnd * 4 + 4, :], ps_vt[:]),
                               inc=(sDV, 1))
                        last_vt_dv = sDV.n
                    cte[i] = sTE.n
                    em.wait("gpsimd", sDV, last_vt_dv)
                    v_dma[par] = em.dma(
                        "gpsimd",
                        v_d.rearrange("(u j p) c -> u p j c", u=4, j=8, p=128)[
                            sq_ * 2 + hf, :, :, d2 * 128:(d2 + 1) * 128],
                        s_vt[par][:], dVsp2[par])
            if it >= 3 and it - 3 < NS3:
                i = it - 3
                sq_, hf, p, hidx, d2 = strips[i]
                par = i % 2
                if p in "qk":
                    col0 = sq_ * T + hf * HT
                    em.wait("tensor", sAC, rsqb_ac[i])
                    em.wait("tensor", sDV, last_rb_fin)
                    for tn in range(2):
                        em.add("tensor", MM(ps_rb[tn][:], c_obdT[:],
                                            s_rsqb[par][:, tn * 512:(tn + 1) * 512]),
                               inc=(sTE, 1) if tn == 1 else None)
                    rbte = sTE.n
                    em.wait("vector", sTE, rbte)
                    em.wait("vector", dFin2[par], fin_dma[par])
                    for tn in range(2):
                        em.add("vector", STT(s_fin[par][:, tn * 512:(tn + 1) * 512],
                                             ps_rb[tn][:], 1.0,
                                             s_rope[par][:, tn * 512:(tn + 1) * 512],
                                             ALU.mult, ALU.mult),
                               inc=(sDV, 1) if tn == 1 else None)
                    fin_dv[i] = sDV.n
                    last_rb_fin = fin_dv[i]
                    em.wait("sync", sDV, fin_dv[i])
                    tgt = (qT_d if p == "q" else kT_d)
                    dst = tgt.rearrange("(d b a r) t -> d a b r t", d=16, b=2, a=2, r=32)
                    em.dma("sync", dst[d2, 0, :, :, col0:col0 + HT],
                           s_fin[par][0:64, :], dFin2[par])
                    fin_dma[par] = em.dma("sync", dst[d2, 1, :, :, col0:col0 + HT],
                                          s_fin[par][64:128, :], dFin2[par])
            if it >= 1 and it - 1 < NS3:
                i = it - 1
                sq_, hf, p, hidx, d2 = strips[i]
                par = i % 2
                cwp, cbp = c_cw[p], c_cb[p]
                em.wait("vector", sTE, proj_te[i])
                if i >= 2:
                    em.wait("vector", sAC, sg_ac[i - 2])
                if hf == 0:
                    em.add("vector", MEMSET(s_strip[par][:, 0:3], 0.0))
                else:
                    em.add("vector", CP(s_strip[par][:, 0:3], s_hist[:, hidx, :]))
                for tn in range(2):
                    em.add("vector", CP(s_strip[par][:, 3 + tn * 512:3 + (tn + 1) * 512],
                                        ps_pr[2 * par + tn][:]),
                           inc=(sDV, 1) if tn == 1 else None)
                evict_dv[i] = sDV.n
                rv = em.relay("scalar", sDV, evict_dv[i], rlyA)
                em.wait("vector", rlyA, rv)
                em.add("vector", TS(s_conv[par][0][:], s_strip[par][:, 0:HT],
                                    cwp[:, d2, 0:1], cbp[:, d2:d2 + 1], ALU.mult, ALU.add))
                em.add("vector", STT(s_conv[par][1][:], s_strip[par][:, 1:1 + HT],
                                     cwp[:, d2, 1:2], s_conv[par][0][:],
                                     ALU.mult, ALU.add))
                em.add("vector", STT(s_conv[par][0][:], s_strip[par][:, 2:2 + HT],
                                     cwp[:, d2, 2:3], s_conv[par][1][:],
                                     ALU.mult, ALU.add))
                em.add("vector", STT(s_conv[par][1][:], s_strip[par][:, 3:3 + HT],
                                     cwp[:, d2, 3:4], s_conv[par][0][:],
                                     ALU.mult, ALU.add), inc=(sDV, 1))
                conv_done = sDV.n
                if hf == 0:
                    em.add("vector", CP(s_hist[:, hidx, :], s_strip[par][:, HT:HT + 3]))
                em.wait("scalar", sDV, conv_done)
                em.add("scalar", ACT(s_conv[par][0][:], s_conv[par][1][:], AF.Exp,
                                     scale=-1.0))
                em.add("scalar", ACT(s_strip[par][:, 0:HT], s_conv[par][0][:], AF.Ln,
                                     bias=c_one[:, 0:1]))
                em.add("scalar", ACT(s_conv[par][0][:], s_strip[par][:, 0:HT], AF.Exp,
                                     scale=-1.0), inc=(sAC, 1))
                sg_ac[i] = sAC.n
                em.wait("gpsimd", sAC, sg_ac[i])
                if i >= 2:
                    em.wait("gpsimd", sTE, cte[i - 2])
                em.add("gpsimd", TT(s_act[par][:], s_conv[par][1][:], s_conv[par][0][:],
                                    ALU.mult), inc=(sGP, 1))
                act_gp[i] = sGP.n
                if p in "qk":
                    csl = slice(hf * HT, (hf + 1) * HT)
                    em.add("gpsimd", TT(s_u[par][:], s_act[par][:], c_csc[:, csl],
                                        ALU.mult))
                    em.add("gpsimd", TT(s_v[par][0:64, :], s_act[par][64:128, :],
                                        c_css[64:128, csl], ALU.mult))
                    em.add("gpsimd", TT(s_v[par][64:128, :], s_act[par][0:64, :],
                                        c_css[0:64, csl], ALU.mult), inc=(sGP, 1))
                    uv_gp = sGP.n
                    em.wait("vector", sGP, uv_gp)
                    if i >= 2:
                        em.wait("vector", sDV, fin_dv[i - 2])
                    em.add("vector", TT(s_rope[par][0:64, :], s_u[par][0:64, :],
                                        s_v[par][0:64, :], ALU.subtract))
                    em.add("vector", TT(s_rope[par][64:128, :], s_u[par][64:128, :],
                                        s_v[par][64:128, :], ALU.add), inc=(sDV, 1))
                    rope_done = sDV.n
                    em.wait("scalar", sDV, rope_done)
                    if i >= 2:
                        em.wait("scalar", sTE, cte[i - 2])
                    em.add("scalar", ACT(s_sq[par][:], s_rope[par][:], AF.Square),
                           inc=(sAC, 1))
                    sq_ac3[i] = sAC.n
        for sm_ in (dXns, dW[0], dW[1], dFin, dFin2[1]):
            em.wait("sync", sm_, sm_.n)
        em.wait("sync", sDV, sDV.n)
        em.wait("sync", sAC, sAC.n)
        em.wait("sync", sGP, sGP.n)
        em.wait("gpsimd", dVsp, dVsp.n)
        em.wait("gpsimd", dVsp2[1], dVsp2[1].n)
        em.play("p3")
        ph.close()

    if PHASES >= 4:
        ph = ExitStack()
        sbl = lambda name, shape, dt: ph.enter_context(nc.sbuf_tensor(name, shape, dt))
        psl = lambda name, shape, dt: ph.enter_context(nc.psum_tensor(name, shape, dt))
        B0 = psl("p4_B0", [128, 512], f)
        B1 = psl("p4_B1", [128, 512], f)
        B2 = psl("p4_B2", [128, 512], f)
        B3 = psl("p4_B3", [128, 512], bf)
        B4 = psl("p4_B4", [128, 512], f)
        B5 = psl("p4_B5", [128, 512], f)
        B6 = psl("p4_B6", [128, 512], f)
        B7 = psl("p4_B7", [128, 512], f)
        s_qc = [sbl(f"s_qc{i}", [128, 16, 128], bf) for i in range(2)]
        s_kc = [sbl(f"s_kc{i}", [128, 16, 128], bf) for i in range(2)]
        s_kt = [sbl(f"s_kt{i}", [128, 2048], bf) for i in range(2)]
        s_vk = [sbl(f"s_vk{i}", [128, 2048], bf) for i in range(2)]
        s_oa = [sbl(f"s_oa{i}", [128, 2048], bf) for i in range(2)]
        s_gT = [sbl(f"s_gT{i}", [32, 128], f) for i in range(2)]
        s_cgT = [sbl(f"s_cgT{i}", [32, 128], f) for i in range(2)]
        s_cgtok = [sbl(f"s_cgtok{i}", [128, 32], f) for i in range(2)]
        s_negcg = [sbl(f"s_negcg{i}", [128, 32], f) for i in range(2)]
        s_ecg = [sbl(f"s_ecg{i}", [128, 32], f) for i in range(2)]
        s_cgf = [sbl(f"s_cgf{i}", [1, 4096], f) for i in range(2)]
        s_blr = [sbl(f"s_blr{i}", [1, 32], f) for i in range(2)]
        s_BL = [sbl(f"s_BL{i}", [128, 32], f) for i in range(2)]
        s_kkr = [sbl(f"s_kkr{i}", [128, 512], bf) for i in range(2)]
        s_qkr = [sbl(f"s_qkr{i}", [128, 512], bf) for i in range(2)]
        s_dtr = [sbl(f"s_dtr{i}", [128, 512], bf) for i in range(2)]
        s_dt4 = [sbl(f"s_dt4{i}", [128, 512], bf) for i in range(2)]
        s_D4 = [sbl(f"s_D4{i}", [128, 512], bf) for i in range(2)]
        s_bn4 = [sbl(f"s_bn4{i}", [128, 512], bf) for i in range(2)]
        s_bt4 = [sbl(f"s_bt4{i}", [128, 512], bf) for i in range(2)]
        s_pw = [[sbl(f"s_pw{i}_{j}", [128, 512], bf) for j in range(2)] for i in range(2)]
        s_tp = [[sbl(f"s_tp{i}_{j}", [128, 512], bf) for j in range(2)] for i in range(2)]
        s_hh = [[sbl(f"s_hh{i}_{j}", [128, 512], bf) for j in range(2)] for i in range(2)]
        s_U4 = [sbl(f"s_U4{i}", [128, 256], bf) for i in range(2)]
        s_Ub4 = [sbl(f"s_Ub4{i}", [128, 256], bf) for i in range(2)]
        s_R4 = [sbl(f"s_R4{i}", [128, 256], bf) for i in range(2)]
        s_r14 = [sbl(f"s_r14{i}", [128, 256], f) for i in range(2)]
        s_o14 = [sbl(f"s_o14{i}", [128, 256], f) for i in range(2)]
        s_w4 = [sbl(f"s_w4_{i}", [128, 4], f) for i in range(2)]
        s_o24 = [sbl(f"s_o24{i}", [128, 256], f) for i in range(2)]

        for b_ in range(BPC):
            em.add("vector", MEMSET(p_S[b_][0][:], 0.0))
            em.add("vector", MEMSET(p_S[b_][1][:], 0.0))
            em.add("vector", MEMSET(p_Sbf[b_][:], 0.0), inc=(sDV, 1))
        s_zero_dv = sDV.n

        NCHUNKS = int(os.environ.get("MK_P4_CHUNKS", 2 * NCH))
        sched = [(ci % 2, ci // 2) for ci in range(NCHUNKS)]
        prep_dma = [0] * NCHUNKS
        prep_pr = [0] * NCHUNKS
        prep_cgf = [0] * NCHUNKS
        prep_dv = [0] * NCHUNKS
        prep_ac = [0] * NCHUNKS
        ch_lastuse_te = [0] * NCHUNKS
        ch_o_spill = [0] * NCHUNKS
        ch_blend_ac = [0] * NCHUNKS
        ch_o_dv = [0] * NCHUNKS
        st = {"rb_free_ac": 0, "kq_free_dv": 0, "B2_free": 0, "B1s_free_ac": 0,
              "B3_free": 0, "B4_free": 0, "B5_free": 0, "b6_free": 0,
              "scr_free": [0, 0]}

        def emit_prep(ci):
            b, c = sched[ci]
            par = ci % 2
            i_tile = b * NCH + c
            row0 = b * T + c * CH
            if ci >= 2:
                em.wait("sync", sTE, ch_lastuse_te[ci - 2])
                em.wait("sync", sDV, ch_o_dv[ci - 2])
            em.dma("sync", s_qc[par][:],
                   qT_d.rearrange("(hp p) t -> p hp t", hp=16, p=128)[:, :, row0:row0 + 128],
                   dT[par])
            em.dma("sync", s_kc[par][:],
                   kT_d.rearrange("(hp p) t -> p hp t", hp=16, p=128)[:, :, row0:row0 + 128],
                   dT[par])
            em.dma("sync", s_kt[par][:], kT_d[:, row0:row0 + 128], dT[par], transpose=True)
            d1 = em.dma("sync", s_vk[par][:], v_d[row0:row0 + 128, :], dT[par])
            em.wait("tensor", sDV, st["b6_free"])
            em.add("tensor", TR(B6[0:32, 0:128], p_g[:, i_tile * 32:(i_tile + 1) * 32],
                                c_idf[:]), inc=(sTE, 1))
            gT_te = sTE.n
            em.wait("vector", sTE, gT_te)
            em.add("vector", CP(s_gT[par][:], B6[0:32, 0:128]))
            em.add("vector", MEMSET(s_cgT[par][:], 0.0))
            em.add("vector", SCAN(s_cgT[par][:], s_gT[par][:], s_gT[par][:], 0.0,
                                  ALU.add, ALU.bypass), inc=(sDV, 1))
            scan_dv = sDV.n
            em.wait("sync", sDV, scan_dv)
            em.dma("sync", bounce_cg[par], s_cgT[par][:], dPr[par])
            em.wait("sync", dPr[par], dPr[par].n)
            em.dma("sync", s_cgf[par][0:1, :],
                   bounce_cg[par].rearrange("h t -> (h t)"), dPr[par])
            cgf_dma = dPr[par].n
            em.wait("tensor", sDV, scan_dv)
            em.add("tensor", TR(B6[0:128, 128:160], s_cgT[par][:], c_idf[0:32, 0:32]), inc=(sTE, 1))
            cgtok_te = sTE.n
            em.wait("vector", sTE, cgtok_te)
            em.add("vector", CP(s_cgtok[par][:], B6[:, 128:160]))
            em.add("vector", TS(s_negcg[par][:], B6[:, 128:160], -1.0, 0.0, ALU.mult),
                   inc=(sDV, 1))
            cg_dv = sDV.n
            em.wait("scalar", sDV, cg_dv)
            em.add("scalar", ACT(s_ecg[par][:], s_cgtok[par][:], AF.Exp), inc=(sAC, 1))
            ecg_ac = sAC.n
            em.wait("sync", sAC, ecg_ac)
            em.dma("sync", bounce_bl[par], s_ecg[par][127:128, 0:32], dPr[par])
            em.wait("sync", dPr[par], dPr[par].n)
            em.dma("sync", s_blr[par][0:1, :], bounce_bl[par], dPr[par])
            blr_dma = dPr[par].n
            em.wait("tensor", dPr[par], blr_dma)
            em.wait("tensor", sDV, cg_dv)
            em.add("tensor", MM(B6[:, 160:192], c_ones[0:1, :], s_blr[par][0:1, :]),
                   inc=(sTE, 1))
            bl_te = sTE.n
            em.wait("vector", sTE, bl_te)
            em.add("vector", CP(s_BL[par][:], B6[:, 160:192]), inc=(sDV, 1))
            prep_dv[ci] = sDV.n
            prep_ac[ci] = ecg_ac
            prep_dma[ci] = d1
            prep_pr[ci] = blr_dma
            prep_cgf[ci] = cgf_dma
            st["b6_free"] = sDV.n

        gidx = 0
        for ci in range(NCHUNKS):
            if ci == 0:
                emit_prep(0)
                if NCHUNKS > 1:
                    emit_prep(1)
                for dst_, src_ in [(wg_bf, wgT), (wo_bf, woT)]:
                    em.dma("gpsimd", dst_, src_, dCgw)
                for e_ in range(E):
                    em.dma("gpsimd", wf_bf[e_], wffn[e_], dCff)
                cast_gwo = dCgw.n
                cast_all = dCff.n
            b, c = sched[ci]
            par = ci % 2
            i_tile = b * NCH + c
            row0 = b * T + c * CH
            Scur = p_S[b][c % 2]
            Snext = p_S[b][(c + 1) % 2]
            Sbf = p_Sbf[b]
            STAGE = int(os.environ.get("MK_P4_STAGE", 99))
            for grp in range(int(os.environ.get("MK_P4_GROUPS", 8))):
                gpar = gidx % 2
                o_dv = bl_dv = kq_dv = dtm_dv = r_dv = exp_ac = sDV.n
                heads = [4 * grp + j for j in range(4)]

                def hsl(buf, h):
                    return buf[64 * (h % 2):64 * (h % 2) + 64, h // 2, :]

                def stile(buf, h):
                    return buf[64 * (h % 2):64 * (h % 2) + 64,
                               (h // 2) * 64:(h // 2) * 64 + 64]

                em.wait("tensor", sAC, st["rb_free_ac"])
                em.wait("tensor", dT[par], prep_dma[ci])
                em.wait("tensor", dPr[par], prep_cgf[ci])
                em.add("tensor", MM(B7[:], c_ones[0:1, :],
                                    s_cgf[par][0:1, grp * 512:(grp + 1) * 512]), inc=(sTE, 1))
                rb_te = sTE.n
                em.wait("tensor", sDV, st["kq_free_dv"])
                em.wait("tensor", sDV, s_zero_dv)
                em.wait("tensor", sDV, st["B4_free"])
                if ci >= 2:
                    em.wait("tensor", sAC, ch_blend_ac[ci - 2])
                for j, h in enumerate(heads):
                    em.add("tensor", MM(([B0, B1][j % 2])[:, (j // 2) * 128:(j // 2) * 128 + 128],
                                        hsl(s_kc[par], h), hsl(s_kc[par], h), start=(j < 2)))
                for j, h in enumerate(heads):
                    em.add("tensor", MM(([B0, B1][j % 2])[:, 256 + (j // 2) * 128:256 + (j // 2) * 128 + 128],
                                        hsl(s_kc[par], h), hsl(s_qc[par], h), start=False))
                for j, h in enumerate(heads):
                    em.add("tensor", MM(([B4, B5][j % 2])[:, (j // 2) * 64:(j // 2) * 64 + 64],
                                        hsl(s_kc[par], h), stile(Sbf, h), start=(j < 2)))
                for j, h in enumerate(heads):
                    em.add("tensor", MM(([B4, B5][j % 2])[:, 128 + (j // 2) * 64:128 + (j // 2) * 64 + 64],
                                        hsl(s_qc[par], h), stile(Sbf, h), start=False),
                           inc=(sTE, 1) if j == 3 else None)
                l0_te = sTE.n
                if STAGE >= 2:
                    em.wait("scalar", sTE, rb_te)
                    em.wait("scalar", sDV, prep_dv[ci])
                    em.wait("scalar", sDV, st["scr_free"][gpar])
                    for j, h in enumerate(heads):
                        em.add("scalar", ACT(s_dtr[gpar][:, j * 128:(j + 1) * 128],
                                             B7[:, j * 128:(j + 1) * 128], AF.Exp,
                                             bias=s_negcg[par][:, h:h + 1]),
                               inc=(sAC, 1) if j == 3 else None)
                    exp_ac = sAC.n
                    st["rb_free_ac"] = exp_ac
                if STAGE >= 3:
                    em.wait("vector", sAC, exp_ac)
                    em.wait("vector", sDV, st["scr_free"][gpar])
                    for j in range(4):
                        em.add("vector", TT(s_dt4[gpar][:, j * 128:(j + 1) * 128],
                                            s_dtr[gpar][:, j * 128:(j + 1) * 128],
                                            c_umask[:], ALU.min),
                               inc=(sDV, 1) if j == 3 else None)
                    dtm_dv = sDV.n
                    rv = em.relay("gpsimd", sDV, dtm_dv, rlyG)
                if STAGE >= 4:
                    em.wait("vector", sTE, l0_te)
                    em.add("vector", CP(s_kkr[gpar][:], B0[:]))
                    em.add("vector", CP(s_qkr[gpar][:], B1[:]), inc=(sDV, 1))
                    kq_dv = sDV.n
                    em.wait("vector", rlyG, rv)
                    for j in range(4):
                        em.add("vector", CP(s_w4[gpar][:, j:j + 1],
                                            s_dt4[gpar][:, j * 128 + 127:j * 128 + 128]))
                    st["kq_free_dv"] = kq_dv
                if STAGE >= 5:
                    em.wait("vector", sDV, prep_dv[ci])
                    em.wait("vector", sAC, prep_ac[ci])
                    for j, h in enumerate(heads):
                        em.add("vector", STT(s_r14[gpar][:, j * 64:(j + 1) * 64],
                                             [B4, B5][j % 2][:, (j // 2) * 64:(j // 2) * 64 + 64],
                                             s_ecg[par][:, h:h + 1],
                                             s_vk[par][:, h * 64:(h + 1) * 64],
                                             ALU.mult, ALU.subtract),
                               inc=(sDV, 1) if j == 3 else None)
                    r1_dv = sDV.n
                    rv = em.relay("gpsimd", sDV, r1_dv, rlyG)
                    em.wait("vector", rlyG, rv)
                    for j, h in enumerate(heads):
                        em.add("vector", TS(s_R4[gpar][:, j * 64:(j + 1) * 64],
                                            s_r14[gpar][:, j * 64:(j + 1) * 64],
                                            p_negb[:, i_tile * 32 + h:i_tile * 32 + h + 1], 0.0,
                                            ALU.mult), inc=(sDV, 1) if j == 3 else None)
                    r_dv = sDV.n
                    for j, h in enumerate(heads):
                        em.add("vector", TS(s_o14[gpar][:, j * 64:(j + 1) * 64],
                                            [B4, B5][j % 2][:, 128 + (j // 2) * 64:
                                                            128 + (j // 2) * 64 + 64],
                                            s_ecg[par][:, h:h + 1], 0.0, ALU.mult),
                               inc=(sDV, 1) if j == 3 else None)
                    t1_dv = sDV.n
                if STAGE >= 6:
                    em.wait("tensor", sDV, dtm_dv)
                    em.wait("tensor", sDV, kq_dv)
                    em.wait("tensor", sDV, st["B3_free"])
                    for j in range(4):
                        em.add("tensor", TR(B3[:, j * 128:(j + 1) * 128],
                                            s_dt4[gpar][:, j * 128:(j + 1) * 128], c_idbf[:]),
                               inc=(sTE, 1) if j == 3 else None)
                    dtT_te = sTE.n
                    em.wait("vector", sTE, dtT_te)
                    em.add("vector", TT(s_D4[gpar][:], B3[:], c_negi4[:], ALU.add),
                           inc=(sDV, 1))
                    d_dv = sDV.n
                    rv = em.relay("gpsimd", sDV, d_dv, rlyG)
                    em.wait("vector", rlyG, rv)
                    for j, h in enumerate(heads):
                        em.add("vector", STT(s_bn4[gpar][:, j * 128:(j + 1) * 128],
                                             [s_kkr, s_qkr][j % 2][gpar]
                                             [:, (j // 2) * 128:(j // 2) * 128 + 128],
                                             p_negb[:, i_tile * 32 + h:i_tile * 32 + h + 1],
                                             s_D4[gpar][:, j * 128:(j + 1) * 128],
                                             ALU.mult, ALU.mult),
                               inc=(sDV, 1) if j == 3 else None)
                    bn_dv = sDV.n
                    em.wait("gpsimd", sDV, kq_dv)
                    em.wait("gpsimd", sDV, dtm_dv)
                    for j in range(4):
                        em.add("gpsimd", TT(s_dtr[gpar][:, j * 128:(j + 1) * 128],
                                            [s_kkr, s_qkr][j % 2][gpar]
                                            [:, 256 + (j // 2) * 128:256 + (j // 2) * 128 + 128],
                                            s_dt4[gpar][:, j * 128:(j + 1) * 128], ALU.mult),
                               inc=(sGP, 1) if j == 3 else None)
                    p2_gp = sGP.n
                if STAGE >= 7:
                    em.wait("tensor", sDV, bn_dv)
                    for j in range(4):
                        em.add("tensor", TR(B3[:, j * 128:(j + 1) * 128],
                                            s_bn4[gpar][:, j * 128:(j + 1) * 128], c_idbf[:]),
                               inc=(sTE, 1) if j == 3 else None)
                    bnT_te = sTE.n
                    em.wait("vector", sTE, bnT_te)
                    em.add("vector", CP(s_bt4[gpar][:], B3[:]))
                    em.add("vector", TT(s_hh[gpar][0][:], B3[:], c_i4[:], ALU.add),
                           inc=(sDV, 1))
                    bt_dv = sDV.n
                    st["B3_free"] = bt_dv
                if STAGE >= 8:
                    tpow_prev, pow_prev = s_bt4[gpar], s_bn4[gpar]
                    h_prev = s_hh[gpar][0]
                    ev_prev = bt_dv
                    tp_ac = 0
                    h_ev = 0
                    for lvl in range(1, 7):
                        lp = lvl % 2
                        em.wait("tensor", sDV, ev_prev)
                        em.wait("tensor", sDV, st["B2_free"])
                        if tp_ac:
                            em.wait("tensor", sAC, tp_ac)
                        if lvl < 6:
                            em.wait("tensor", sAC, st["B1s_free_ac"])
                            for j in range(4):
                                em.add("tensor", MM(B1[:, j * 128:(j + 1) * 128],
                                                    pow_prev[:, j * 128:(j + 1) * 128],
                                                    tpow_prev[:, j * 128:(j + 1) * 128],
                                                    start=(j == 0)))
                        for j in range(4):
                            em.add("tensor", MM(B2[:, j * 128:(j + 1) * 128],
                                                tpow_prev[:, j * 128:(j + 1) * 128],
                                                pow_prev[:, j * 128:(j + 1) * 128],
                                                start=(j == 0)),
                                   inc=(sTE, 1) if j == 3 else None)
                        pw_te = sTE.n
                        em.wait("vector", sTE, pw_te)
                        em.add("vector", CP(s_pw[gpar][lp][:], B2[:]), inc=(sDV, 1))
                        pow_dv = sDV.n
                        st["B2_free"] = pow_dv
                        if lvl < 6:
                            em.wait("scalar", sTE, pw_te)
                            em.add("scalar", lambda e, a=s_tp[gpar][lp], b2=B1: e.copy(
                                a[:], b2[:]), inc=(sAC, 1))
                            tp_ac = sAC.n
                            st["B1s_free_ac"] = tp_ac
                        em.wait("tensor", sDV, pow_dv)
                        if h_ev:
                            em.wait("tensor", sDV, h_ev)
                        for j in range(4):
                            em.add("tensor", MM(B2[:, j * 128:(j + 1) * 128], c_idbf[:],
                                                h_prev[:, j * 128:(j + 1) * 128],
                                                start=(j == 0), stop=False))
                            em.add("tensor", MM(B2[:, j * 128:(j + 1) * 128],
                                                s_pw[gpar][lp][:, j * 128:(j + 1) * 128],
                                                h_prev[:, j * 128:(j + 1) * 128],
                                                start=False, stop=True),
                                   inc=(sTE, 1) if j == 3 else None)
                        h_te = sTE.n
                        em.wait("vector", sTE, h_te)
                        em.add("vector", CP(s_hh[gpar][lp][:], B2[:]), inc=(sDV, 1))
                        h_ev = sDV.n
                        st["B2_free"] = h_ev
                        h_prev = s_hh[gpar][lp]
                        tpow_prev, pow_prev = s_tp[gpar][lp], s_pw[gpar][lp]
                        ev_prev = h_ev
                if STAGE >= 9:
                    em.wait("tensor", sDV, r_dv)
                    em.wait("tensor", sDV, st["B2_free"])
                    for j in range(4):
                        em.add("tensor", MM(B2[:, j * 64:(j + 1) * 64],
                                            h_prev[:, j * 128:(j + 1) * 128],
                                            s_R4[gpar][:, j * 64:(j + 1) * 64],
                                            start=(j == 0)),
                               inc=(sTE, 1) if j == 3 else None)
                    u_te = sTE.n
                    em.wait("vector", sTE, u_te)
                    em.add("vector", CP(s_U4[gpar][:], B2[:, 0:256]))
                    for j, h in enumerate(heads):
                        em.add("vector", TS(s_Ub4[gpar][:, j * 64:(j + 1) * 64],
                                            B2[:, j * 64:(j + 1) * 64],
                                            s_w4[gpar][:, j:j + 1], 0.0,
                                            ALU.mult), inc=(sDV, 1) if j == 3 else None)
                    u_dv = sDV.n
                if STAGE >= 10:
                    em.wait("tensor", sGP, p2_gp)
                    em.wait("tensor", sDV, u_dv)
                    for j in range(4):
                        em.add("tensor", MM(B2[:, 256 + j * 64:256 + (j + 1) * 64],
                                            s_dtr[gpar][:, j * 128:(j + 1) * 128],
                                            s_U4[gpar][:, j * 64:(j + 1) * 64],
                                            start=False))
                    em.wait("tensor", sDV, t1_dv)
                    for j, h in enumerate(heads):
                        em.add("tensor", MM(B4[64 * (h % 2):64 * (h % 2) + 64,
                                               256 + j * 64:256 + (j + 1) * 64],
                                            s_kt[par][:, h * 64:(h + 1) * 64],
                                            s_Ub4[gpar][:, j * 64:(j + 1) * 64],
                                            start=(j == 0),
                                            tile_position=(0, 64 * (h % 2))),
                               inc=(sTE, 1) if j == 3 else None)
                    oi_te = sTE.n
                    ch_lastuse_te[ci] = oi_te
                if STAGE >= 11:
                    em.wait("vector", sTE, oi_te)
                    if ci >= 2:
                        em.wait("vector", dOsp[par], ch_o_spill[ci - 2])
                    for j, h in enumerate(heads):
                        em.add("vector", STT(s_o24[gpar][:, j * 64:(j + 1) * 64],
                                             B2[:, 256 + j * 64:256 + (j + 1) * 64], 1.0,
                                             s_o14[gpar][:, j * 64:(j + 1) * 64],
                                             ALU.mult, ALU.add),
                               inc=(sDV, 1) if j == 3 else None)
                    o1_dv = sDV.n
                    rv = em.relay("gpsimd", sDV, o1_dv, rlyG)
                    em.wait("vector", rlyG, rv)
                    for j, h in enumerate(heads):
                        em.add("vector", STT(s_oa[par][:, h * 64:(h + 1) * 64],
                                             s_vk[par][:, h * 64:(h + 1) * 64],
                                             c_dD[:, h:h + 1],
                                             s_o24[gpar][:, j * 64:(j + 1) * 64],
                                             ALU.mult, ALU.add),
                               inc=(sDV, 1) if j == 3 else None)
                    o_dv = sDV.n
                    st["B2_free"] = o1_dv
                    st["scr_free"][gpar] = o_dv
                if STAGE >= 12:
                    for j, h in enumerate(heads):
                        em.add("vector", STT(stile(Snext, h), stile(Scur, h),
                                             s_BL[par][64 * (h % 2):64 * (h % 2) + 64, h:h + 1],
                                             B4[64 * (h % 2):64 * (h % 2) + 64,
                                                256 + j * 64:256 + (j + 1) * 64],
                                             ALU.mult, ALU.add),
                               inc=(sDV, 1) if j == 3 else None)
                    bl_dv = sDV.n
                    st["B4_free"] = bl_dv
                    em.wait("scalar", sDV, bl_dv)
                    for j, h in enumerate(heads):
                        em.add("scalar",
                               lambda e, a=stile(Sbf, h), bb2=stile(Snext, h): e.copy(a, bb2),
                               inc=(sAC, 1) if j == 3 else None)
                if STAGE < 12:
                    em.add("vector", MEMSET(s_oa[par][:, 0:64], 0.0), inc=(sDV, 1))
                    o_dv = bl_dv = sDV.n
                ch_blend_ac[ci] = sAC.n
                ch_o_dv[ci] = o_dv
                gidx += 1
            em.wait("sync", sDV, ch_o_dv[ci])
            ch_o_spill[ci] = em.dma("sync", o_d[row0:row0 + 128, :], s_oa[par][:], dOsp[par])
            if ci + 2 < NCHUNKS:
                emit_prep(ci + 2)
        for sm_ in (dT[0], dT[1], dPr[0], dPr[1], dOsp[0], dOsp[1]):
            em.wait("sync", sm_, sm_.n)
        em.wait("sync", sDV, sDV.n)
        em.wait("sync", sAC, sAC.n)
        em.wait("sync", sGP, sGP.n)
        em.play("p4")
        ph.close()
    stack2.close()

    if PHASES >= 5:
        ph = ExitStack()
        sbl = lambda name, shape, dt: ph.enter_context(nc.sbuf_tensor(name, shape, dt))
        psl = lambda name, shape, dt: ph.enter_context(nc.psum_tensor(name, shape, dt))
        SP5 = 1024
        NSP5 = TOK // SP5
        TPS5 = SP5 // 128
        PG = [[psl(f"p5_G{i}{j}", [128, 512], f) for j in range(2)] for i in range(2)]
        PSS = psl("p5_SS", [128, 512], f)
        PRB = psl("p5_RB", [128, 512], f)
        PW = [psl(f"p5_W{i}", [128, 512], f) for i in range(2)]
        s_x5 = sbl("s_x5", [128, 16, SP5], bf)
        s_oT5 = sbl("s_oT5", [128, 16, SP5], bf)
        s_o55 = sbl("s_o55", [128, 16, SP5], bf)
        s_sq5 = sbl("s_sq5", [128, SP5], bf)
        s_r2b = sbl("s_r2b", [2, 16, SP5], bf)
        s_r25 = sbl("s_r25", [2, 512], f)
        s_wg5 = [sbl(f"s_wg5_{i}", [128, 16, 128], bf) for i in range(2)]
        s_wo5 = [sbl(f"s_wo5_{i}", [128, 16, 512], bf) for i in range(2)]
        s_a5 = sbl("s_a5", [128, SP5], f)
        s_sg5 = sbl("s_sg5", [128, SP5], f)
        s_t25 = sbl("s_t25", [128, 512], f)
        s_ev5 = [sbl(f"s_ev5_{i}", [128, 512], f) for i in range(2)]
        dO5 = [em.sem("dO5a"), em.sem("dO5b")]
        dWg5 = [em.sem("dWg5a"), em.sem("dWg5b")]
        dWoQ = [em.sem("dWoQa"), em.sem("dWoQb")]
        dX5 = em.sem("dX5")
        dSd = [em.sem("dSda"), em.sem("dSdb")]
        em.wait("sync", dCgw, cast_gwo)

        sq_ac = [0] * 16
        ss_te = [[0, 0] for _ in range(16)]
        rstd_ac = [[0, 0] for _ in range(16)]
        gate_te = [[0, 0] for _ in range(16)]
        rb_te5 = [[0, 0] for _ in range(16)]
        a_dv = [[0, 0] for _ in range(16)]
        o5_dv = [[0, 0] for _ in range(16)]
        prev_rbte = [0] * 16
        prev_adv = [0] * 16
        prev_gate_all = 0
        wo_q_te = [0, 0, 0, 0]
        prev_wo_q_te = [0, 0, 0, 0]
        wg_ld5 = [0, 0]
        o_ld5 = [0, 0]
        woq_ld = [0, 0, 0, 0]
        ev_dv5 = [0, 0]
        sd_dma = [0, 0]
        for sp in range(NSP5):
            rows0 = sp * SP5
            em.wait("sync", sTE, prev_gate_all)
            x5_ld = em.dma("sync", s_x5[:], xnT_d[:, :, rows0:rows0 + SP5], dX5)
            for it in range(18):
                d2 = it
                e2 = it - 2
                par = d2 % 2
                ep = e2 % 2
                if d2 < 16:
                    if d2 >= 2:
                        em.wait("sync", sAC, sq_ac[d2 - 2])
                    if sp > 0:
                        em.wait("sync", sDV, prev_adv[d2])
                    o_ld5[par] = em.dma("sync", s_oT5[:, d2, :],
                                        o_d[rows0:rows0 + SP5, d2 * 128:(d2 + 1) * 128],
                                        dO5[par], transpose=True)
                if 0 <= e2 < 16:
                    if e2 >= 2:
                        em.wait("sync", sTE, gate_te[e2 - 2][1])
                    q5, c0 = e2 // 4, (e2 % 4) * 128
                    wg_ld5[ep] = em.dma(
                        "sync", s_wg5[ep][:],
                        wg_bf[q5].rearrange("kb p c -> p kb c")[:, :, c0:c0 + 128],
                        dWg5[ep])
                if d2 in (8, 12):
                    qq = (d2 - 8) // 4
                    em.wait("sync", sTE, prev_wo_q_te[qq + 2])
                    woq_ld[qq] = em.dma("sync", s_wo5[qq % 2][:],
                                        wo_bf[qq].rearrange("kb p c -> p kb c"),
                                        dWoQ[qq % 2])
                if 1 <= d2 <= 16:
                    dd = d2 - 1
                    em.wait("scalar", sTE, ss_te[dd][1])
                    em.wait("scalar", sTE, prev_rbte[dd])
                    em.add("scalar", ACT(s_r25[:], PSS[0:2, 0:512], AF.Ln,
                                         bias=c_eps6r[0:2, 0:1]))
                    em.add("scalar", lambda e, dd=dd: e.activation(
                        out=s_r2b[:, dd, 512:1024], in_=s_r25[:], func=AF.Exp,
                        scale=-0.5), inc=(sAC, 1))
                    rstd_ac[dd][1] = sAC.n
                if d2 < 16:
                    em.wait("scalar", dO5[par], o_ld5[par])
                    if d2 >= 1:
                        em.wait("scalar", sTE, ss_te[d2 - 1][1])
                    em.add("scalar", ACT(s_sq5[:], s_oT5[:, d2, :], AF.Square),
                           inc=(sAC, 1))
                    sq_ac[d2] = sAC.n
                if 0 <= e2 < 16:
                    em.wait("tensor", sAC, rstd_ac[e2][0])
                    em.wait("tensor", sDV, a_dv[e2 - 1][1] if e2 >= 1 else prev_adv[15])
                    em.add("tensor", MM(PRB[:], c_obd64T[:], s_r2b[:, e2, 0:512]),
                           inc=(sTE, 1))
                    rb_te5[e2][0] = sTE.n
                    em.wait("tensor", dX5, x5_ld)
                    em.wait("tensor", dWg5[ep], wg_ld5[ep])
                    em.wait("tensor", sDV, o5_dv[e2 - 2][0])
                    for kb in range(16):
                        em.add("tensor", MM(PG[ep][0][:], s_wg5[ep][:, kb, :],
                                            s_x5[:, kb, 0:512],
                                            start=(kb == 0), stop=(kb == 15)),
                               inc=(sTE, 1) if kb == 15 else None)
                    gate_te[e2][0] = sTE.n
                if d2 < 16:
                    em.wait("tensor", sAC, sq_ac[d2])
                    em.wait("tensor", sAC, rstd_ac[d2 - 1][1])
                    em.add("tensor", MM(PSS[0:2, 0:512], c_obd64[:], s_sq5[:, 0:512]),
                           inc=(sTE, 1))
                    ss_te[d2][0] = sTE.n
                if 0 <= e2 < 16:
                    em.wait("vector", sTE, rb_te5[e2][0])
                    em.add("vector", STT(s_a5[:, 0:512], PRB[:], c_onc[:, 0:1],
                                         s_oT5[:, e2, 0:512], ALU.mult, ALU.mult),
                           inc=(sDV, 1))
                    a_dv[e2][0] = sDV.n
                if 0 <= e2 < 16:
                    em.wait("scalar", sDV, a_dv[e2][0])
                    em.add("scalar", ACT(s_sg5[:, 0:512], s_a5[:, 0:512], AF.Exp,
                                         scale=-1.0))
                    em.add("scalar", ACT(s_sg5[:, 0:512], s_sg5[:, 0:512], AF.Ln,
                                         bias=c_one[:, 0:1]))
                    em.add("scalar", ACT(s_sg5[:, 0:512], s_sg5[:, 0:512], AF.Exp,
                                         scale=-1.0), inc=(sAC, 1))
                    sg0_ac = sAC.n
                if d2 < 16:
                    em.wait("scalar", sTE, ss_te[d2][0])
                    em.add("scalar", ACT(s_r25[:], PSS[0:2, 0:512], AF.Ln,
                                         bias=c_eps6r[0:2, 0:1]))
                    em.add("scalar", lambda e, d2=d2: e.activation(
                        out=s_r2b[:, d2, 0:512], in_=s_r25[:], func=AF.Exp,
                        scale=-0.5), inc=(sAC, 1))
                    rstd_ac[d2][0] = sAC.n
                if 0 <= e2 < 16:
                    em.wait("tensor", sAC, rstd_ac[e2][1])
                    em.wait("tensor", sDV, a_dv[e2][0])
                    em.add("tensor", MM(PRB[:], c_obd64T[:], s_r2b[:, e2, 512:1024]),
                           inc=(sTE, 1))
                    rb_te5[e2][1] = sTE.n
                    em.wait("tensor", sDV, o5_dv[e2 - 2][1])
                    for kb in range(16):
                        em.add("tensor", MM(PG[ep][1][:], s_wg5[ep][:, kb, :],
                                            s_x5[:, kb, 512:1024],
                                            start=(kb == 0), stop=(kb == 15)),
                               inc=(sTE, 1) if kb == 15 else None)
                    gate_te[e2][1] = sTE.n
                if d2 < 16:
                    em.wait("tensor", sAC, rstd_ac[d2][0])
                    em.add("tensor", MM(PSS[0:2, 0:512], c_obd64[:], s_sq5[:, 512:1024]),
                           inc=(sTE, 1))
                    ss_te[d2][1] = sTE.n
                if 0 <= e2 < 16:
                    em.wait("vector", sAC, sg0_ac)
                    em.add("vector", TT(s_t25[:], s_a5[:, 0:512], s_sg5[:, 0:512],
                                        ALU.mult))
                    em.wait("vector", sTE, gate_te[e2][0])
                    em.add("vector", TT(s_o55[:, e2, 0:512], PG[ep][0][:], s_t25[:],
                                        ALU.mult), inc=(sDV, 1))
                    o5_dv[e2][0] = sDV.n
                    em.wait("vector", sTE, rb_te5[e2][1])
                    em.add("vector", STT(s_a5[:, 512:1024], PRB[:], c_onc[:, 0:1],
                                         s_oT5[:, e2, 512:1024], ALU.mult, ALU.mult),
                           inc=(sDV, 1))
                    a_dv[e2][1] = sDV.n
                    em.wait("scalar", sDV, a_dv[e2][1])
                    em.add("scalar", ACT(s_sg5[:, 512:1024], s_a5[:, 512:1024], AF.Exp,
                                         scale=-1.0))
                    em.add("scalar", ACT(s_sg5[:, 512:1024], s_sg5[:, 512:1024], AF.Ln,
                                         bias=c_one[:, 0:1]))
                    em.add("scalar", ACT(s_sg5[:, 512:1024], s_sg5[:, 512:1024], AF.Exp,
                                         scale=-1.0), inc=(sAC, 1))
                    sg1_ac = sAC.n
                    em.wait("vector", sAC, sg1_ac)
                    em.add("vector", TT(s_t25[:], s_a5[:, 512:1024], s_sg5[:, 512:1024],
                                        ALU.mult))
                    em.wait("vector", sTE, gate_te[e2][1])
                    em.add("vector", TT(s_o55[:, e2, 512:1024], PG[ep][1][:], s_t25[:],
                                        ALU.mult), inc=(sDV, 1))
                    o5_dv[e2][1] = sDV.n
            prev_gate_all = sTE.n
            prev_rbte = [rb_te5[i][1] for i in range(16)]
            prev_adv = [a_dv[i][1] for i in range(16)]
            em.wait("tensor", sDV, o5_dv[15][1])
            for q in range(4):
                qp = q % 2
                em.wait("tensor", dWoQ[qp], woq_ld[q])
                for tt in range(TPS5):
                    bank = PW[tt % 2]
                    em.wait("tensor", sDV, ev_dv5[tt % 2])
                    for d2 in range(16):
                        em.add("tensor", MM(bank[:], s_o55[:, d2, tt * 128:(tt + 1) * 128],
                                            s_wo5[qp][:, d2, :],
                                            start=(d2 == 0), stop=(d2 == 15)),
                               inc=(sTE, 1) if d2 == 15 else None)
                    rte = sTE.n
                    em.wait("vector", sTE, rte)
                    em.wait("vector", dSd[tt % 2], sd_dma[tt % 2])
                    em.add("vector", CP(s_ev5[tt % 2][:], bank[:]), inc=(sDV, 1))
                    ev_dv5[tt % 2] = sDV.n
                    em.wait("sync", sDV, ev_dv5[tt % 2])
                    sd_dma[tt % 2] = em.dma(
                        "sync", s_d[rows0 + tt * 128:rows0 + (tt + 1) * 128,
                                    q * 512:(q + 1) * 512],
                        s_ev5[tt % 2][:], dSd[tt % 2])
                wo_q_te[q] = sTE.n
                if q + 2 < 4:
                    em.wait("sync", sTE, wo_q_te[q])
                    woq_ld[q + 2] = em.dma("sync", s_wo5[q % 2][:],
                                           wo_bf[q + 2].rearrange("kb p c -> p kb c"),
                                           dWoQ[q % 2])
            prev_wo_q_te = list(wo_q_te)
        for sm_ in (dO5[0], dO5[1], dWg5[0], dWg5[1], dWoQ[0], dWoQ[1], dX5,
                    dSd[0], dSd[1]):
            em.wait("sync", sm_, sm_.n)
        em.wait("sync", sDV, sDV.n)
        em.wait("sync", sAC, sAC.n)
        em.wait("sync", sTE, sTE.n)
        em.play("p5")
        ph.close()

    if PHASES >= 6:
        ph = ExitStack()
        sbl = lambda name, shape, dt: ph.enter_context(nc.sbuf_tensor(name, shape, dt))
        psl = lambda name, shape, dt: ph.enter_context(nc.psum_tensor(name, shape, dt))
        SP6 = 1024
        NSP6 = TOK // SP6
        TPS6 = SP6 // 128
        PB = [psl(f"p6_B{i}", [128, 512], f) for i in range(8)]
        s_x6 = sbl("s_x6", [128, 16, SP6], bf)
        s_w6 = [sbl(f"s_w6_{i}", [128, 16, 512], bf) for i in range(2)]
        s_acc = sbl("s_acc", [128, TPS6 * 2048], f)
        s_cbT = sbl("s_cbT", [8, TPS6 * 128], f)
        c_fb6 = sbl("c_fb6", [8, 2048], f)
        s_sg6 = sbl("s_sg6", [128, 2048], f)
        s_xr6 = [sbl(f"s_xr6_{i}", [128, 2048], f) for i in range(2)]
        s_sr6 = [sbl(f"s_sr6_{i}", [128, 2048], f) for i in range(2)]
        s_ob6 = [sbl(f"s_ob6_{i}", [128, 2048], f) for i in range(2)]
        dX6 = em.sem("dX6")
        dW6 = [em.sem("dW6a"), em.sem("dW6b")]
        dRes = [em.sem("dResa"), em.sem("dResb")]
        dOut = [em.sem("dOuta"), em.sem("dOutb")]
        em.wait("sync", dCff, cast_all)
        fb_ld = em.dma("sync", c_fb6[:], fb, dX6)

        def accv(tt, q):
            return s_acc[:, tt * 2048 + q * 512:tt * 2048 + (q + 1) * 512]

        x6war_te = 0
        prev_ev_all = 0
        w6_ld = [0, 0]
        round_te = [0] * 32
        res_ld = [0, 0]
        out_dma6 = [0, 0]
        res_war = [0, 0]
        gtt_dv = 0
        for sp in range(NSP6):
            rows0 = sp * SP6
            i0 = sp * TPS6
            em.wait("sync", sTE, x6war_te)
            x6_ld = em.dma("sync", s_x6[:], xnT_d[:, :, rows0:rows0 + SP6], dX6)
            for r0 in range(2):
                e0, q0 = r0 // 4, r0 % 4
                w6_ld[r0 % 2] = em.dma("sync", s_w6[r0 % 2][:],
                                       wf_bf[e0, q0].rearrange("kb p c -> p kb c"),
                                       dW6[r0 % 2])
            em.wait("tensor", sDV, prev_ev_all)
            for tt in range(TPS6):
                em.add("tensor", TR(PB[tt][0:8, 0:128],
                                    p_comb[:, (i0 + tt) * 8:(i0 + tt + 1) * 8], c_idf[:]),
                       inc=(sTE, 1) if tt == TPS6 - 1 else None)
            cbt_te = sTE.n
            em.wait("vector", sTE, cbt_te)
            for tt in range(TPS6):
                em.add("vector", CP(s_cbT[:, tt * 128:(tt + 1) * 128], PB[tt][0:8, 0:128]),
                       inc=(sDV, 1) if tt == TPS6 - 1 else None)
            cbt_dv = sDV.n
            em.wait("tensor", sDV, cbt_dv)
            em.wait("tensor", dX6, fb_ld)
            seed_dv = 0
            for q in range(4):
                if seed_dv:
                    em.wait("tensor", sDV, seed_dv)
                for tt in range(TPS6):
                    em.add("tensor", MM(PB[tt][:], s_cbT[:, tt * 128:(tt + 1) * 128],
                                        c_fb6[:, q * 512:(q + 1) * 512]),
                           inc=(sTE, 1) if tt == TPS6 - 1 else None)
                seed_te = sTE.n
                em.wait("vector", sTE, seed_te)
                for tt in range(TPS6):
                    em.add("vector", CP(accv(tt, q), PB[tt][:]),
                           inc=(sDV, 1) if tt == TPS6 - 1 else None)
                seed_dv = sDV.n
            ev_half = [seed_dv, seed_dv]
            for e in range(E):
                for q in range(4):
                    r = e * 4 + q
                    wp = r % 2
                    em.wait("tensor", dW6[wp], w6_ld[wp])
                    em.wait("tensor", dX6, x6_ld)
                    for half in range(2):
                        em.wait("tensor", sDV, ev_half[half])
                        for kb in range(16):
                            for tt in range(4 * half, 4 * half + 4):
                                em.add("tensor",
                                       MM(PB[tt][:], s_x6[:, kb, tt * 128:(tt + 1) * 128],
                                          s_w6[wp][:, kb, :],
                                          start=(kb == 0), stop=(kb == 15)),
                                       inc=(sTE, 1) if (kb == 15 and tt == 4 * half + 3)
                                       else None)
                        hte = sTE.n
                        em.wait("vector", sTE, hte)
                        for tt in range(4 * half, 4 * half + 4):
                            em.add("vector",
                                   STT(accv(tt, q), PB[tt][:],
                                       p_comb[:, (i0 + tt) * 8 + e:(i0 + tt) * 8 + e + 1],
                                       accv(tt, q), ALU.mult, ALU.add),
                                   inc=(sDV, 1) if tt == 4 * half + 3 else None)
                        ev_half[half] = sDV.n
                    round_te[r] = sTE.n
                    if r + 2 < 32:
                        e2_, q2_ = (r + 2) // 4, (r + 2) % 4
                        em.wait("sync", sTE, round_te[r])
                        w6_ld[wp] = em.dma("sync", s_w6[wp][:],
                                           wf_bf[e2_, q2_].rearrange("kb p c -> p kb c"),
                                           dW6[wp])
            prev_ev_all = ev_half[1]
            x6war_te = round_te[31]
            prev_ob_dv = 0
            for tt in range(TPS6):
                par = tt % 2
                acct = s_acc[:, tt * 2048:(tt + 1) * 2048]
                em.wait("scalar", sDV, ev_half[1] if tt == 0 else prev_ob_dv)
                em.add("scalar", ACT(s_sg6[:], acct, AF.Exp, scale=-1.0))
                em.add("scalar", ACT(s_sg6[:], s_sg6[:], AF.Ln, bias=c_one[:, 0:1]))
                em.add("scalar", ACT(s_sg6[:], s_sg6[:], AF.Exp, scale=-1.0),
                       inc=(sAC, 1))
                sg_ac6 = sAC.n
                em.wait("sync", sDV, res_war[par])
                em.dma("sync", s_xr6[par][:], x_in[rows0 + tt * 128:rows0 + (tt + 1) * 128, :],
                       dRes[par])
                res_ld[par] = em.dma(
                    "sync", s_sr6[par][:],
                    s_d[rows0 + tt * 128:rows0 + (tt + 1) * 128, :], dRes[par])
                em.wait("vector", sAC, sg_ac6)
                em.add("vector", TT(s_sg6[:], acct, s_sg6[:], ALU.mult), inc=(sDV, 1))
                gtt_dv = sDV.n
                em.wait("vector", dRes[par], res_ld[par])
                em.wait("vector", dOut[par], out_dma6[par])
                em.add("vector", TT(s_ob6[par][:], s_sg6[:], s_sr6[par][:], ALU.mult))
                em.add("vector", TT(s_ob6[par][:], s_ob6[par][:], s_xr6[par][:], ALU.add),
                       inc=(sDV, 1))
                ob_dv = sDV.n
                res_war[par] = ob_dv
                prev_ob_dv = ob_dv
                em.wait("sync", sDV, ob_dv)
                out_dma6[par] = em.dma(
                    "sync", out_d[rows0 + tt * 128:rows0 + (tt + 1) * 128, :],
                    s_ob6[par][:], dOut[par])
        for sm_ in (dX6, dW6[0], dW6[1], dRes[0], dRes[1], dOut[0], dOut[1]):
            em.wait("sync", sm_, sm_.n)
        em.wait("sync", sDV, sDV.n)
        em.wait("sync", sAC, sAC.n)
        em.wait("sync", sTE, sTE.n)
        em.wait("sync", sGP, sGP.n)
        em.play("p6")
        ph.close()

    if PHASES < 6:
        ph = ExitStack()
        s_z = ph.enter_context(nc.sbuf_tensor("s_z", [128, D], F32))
        em.add("vector", MEMSET(s_z[:], 0.0), inc=(sDV, 1))
        em.wait("sync", sDV, sDV.n)
        for i in range(NT):
            em.dma("sync", out_d[i * 128:(i + 1) * 128, :], s_z[:], dS)
        em.wait("sync", dS, dS.n)
        em.play("stub")
        ph.close()

    stack.close()
    return nc


def kernel(**inputs):
    x = np.asarray(inputs["L0_attn_x"], np.float32)
    shared = host_prepare(inputs)
    nc = build_program()
    ncores = DEVCORES or NCORES
    in_maps = []
    for c in range(ncores):
        m = dict(shared)
        m["x"] = np.ascontiguousarray(x[c * BPC:(c + 1) * BPC].reshape(TOK, D))
        in_maps.append(m)
    res = run_bass_kernel_spmd(nc, in_maps, core_ids=list(range(ncores)))
    out = np.concatenate([np.asarray(r["out"]).reshape(BPC, T, D) for r in res.results], axis=0)
    if DEVCORES:
        out = np.concatenate([out] + [out[:BPC]] * (NCORES - DEVCORES), axis=0)
    if DEBUG:
        kernel.last_results = res.results
    return out


if __name__ == "__main__":
    nc = build_program()
    print("BUILD OK")


# version 04
# speedup vs baseline: 1.3805x; 1.0609x over previous
import os
import numpy as np
import ml_dtypes
import concourse.bass as bass
import concourse.mybir as mybir
from concourse.bass_utils import run_bass_kernel_spmd
from concourse.hw_specs import get_activation_tables
from contextlib import ExitStack

F32 = mybir.dt.float32
BF16 = mybir.dt.bfloat16
AF = mybir.ActivationFunctionType
ALU = mybir.AluOpType
AX = mybir.AxisListType.X

NCORES = 8
BPC = 2
T = 2048
D = 2048
H = 32
DH = 64
TOK = BPC * T
NT = TOK // 128
CH = 128
NCH = T // CH
E = 8

DEBUG = bool(os.environ.get("MK_DEBUG"))
DEVCORES = int(os.environ.get("MK_CORES", "0"))
PHASES = int(os.environ.get("MK_PHASES", "6"))


def _strip_order():
    sio = np.zeros(D, np.int64)
    for d in range(16):
        h0, h1 = 2 * d, 2 * d + 1
        rows = np.concatenate([
            h0 * 64 + np.arange(0, 64, 2), h1 * 64 + np.arange(0, 64, 2),
            h0 * 64 + 1 + np.arange(0, 64, 2), h1 * 64 + 1 + np.arange(0, 64, 2)])
        sio[d * 128:(d + 1) * 128] = rows
    return sio


def _yarn_tables():
    ROPE_BASE, SCALING, BETA_FAST, BETA_SLOW = 10000.0, 32.0, 1.0, 32.0
    half = (np.arange(0, DH, 2, dtype=np.float32) / DH).astype(np.float32)
    inv_freq = (1.0 / ROPE_BASE ** half).astype(np.float32)
    wavelen = (2.0 * np.pi / inv_freq).astype(np.float32)
    ramp = np.clip((wavelen - BETA_FAST) / (BETA_SLOW - BETA_FAST), 0.0, 1.0).astype(np.float32)
    scale = (1.0 + (SCALING - 1.0) * ramp).astype(np.float32)
    t = np.arange(T, dtype=np.float32)
    freqs = (t[:, None] / scale[None, :] * inv_freq[None, :]).astype(np.float32)
    idx = (2 * np.arange(DH // 2)) % (DH // 2)
    ctab = np.cos(freqs[:, idx]).T.astype(np.float32)
    stab = np.sin(freqs[:, idx]).T.astype(np.float32)
    return ctab, stab


def host_prepare(inp):
    f32 = np.float32
    bfl = ml_dtypes.bfloat16
    sio = _strip_order()
    Wq = np.asarray(inp["L0_attn_Wq"], f32)
    Wk = np.asarray(inp["L0_attn_Wk"], f32)
    Wv = np.asarray(inp["L0_attn_Wv"], f32)
    Wg = np.asarray(inp["L0_attn_Wg"], f32)
    Wo = np.asarray(inp["L0_attn_Wo"], f32)
    d = {}

    def blkT(W, order=None):
        WT = W.T if order is None else np.ascontiguousarray(W.T[:, order])
        return np.ascontiguousarray(WT.reshape(D, 16, 128).transpose(1, 0, 2))

    d["wqT"] = blkT(Wq, sio)
    d["wkT"] = blkT(Wk, sio)
    d["wvT"] = blkT(Wv)

    def blk512(WT):
        return np.ascontiguousarray(WT.reshape(16, 128, 4, 512).transpose(2, 0, 1, 3))

    d["wgT"] = blk512(np.ascontiguousarray(Wg.T))
    d["woT"] = blk512(np.ascontiguousarray(Wo.T))
    Wf = np.asarray(inp["L0_ffn_W"], f32)
    d["wffn"] = np.ascontiguousarray(Wf.reshape(8, 16, 128, 4, 512).transpose(0, 3, 1, 2, 4))

    def small(WT):
        return np.ascontiguousarray(np.asarray(WT, f32).reshape(16, 128, -1))

    d["wbT"] = small(np.asarray(inp["L0_attn_Wb"], f32).T)
    d["wgkT"] = small(np.asarray(inp["L0_attn_Wgk"], f32).T)
    d["wgf"] = small(np.asarray(inp["L0_ffn_Wg"], f32))
    d["fb"] = np.ascontiguousarray(np.asarray(inp["L0_ffn_b"], f32))
    for nm, order in (("q", sio), ("k", sio), ("v", None)):
        w = np.asarray(inp[f"L0_attn_w{nm}_conv"], f32)
        b = np.asarray(inp[f"L0_attn_b{nm}_conv"], f32)
        if order is not None:
            w, b = w[order], b[order]
        d[f"cw_{nm}"] = np.ascontiguousarray(w.reshape(16, 128, 4).transpose(1, 0, 2))
        d[f"cb_{nm}"] = np.ascontiguousarray(b.reshape(16, 128).T)

    def rep(v):
        v = np.asarray(v, f32).ravel()
        return np.ascontiguousarray(np.broadcast_to(v[None, :], (128, v.size)))

    d["bb_rep"] = rep(inp["L0_attn_bb"])
    d["bgk_rep"] = rep(inp["L0_attn_bgk"])
    d["dt_rep"] = rep(inp["L0_attn_dt_bias"])
    d["alog_rep"] = rep(inp["L0_attn_A_log"])
    d["dD_rep"] = rep(inp["L0_attn_D"])
    d["onorm_rep"] = np.ascontiguousarray(np.broadcast_to(
        np.tile(np.asarray(inp["L0_attn_onorm_w"], f32), H)[None, :], (128, D)))
    ctab, stab = _yarn_tables()
    d["cs_c"] = np.ascontiguousarray(np.tile(ctab, (4, 1)))
    d["cs_s"] = np.ascontiguousarray(np.tile(stab, (4, 1)))
    I = np.eye(128, dtype=f32)
    d["id_bf"] = I.astype(bfl)
    d["id_f32"] = I
    d["i4_bf"] = np.ascontiguousarray(np.tile(I, (1, 4)).astype(bfl))
    d["negi4_bf"] = np.ascontiguousarray(np.tile(-I, (1, 4)).astype(bfl))
    d["umask_bf"] = np.triu(np.full((128, 128), 3.3e38, f32)).astype(bfl)
    d["ones_f32"] = np.ones((1, 128), f32)
    obd = np.zeros((128, 2), f32)
    for r in range(128):
        obd[r, (r // 32) % 2] = 1.0
    d["obd_bf"] = obd.astype(bfl)
    d["obdT_bf"] = np.ascontiguousarray(obd.T.astype(bfl))
    obd64 = np.zeros((128, 2), f32)
    for r in range(128):
        obd64[r, r // 64] = 1.0
    d["obd64_bf"] = obd64.astype(bfl)
    d["obd64T_bf"] = np.ascontiguousarray(obd64.T.astype(bfl))
    d["onc"] = np.ascontiguousarray(
        (np.tile(np.asarray(inp["L0_attn_onorm_w"], f32), 2) * 8.0)[:, None])
    return d


class Sem:
    __slots__ = ("h", "n")

    def __init__(self, h):
        self.h = h
        self.n = 0


class Em:
    ENGS = ("sync", "tensor", "vector", "scalar", "gpsimd")

    def __init__(self, nc, stack):
        self.nc = nc
        self.stack = stack
        self.cmds = {e: [] for e in self.ENGS}
        self.sems = {}

    def sem(self, name):
        if name not in self.sems:
            self.sems[name] = Sem(self.stack.enter_context(self.nc.semaphore(name)))
        return self.sems[name]

    def add(self, eng, fn, inc=None):
        if inc is not None:
            sem, k = inc
            sem.n += k
            self.cmds[eng].append(lambda e, fn=fn, h=sem.h, k=k: fn(e).then_inc(h, k))
        else:
            self.cmds[eng].append(fn)

    def wait(self, eng, sem, val=None):
        v = sem.n if val is None else val
        if v > 0:
            self.cmds[eng].append(lambda e, h=sem.h, v=v: e.wait_ge(h, v))

    def relay(self, eng, waitsem, val, incsem):
        self.wait(eng, waitsem, val)
        incsem.n += 1
        self.cmds[eng].append(lambda e, h=incsem.h: e.sem_inc(h, 1))
        return incsem.n

    def dma(self, eng, out, in_, sem, transpose=False):
        if transpose:
            self.add(eng, lambda e, o=out, i=in_: e.dma_start_transpose(o, i), inc=(sem, 16))
        else:
            self.add(eng, lambda e, o=out, i=in_: e.dma_start(out=o, in_=i), inc=(sem, 16))
        return sem.n

    def play(self, name):
        with self.nc.Block(name) as block:
            for eng in self.ENGS:
                dec = getattr(block, eng)
                cl = self.cmds[eng]

                def body(e, cl=cl):
                    for fn in cl:
                        fn(e)
                dec(body)
        self.cmds = {e: [] for e in self.ENGS}


def MM(out, lhsT, rhs, start=True, stop=True, tile_position=None):
    return lambda e: e.matmul(out, lhsT=lhsT, rhs=rhs, start=start, stop=stop,
                              skip_group_check=True, tile_position=tile_position)


def TR(out, in_, ident):
    return lambda e: e.transpose(out, in_, ident)


def CP(out, in_):
    return lambda e: e.tensor_copy(out, in_)


def TT(out, a, b, op):
    return lambda e: e.tensor_tensor(out=out, in0=a, in1=b, op=op)


def TS(out, a, s1, s2, op0, op1=ALU.bypass):
    return lambda e: e.tensor_scalar(out=out, in0=a, scalar1=s1, scalar2=s2, op0=op0, op1=op1)


def STT(out, a, s, b, op0, op1):
    return lambda e: e.scalar_tensor_tensor(out=out, in0=a, scalar=s, in1=b, op0=op0, op1=op1)


def ACT(out, in_, func, bias=None, scale=None):
    kw = {}
    if bias is not None:
        kw["bias"] = bias
    if scale is not None:
        kw["scale"] = scale
    return lambda e: e.activation(out=out, in_=in_, func=func, **kw)


def MEMSET(ap, v):
    return lambda e: e.memset(ap, v)


def RED(out, in_, op, negate=False):
    return lambda e: e.tensor_reduce(out=out, in_=in_, axis=AX, op=op, negate=negate)


def SCAN(out, d0, d1, init, op0, op1):
    return lambda e: e.tensor_tensor_scan(out=out, data0=d0, data1=d1, initial=init,
                                          op0=op0, op1=op1)


def build_program():
    nc = bass.Bass("TRN2", target_bir_lowering=False)
    stack = ExitStack()
    em = Em(nc, stack)
    f, bf = F32, BF16

    din = lambda name, shape, dt: nc.dram_tensor(name, shape, dt, kind="ExternalInput").ap()
    dout_ = lambda name, shape, dt: nc.dram_tensor(name, shape, dt, kind="ExternalOutput").ap()
    dint = lambda name, shape, dt: nc.dram_tensor(
        name, shape, dt, kind=("ExternalOutput" if DEBUG else "Internal")).ap()

    x_in = din("x", [TOK, D], f)
    wqT = din("wqT", [16, D, 128], f)
    wkT = din("wkT", [16, D, 128], f)
    wvT = din("wvT", [16, D, 128], f)
    wgT = din("wgT", [4, 16, 128, 512], f)
    woT = din("woT", [4, 16, 128, 512], f)
    wffn = din("wffn", [8, 4, 16, 128, 512], f)
    wbT = din("wbT", [16, 128, 32], f)
    wgkT = din("wgkT", [16, 128, 32], f)
    wgf = din("wgf", [16, 128, 8], f)
    fb = din("fb", [8, 2048], f)
    cw = {p: din(f"cw_{p}", [128, 16, 4], f) for p in "qkv"}
    cb = {p: din(f"cb_{p}", [128, 16], f) for p in "qkv"}
    bb_rep = din("bb_rep", [128, 32], f)
    bgk_rep = din("bgk_rep", [128, 32], f)
    dt_rep = din("dt_rep", [128, 32], f)
    alog_rep = din("alog_rep", [128, 32], f)
    dD_rep = din("dD_rep", [128, 32], f)
    onorm_rep = din("onorm_rep", [128, D], f)
    cs_c = din("cs_c", [128, T], f)
    cs_s = din("cs_s", [128, T], f)
    id_bf = din("id_bf", [128, 128], bf)
    id_f32 = din("id_f32", [128, 128], f)
    i4_bf = din("i4_bf", [128, 512], bf)
    negi4_bf = din("negi4_bf", [128, 512], bf)
    umask_bf = din("umask_bf", [128, 128], bf)
    ones_f32 = din("ones_f32", [1, 128], f)
    obd_bf = din("obd_bf", [128, 2], bf)
    obdT_bf = din("obdT_bf", [2, 128], bf)
    obd64_bf = din("obd64_bf", [128, 2], bf)
    obd64T_bf = din("obd64T_bf", [2, 128], bf)
    onc = din("onc", [128, 1], f)

    out_d = dout_("out", [TOK, D], f)
    xnT_d = dint("xnT_d", [128, 16, TOK], bf)
    qT_d = dint("qT_d", [D, TOK], bf)
    kT_d = dint("kT_d", [D, TOK], bf)
    v_d = dint("v_d", [TOK, D], bf)
    o_d = dint("o_d", [TOK, D], bf)
    s_d = dint("s_d", [TOK, D], f)
    dbg_d = dint("dbg_d", [128, NT * 72], f)
    wq_bf = dint("wq_bf", [16, D, 128], bf)
    wk_bf = dint("wk_bf", [16, D, 128], bf)
    wv_bf = dint("wv_bf", [16, D, 128], bf)
    wg_bf = dint("wg_bf", [4, 16, 128, 512], bf)
    wo_bf = dint("wo_bf", [4, 16, 128, 512], bf)
    wf_bf = dint("wf_bf", [8, 4, 16, 128, 512], bf)
    bounce_cg = dint("bounce_cg", [2, 32, 128], f)
    bounce_bl = dint("bounce_bl", [2, 1, 32], f)

    dS = em.sem("dS")
    dG = em.sem("dG")
    dConst = em.sem("dConst")
    dX = [em.sem("dX0"), em.sem("dX1")]
    dSpX = [em.sem("dSpX0"), em.sem("dSpX1")]
    dCqkv = em.sem("dCqkv")
    dCgw = em.sem("dCgw")
    dCff = em.sem("dCff")
    dXns = em.sem("dXns")
    dW = [em.sem("dW0"), em.sem("dW1")]
    dFin = em.sem("dFin")
    dVsp = em.sem("dVsp")
    dT = [em.sem("dT0"), em.sem("dT1")]
    dPr = [em.sem("dPr0"), em.sem("dPr1")]
    dOsp = [em.sem("dOsp0"), em.sem("dOsp1")]
    sTE = em.sem("sTE")
    rlyA = em.sem("rlyA")
    rlyG = em.sem("rlyG")
    sDV = em.sem("sDV")
    sAC = em.sem("sAC")
    sGP = em.sem("sGP")

    sb = lambda name, shape, dt: stack.enter_context(nc.sbuf_tensor(name, shape, dt))
    stack2 = ExitStack()
    sb2 = lambda name, shape, dt: stack2.enter_context(nc.sbuf_tensor(name, shape, dt))
    c_dD = sb("c_dD", [128, 32], f)
    c_idbf = sb("c_idbf", [128, 128], bf)
    c_idf = sb("c_idf", [128, 128], f)
    c_i4 = sb("c_i4", [128, 512], bf)
    c_negi4 = sb("c_negi4", [128, 512], bf)
    c_umask = sb("c_umask", [128, 128], bf)
    c_ones = sb("c_ones", [1, 128], f)
    c_obd = sb("c_obd", [128, 2], bf)
    c_obdT = sb("c_obdT", [2, 128], bf)
    c_eps5 = sb("c_eps5", [128, 1], f)
    c_eps6 = sb("c_eps6", [128, 1], f)
    c_eps6r = sb("c_eps6r", [128, 1], f)
    c_one = sb("c_one", [128, 1], f)
    c_cw = {p: sb(f"c_cw_{p}", [128, 16, 4], f) for p in "qkv"}
    c_cb = {p: sb(f"c_cb_{p}", [128, 16], f) for p in "qkv"}
    p_comb = sb("p_comb", [128, NT * 8], f)
    c_obd64 = sb("c_obd64", [128, 2], bf)
    c_obd64T = sb("c_obd64T", [2, 128], bf)
    c_onc = sb("c_onc", [128, 1], f)
    p_beta = sb2("p_beta", [128, NT * 32], f)
    p_negb = sb2("p_negb", [128, NT * 32], f)
    p_g = sb2("p_g", [128, NT * 32], f)
    p_S = [[sb2(f"p_S_{b}_{pp}", [128, 16 * 64], f) for pp in range(2)] for b in range(BPC)]
    p_Sbf = [sb2(f"p_Sbf_{b}", [128, 16 * 64], bf) for b in range(BPC)]
    c_csc = sb2("c_csc", [128, T], f)
    c_css = sb2("c_css", [128, T], f)

    names = list(get_activation_tables(nc.m.arch).keys())
    set_id = names.index("natural_log_exp_and_others")

    ph = ExitStack()
    sbl = lambda name, shape, dt: ph.enter_context(nc.sbuf_tensor(name, shape, dt))
    psl = lambda name, shape, dt: ph.enter_context(nc.psum_tensor(name, shape, dt))

    c_wbT = sbl("c_wbT", [128, 16, 32], f)
    c_wgkT = sbl("c_wgkT", [128, 16, 32], f)
    c_wgf = sbl("c_wgf", [128, 16, 8], f)
    c_bb = sbl("c_bb", [128, 32], f)
    c_bgkdt = sbl("c_bgkdt", [128, 32], f)
    c_nexpA = sbl("c_nexpA", [128, 32], f)
    s_bgk = sbl("s_bgk", [128, 32], f)
    s_dt = sbl("s_dt", [128, 32], f)
    s_alog = sbl("s_alog", [128, 32], f)
    s_x = [sbl(f"s_x{i}", [128, D], f) for i in range(2)]

    x_ld = [0] * NT
    x_ld[0] = em.dma("sync", s_x[0][:], x_in[0:128, :], dX[0])
    x_ld[1] = em.dma("sync", s_x[1][:], x_in[128:256, :], dX[1])
    for dst, src in [(c_idbf[:], id_bf), (c_idf[:], id_f32),
                     (c_wbT[:], wbT.rearrange("kb p n -> p kb n")),
                     (c_wgkT[:], wgkT.rearrange("kb p n -> p kb n")),
                     (c_wgf[:], wgf.rearrange("kb p n -> p kb n")),
                     (c_bb[:], bb_rep), (c_dD[:], dD_rep),
                     (c_i4[:], i4_bf),
                     (c_negi4[:], negi4_bf), (c_umask[:], umask_bf), (c_ones[:], ones_f32),
                     (c_obd[:], obd_bf), (c_obdT[:], obdT_bf),
                     (c_obd64[:], obd64_bf), (c_obd64T[:], obd64T_bf), (c_onc[:], onc),
                     (s_bgk[:], bgk_rep), (s_dt[:], dt_rep), (s_alog[:], alog_rep),
                     (c_csc[:], cs_c), (c_css[:], cs_s)]:
        em.dma("sync", dst, src, dConst)
    for p in "qkv":
        em.dma("sync", c_cw[p][:], cw[p], dConst)
        em.dma("sync", c_cb[p][:], cb[p], dConst)
    n_const = dConst.n

    for dst, src in [(wq_bf, wqT), (wk_bf, wkT), (wv_bf, wvT)]:
        em.dma("gpsimd", dst, src, dCqkv)
    cast_qkv = dCqkv.n
    cast_gwo = cast_all = 0

    def _preload(e, sid=set_id):
        return e.add_instruction(mybir.InstLoadActFuncSet(
            name=nc.get_next_instruction_name(), act_func_set_id=sid, ins=[], outs=[]))
    em.add("scalar", _preload)

    em.add("vector", MEMSET(c_eps5[:], 1e-5))
    em.add("vector", MEMSET(c_eps6[:], 1e-6))
    em.add("vector", MEMSET(c_eps6r[:], 64e-6))
    em.add("vector", MEMSET(c_one[:], 1.0), inc=(sDV, 1))
    em.wait("vector", dConst, n_const)
    em.add("vector", TT(c_bgkdt[:], s_bgk[:], s_dt[:], ALU.add), inc=(sDV, 1))
    bgkdt_done = sDV.n
    em.wait("scalar", dConst, n_const)
    em.add("scalar", ACT(c_nexpA[:], s_alog[:], AF.Exp))


    s_xnbf = [sbl(f"s_xnbf{i}", [128, D], bf) for i in range(2)]
    s_xnf = [sbl(f"s_xnf{i}", [128, D], f) for i in range(2)]
    s_xTf = [sbl(f"s_xTf{i}", [128, 16 * 128], f) for i in range(2)]
    s_hi = [sbl(f"s_hi{i}", [128, 16, 1024], bf) for i in range(2)]
    s_bnst = [sbl(f"s_bnst{i}", [128, 24], f) for i in range(2)]
    s_mv = [sbl(f"s_mv{i}", [128, 2], f) for i in range(2)]
    s_rstd = [sbl(f"s_rstd{i}", [128, 1], f) for i in range(2)]
    s_mb = [sbl(f"s_mb{i}", [128, 1], f) for i in range(2)]
    s_sm = sbl("s_sm", [128, 8], f)
    s_e8 = sbl("s_e8", [128, 8], f)
    s_msk = sbl("s_msk", [128, 8], f)
    s_t32 = sbl("s_t32", [128, 32], f)
    s_t32b = sbl("s_t32b", [128, 32], f)
    s_t32c = sbl("s_t32c", [128, 32], f)
    ps_tbf = [psl(f"ps_tbf{i}", [128, 512], bf) for i in range(2)]
    ps_tf = [psl(f"ps_tf{i}", [128, 512], f) for i in range(2)]
    ps_small = psl("ps_small", [128, 512], f)

    xn_done = [0] * NT
    tile_ev = [0] * NT
    small_done = [0] * NT
    epi_done = [0] * NT
    spill = {}

    for i in range(NT):
        par = i % 2
        seg = i // 8
        if i >= 2:
            em.wait("sync", sAC, xn_done[i - 2])
            x_ld[i] = em.dma("sync", s_x[par][:], x_in[i * 128:(i + 1) * 128, :], dX[par])
        em.wait("vector", dX[par], x_ld[i])
        for j in range(4):
            em.add("vector", lambda e, j=j, par=par: e.bn_stats(
                s_bnst[par][:, j * 6:(j + 1) * 6], s_x[par][:, j * 512:(j + 1) * 512]),
                   inc=(sDV, 1) if j == 3 else None)
        stats_dv = sDV.n
        rv = em.relay("scalar", sDV, stats_dv, rlyA)
        em.wait("vector", rlyA, rv)
        em.add("vector", lambda e, par=par: e.bn_aggr(s_mv[par][:], s_bnst[par][:]),
               inc=(sDV, 1))
        bn_done = sDV.n
        em.wait("scalar", sDV, bn_done)
        em.add("scalar", ACT(s_rstd[par][:], s_mv[par][:, 1:2], AF.Ln, bias=c_eps5[:, 0:1]))
        em.add("scalar", lambda e, par=par: e.activation(
            out=s_rstd[par][:], in_=s_rstd[par][:], func=AF.Exp, scale=-0.5), inc=(sAC, 1))
        rstd_done = sAC.n
        em.wait("vector", sAC, rstd_done)
        em.add("vector", lambda e, par=par: e.scalar_tensor_tensor(
            out=s_mb[par][:], in0=s_mv[par][:, 0:1], scalar=-1.0, in1=s_rstd[par][:],
            op0=ALU.mult, op1=ALU.mult), inc=(sDV, 1))
        mb_done = sDV.n
        em.wait("scalar", sDV, mb_done)
        if i >= 2:
            em.wait("scalar", sTE, small_done[i - 2])
        em.add("scalar", lambda e, par=par: e.activation(
            out=s_xnbf[par][:], in_=s_x[par][:], func=AF.Identity,
            bias=s_mb[par][:, 0:1], scale=s_rstd[par][:, 0:1]))
        em.add("scalar", lambda e, par=par: e.activation(
            out=s_xnf[par][:], in_=s_x[par][:], func=AF.Identity,
            bias=s_mb[par][:, 0:1], scale=s_rstd[par][:, 0:1]), inc=(sAC, 1))
        xn_done[i] = sAC.n
        em.wait("tensor", sAC, xn_done[i])
        if i >= 1:
            em.wait("tensor", sDV, tile_ev[i - 1])
        segcol = (i % 8) * 128
        if i % 8 == 0 and seg >= 2:
            em.wait("vector", dSpX[seg % 2], spill[seg - 2])
        if i >= 2:
            em.wait("vector", sTE, small_done[i - 2])
        for half in range(2):
            for j8 in range(8):
                j = half * 8 + j8
                bank = ps_tbf[j8 // 4]
                em.add("tensor", TR(bank[:, (j8 % 4) * 128:(j8 % 4 + 1) * 128],
                                    s_xnbf[par][:, j * 128:(j + 1) * 128], c_idbf[:]),
                       inc=(sTE, 1) if j8 == 7 else None)
            tb = sTE.n
            for j8 in range(8):
                j = half * 8 + j8
                bank = ps_tf[j8 // 4]
                em.add("tensor", TR(bank[:, (j8 % 4) * 128:(j8 % 4 + 1) * 128],
                                    s_xnf[par][:, j * 128:(j + 1) * 128], c_idf[:]),
                       inc=(sTE, 1) if j8 == 7 else None)
            tf_ = sTE.n
            em.wait("vector", sTE, tb)
            for q in range(2):
                j0 = half * 8 + q * 4
                em.add("vector", lambda e, q=q, j0=j0, seg=seg, segcol=segcol: e.tensor_copy(
                    s_hi[seg % 2][:, j0:j0 + 4, segcol:segcol + 128], ps_tbf[q][:]))
            em.wait("vector", sTE, tf_)
            for q in range(2):
                j0 = half * 8 + q * 4
                em.add("vector", lambda e, q=q, j0=j0, par=par: e.tensor_copy(
                    s_xTf[par][:, j0 * 128:(j0 + 4) * 128], ps_tf[q][:]),
                       inc=(sDV, 1) if q == 1 else None)
            if half == 0:
                em.wait("tensor", sDV, sDV.n)
        tile_ev[i] = sDV.n
        if i % 8 == 7:
            em.wait("sync", sDV, tile_ev[i])
            spill[seg] = em.dma("sync", xnT_d[:, :, seg * 1024:(seg + 1) * 1024],
                                s_hi[seg % 2][:], dSpX[seg % 2])
        em.wait("tensor", sDV, tile_ev[i])
        if i >= 1:
            em.wait("tensor", sDV, epi_done[i - 1])
        for j in range(16):
            em.add("tensor", MM(ps_small[:, 0:32], s_xTf[par][:, j * 128:(j + 1) * 128],
                                c_wbT[:, j, :], start=(j == 0), stop=(j == 15)))
        for j in range(16):
            em.add("tensor", MM(ps_small[:, 32:64], s_xTf[par][:, j * 128:(j + 1) * 128],
                                c_wgkT[:, j, :], start=False, stop=(j == 15)))
        for j in range(16):
            em.add("tensor", MM(ps_small[:, 64:72], s_xTf[par][:, j * 128:(j + 1) * 128],
                                c_wgf[:, j, :], start=False, stop=(j == 15)),
                   inc=(sTE, 1) if j == 15 else None)
        small_done[i] = sTE.n
        em.wait("vector", sTE, small_done[i])
        em.wait("vector", sDV, bgkdt_done)
        em.add("vector", TT(s_t32[:], ps_small[:, 0:32], c_bb[:], ALU.add))
        em.add("vector", TT(s_t32b[:], ps_small[:, 32:64], c_bgkdt[:], ALU.add),
               inc=(sDV, 1))
        raw_done = sDV.n
        em.wait("scalar", sDV, raw_done)
        em.add("scalar", ACT(s_t32c[:], s_t32[:], AF.Exp, scale=-1.0))
        em.add("scalar", ACT(s_t32c[:], s_t32c[:], AF.Ln, bias=c_one[:, 0:1]))
        em.add("scalar", ACT(p_beta[:, i * 32:(i + 1) * 32], s_t32c[:], AF.Exp, scale=-1.0),
               inc=(sAC, 1))
        beta_done = sAC.n
        em.add("scalar", ACT(s_t32[:], s_t32b[:], AF.Abs))
        em.add("scalar", ACT(s_t32[:], s_t32[:], AF.Exp, scale=-1.0))
        em.add("scalar", ACT(s_t32[:], s_t32[:], AF.Ln, bias=c_one[:, 0:1]), inc=(sAC, 1))
        spl_done = sAC.n
        em.wait("vector", sAC, spl_done)
        em.add("vector", TS(s_t32c[:], s_t32b[:], 0.0, 0.0, ALU.max))
        em.add("vector", RED(s_sm[:, 0:1], ps_small[:, 64:72], ALU.max, negate=True))
        em.add("vector", TS(p_negb[:, i * 32:(i + 1) * 32], p_beta[:, i * 32:(i + 1) * 32],
                            -1.0, 0.0, ALU.mult), inc=(sDV, 1))
        red_dv = sDV.n
        rv = em.relay("scalar", sDV, red_dv, rlyA)
        em.wait("vector", rlyA, rv)
        em.add("vector", TT(p_g[:, i * 32:(i + 1) * 32], s_t32[:], s_t32c[:], ALU.add))
        em.add("vector", TS(s_msk[:], ps_small[:, 64:72], s_sm[:, 0:1], 0.0,
                            ALU.add, ALU.is_ge), inc=(sDV, 1))
        nm_done = sDV.n
        em.wait("scalar", sDV, nm_done)
        em.add("scalar", ACT(s_e8[:], ps_small[:, 64:72], AF.Exp, bias=s_sm[:, 0:1]),
               inc=(sAC, 1))
        e8_done = sAC.n
        em.wait("vector", sAC, e8_done)
        em.add("vector", RED(s_sm[:, 1:2], s_e8[:], ALU.add), inc=(sDV, 1))
        ssum_done = sDV.n
        em.wait("scalar", sDV, ssum_done)
        em.add("scalar", ACT(s_sm[:, 2:3], s_sm[:, 1:2], AF.Ln))
        em.add("scalar", ACT(s_sm[:, 3:4], s_sm[:, 2:3], AF.Exp, scale=-1.0), inc=(sAC, 1))
        rs_done = sAC.n
        em.wait("vector", sAC, rs_done)
        em.add("vector", STT(p_comb[:, i * 8:(i + 1) * 8], s_e8[:], s_sm[:, 3:4], s_msk[:],
                             ALU.mult, ALU.mult), inc=(sDV, 1))
        epi_done[i] = sDV.n

    em.add("scalar", ACT(c_nexpA[:], s_alog[:], AF.Exp))
    em.add("scalar", lambda e: e.activation(out=c_nexpA[:], in_=c_nexpA[:],
                                            func=AF.Identity, scale=-1.0), inc=(sAC, 1))
    nexpA_done = sAC.n
    em.wait("vector", sAC, nexpA_done)
    for i in range(NT):
        em.add("vector", TT(p_g[:, i * 32:(i + 1) * 32], p_g[:, i * 32:(i + 1) * 32],
                            c_nexpA[:], ALU.mult), inc=(sDV, 1) if i == NT - 1 else None)
    if DEBUG:
        em.wait("sync", sDV, sDV.n)
        em.wait("sync", sAC, sAC.n)
        for nm, src, w in (("beta", p_beta, 32), ("g", p_g, 32), ("comb", p_comb, 8)):
            off = {"beta": 0, "g": NT * 32, "comb": NT * 64}[nm]
            em.dma("sync", dbg_d[:, off:off + NT * w], src[:], dS)
    em.wait("sync", dS, dS.n)
    for sm_ in (dConst, dX[0], dX[1], dSpX[0], dSpX[1]):
        em.wait("sync", sm_, sm_.n)
    em.wait("sync", sDV, sDV.n)
    em.wait("sync", sAC, sAC.n)
    em.wait("sync", sTE, sTE.n)
    em.play("p1")
    ph.close()

    if PHASES >= 3:
        ph = ExitStack()
        sbl = lambda name, shape, dt: ph.enter_context(nc.sbuf_tensor(name, shape, dt))
        psl = lambda name, shape, dt: ph.enter_context(nc.psum_tensor(name, shape, dt))
        HT = T // 2
        s_xns = sbl("s_xns", [128, 16, HT], bf)
        s_wt = [sbl(f"s_wt{i}", [128, 16, 128], bf) for i in range(2)]
        s_strip = [sbl(f"s_strip{i}", [128, 3 + HT], f) for i in range(2)]
        s_hist = sbl("s_hist", [128, 48, 3], f)
        s_conv = [[sbl(f"s_conv{i}_{j}", [128, HT], f) for j in range(2)] for i in range(2)]
        s_act = [sbl(f"s_act{i}", [128, HT], bf) for i in range(2)]
        s_u = [sbl(f"s_u{i}", [128, HT], f) for i in range(2)]
        s_v = [sbl(f"s_v{i}", [128, HT], f) for i in range(2)]
        s_rope = [sbl(f"s_rope{i}", [128, HT], bf) for i in range(2)]
        s_sq = [sbl(f"s_sq{i}", [128, HT], bf) for i in range(2)]
        s_rsq = sbl("s_rsq", [2, 512], f)
        s_rsqb = [sbl(f"s_rsqb{i}", [2, HT], bf) for i in range(2)]
        s_fin = [sbl(f"s_fin{i}", [128, HT], bf) for i in range(2)]
        s_vt = [sbl(f"s_vt{i}", [128, 8, 128], bf) for i in range(2)]
        ps_pr = [psl(f"ps_pr{i}", [128, 512], f) for i in range(4)]
        ps_ss = psl("ps_ss", [128, 512], f)
        ps_rb = [psl(f"ps_rb{i}", [128, 512], f) for i in range(2)]
        ps_vt = psl("ps_vt", [128, 512], bf)
        dFin2 = [dFin, em.sem("dFin2")]
        dVsp2 = [dVsp, em.sem("dVsp2")]

        em.wait("gpsimd", dCqkv, cast_qkv)
        em.wait("sync", dCqkv, cast_qkv)
        strips = []
        for sq_ in range(2):
            for hf in range(2):
                for ip, p in enumerate("qkv"):
                    for d2 in range(16):
                        strips.append((sq_, hf, p, ip * 16 + d2, d2))
        NS3 = len(strips)
        proj_te = [0] * NS3
        evict_dv = [0] * NS3
        sg_ac = [0] * NS3
        act_gp = [0] * NS3
        sq_ac3 = [0] * NS3
        cte = [0] * NS3
        rsqb_ac = [0] * NS3
        fin_dv = [0] * NS3
        w_ld3 = [0, 0]
        seq_ld = 0
        fin_dma = [0, 0]
        v_dma = [0, 0]
        last_ss_ac = 0
        last_vt_dv = 0
        last_rb_fin = 0
        for it in range(NS3 + 3):
            if it < NS3:
                i = it
                sq_, hf, p, hidx, d2 = strips[i]
                par = i % 2
                col0 = sq_ * T + hf * HT
                if i % 48 == 0:
                    if i:
                        em.wait("sync", sTE, proj_te[i - 1])
                    for qq in range(4):
                        seq_ld = em.dma("sync", s_xns[:, qq * 4:(qq + 1) * 4, :],
                                        xnT_d[:, qq * 4:(qq + 1) * 4, col0:col0 + HT],
                                        dXns)
                wbf = {"q": wq_bf, "k": wk_bf, "v": wv_bf}[p]
                if i >= 2:
                    em.wait("sync", sTE, proj_te[i - 2])
                w_ld3[par] = em.dma("sync", s_wt[par][:],
                                    wbf[d2].rearrange("(kb p) m -> p kb m", kb=16, p=128),
                                    dW[par])
                em.wait("tensor", dW[par], w_ld3[par])
                em.wait("tensor", dXns, seq_ld)
                if i >= 2:
                    em.wait("tensor", sDV, evict_dv[i - 2])
                for tn in range(2):
                    bank = ps_pr[2 * par + tn]
                    for kb in range(16):
                        em.add("tensor", MM(bank[:], s_wt[par][:, kb, :],
                                            s_xns[:, kb, tn * 512:(tn + 1) * 512],
                                            start=(kb == 0), stop=(kb == 15)),
                               inc=(sTE, 1) if (tn == 1 and kb == 15) else None)
                proj_te[i] = sTE.n
            if it >= 2 and it - 2 < NS3:
                i = it - 2
                sq_, hf, p, hidx, d2 = strips[i]
                par = i % 2
                if p in "qk":
                    for tn in range(2):
                        em.wait("tensor", sAC, last_ss_ac)
                        if tn == 0:
                            em.wait("tensor", sAC, sq_ac3[i])
                        em.add("tensor", MM(ps_ss[0:2, 0:512], c_obd[:],
                                            s_sq[par][:, tn * 512:(tn + 1) * 512]),
                               inc=(sTE, 1))
                        te3 = sTE.n
                        em.wait("scalar", sTE, te3)
                        em.add("scalar", ACT(s_rsq[:], ps_ss[0:2, 0:512], AF.Ln,
                                             bias=c_eps6[0:2, 0:1]))
                        em.add("scalar", lambda e, par=par, tn=tn: e.activation(
                            out=s_rsqb[par][:, tn * 512:(tn + 1) * 512], in_=s_rsq[:],
                            func=AF.Exp, scale=-0.5), inc=(sAC, 1))
                        last_ss_ac = sAC.n
                    rsqb_ac[i] = sAC.n
                    cte[i] = sTE.n
                else:
                    em.wait("tensor", sGP, act_gp[i])
                    for rnd in range(2):
                        em.wait("tensor", sDV, last_vt_dv)
                        for j4 in range(4):
                            j8 = rnd * 4 + j4
                            em.add("tensor", TR(ps_vt[:, j4 * 128:(j4 + 1) * 128],
                                                s_act[par][:, j8 * 128:(j8 + 1) * 128],
                                                c_idbf[:]),
                                   inc=(sTE, 1) if j4 == 3 else None)
                        te3 = sTE.n
                        em.wait("vector", sTE, te3)
                        if rnd == 0:
                            em.wait("vector", dVsp2[par], v_dma[par])
                        em.add("vector", lambda e, par=par, rnd=rnd: e.tensor_copy(
                            s_vt[par][:, rnd * 4:rnd * 4 + 4, :], ps_vt[:]),
                               inc=(sDV, 1))
                        last_vt_dv = sDV.n
                    cte[i] = sTE.n
                    em.wait("gpsimd", sDV, last_vt_dv)
                    v_dma[par] = em.dma(
                        "gpsimd",
                        v_d.rearrange("(u j p) c -> u p j c", u=4, j=8, p=128)[
                            sq_ * 2 + hf, :, :, d2 * 128:(d2 + 1) * 128],
                        s_vt[par][:], dVsp2[par])
            if it >= 3 and it - 3 < NS3:
                i = it - 3
                sq_, hf, p, hidx, d2 = strips[i]
                par = i % 2
                if p in "qk":
                    col0 = sq_ * T + hf * HT
                    em.wait("tensor", sAC, rsqb_ac[i])
                    em.wait("tensor", sDV, last_rb_fin)
                    for tn in range(2):
                        em.add("tensor", MM(ps_rb[tn][:], c_obdT[:],
                                            s_rsqb[par][:, tn * 512:(tn + 1) * 512]),
                               inc=(sTE, 1) if tn == 1 else None)
                    rbte = sTE.n
                    em.wait("vector", sTE, rbte)
                    em.wait("vector", dFin2[par], fin_dma[par])
                    for tn in range(2):
                        em.add("vector", STT(s_fin[par][:, tn * 512:(tn + 1) * 512],
                                             ps_rb[tn][:], 1.0,
                                             s_rope[par][:, tn * 512:(tn + 1) * 512],
                                             ALU.mult, ALU.mult),
                               inc=(sDV, 1) if tn == 1 else None)
                    fin_dv[i] = sDV.n
                    last_rb_fin = fin_dv[i]
                    em.wait("sync", sDV, fin_dv[i])
                    tgt = (qT_d if p == "q" else kT_d)
                    dst = tgt.rearrange("(d b a r) t -> d a b r t", d=16, b=2, a=2, r=32)
                    em.dma("sync", dst[d2, 0, :, :, col0:col0 + HT],
                           s_fin[par][0:64, :], dFin2[par])
                    fin_dma[par] = em.dma("sync", dst[d2, 1, :, :, col0:col0 + HT],
                                          s_fin[par][64:128, :], dFin2[par])
            if it >= 1 and it - 1 < NS3:
                i = it - 1
                sq_, hf, p, hidx, d2 = strips[i]
                par = i % 2
                cwp, cbp = c_cw[p], c_cb[p]
                em.wait("vector", sTE, proj_te[i])
                if i >= 2:
                    em.wait("vector", sAC, sg_ac[i - 2])
                if hf == 0:
                    em.add("vector", MEMSET(s_strip[par][:, 0:3], 0.0))
                else:
                    em.add("vector", CP(s_strip[par][:, 0:3], s_hist[:, hidx, :]))
                for tn in range(2):
                    em.add("vector", CP(s_strip[par][:, 3 + tn * 512:3 + (tn + 1) * 512],
                                        ps_pr[2 * par + tn][:]),
                           inc=(sDV, 1) if tn == 1 else None)
                evict_dv[i] = sDV.n
                rv = em.relay("scalar", sDV, evict_dv[i], rlyA)
                em.wait("vector", rlyA, rv)
                em.add("vector", TS(s_conv[par][0][:], s_strip[par][:, 0:HT],
                                    cwp[:, d2, 0:1], cbp[:, d2:d2 + 1], ALU.mult, ALU.add))
                em.add("vector", STT(s_conv[par][1][:], s_strip[par][:, 1:1 + HT],
                                     cwp[:, d2, 1:2], s_conv[par][0][:],
                                     ALU.mult, ALU.add))
                em.add("vector", STT(s_conv[par][0][:], s_strip[par][:, 2:2 + HT],
                                     cwp[:, d2, 2:3], s_conv[par][1][:],
                                     ALU.mult, ALU.add))
                em.add("vector", STT(s_conv[par][1][:], s_strip[par][:, 3:3 + HT],
                                     cwp[:, d2, 3:4], s_conv[par][0][:],
                                     ALU.mult, ALU.add), inc=(sDV, 1))
                conv_done = sDV.n
                if hf == 0:
                    em.add("vector", CP(s_hist[:, hidx, :], s_strip[par][:, HT:HT + 3]))
                em.wait("scalar", sDV, conv_done)
                em.add("scalar", ACT(s_conv[par][0][:], s_conv[par][1][:], AF.Exp,
                                     scale=-1.0))
                em.add("scalar", ACT(s_strip[par][:, 0:HT], s_conv[par][0][:], AF.Ln,
                                     bias=c_one[:, 0:1]))
                em.add("scalar", ACT(s_conv[par][0][:], s_strip[par][:, 0:HT], AF.Exp,
                                     scale=-1.0), inc=(sAC, 1))
                sg_ac[i] = sAC.n
                em.wait("gpsimd", sAC, sg_ac[i])
                if i >= 2:
                    em.wait("gpsimd", sTE, cte[i - 2])
                em.add("gpsimd", TT(s_act[par][:], s_conv[par][1][:], s_conv[par][0][:],
                                    ALU.mult), inc=(sGP, 1))
                act_gp[i] = sGP.n
                if p in "qk":
                    csl = slice(hf * HT, (hf + 1) * HT)
                    em.add("gpsimd", TT(s_u[par][:], s_act[par][:], c_csc[:, csl],
                                        ALU.mult))
                    em.add("gpsimd", TT(s_v[par][0:64, :], s_act[par][64:128, :],
                                        c_css[64:128, csl], ALU.mult))
                    em.add("gpsimd", TT(s_v[par][64:128, :], s_act[par][0:64, :],
                                        c_css[0:64, csl], ALU.mult), inc=(sGP, 1))
                    uv_gp = sGP.n
                    em.wait("vector", sGP, uv_gp)
                    if i >= 2:
                        em.wait("vector", sDV, fin_dv[i - 2])
                    em.add("vector", TT(s_rope[par][0:64, :], s_u[par][0:64, :],
                                        s_v[par][0:64, :], ALU.subtract))
                    em.add("vector", TT(s_rope[par][64:128, :], s_u[par][64:128, :],
                                        s_v[par][64:128, :], ALU.add), inc=(sDV, 1))
                    rope_done = sDV.n
                    em.wait("scalar", sDV, rope_done)
                    if i >= 2:
                        em.wait("scalar", sTE, cte[i - 2])
                    em.add("scalar", ACT(s_sq[par][:], s_rope[par][:], AF.Square),
                           inc=(sAC, 1))
                    sq_ac3[i] = sAC.n
        for sm_ in (dXns, dW[0], dW[1], dFin, dFin2[1]):
            em.wait("sync", sm_, sm_.n)
        em.wait("sync", sDV, sDV.n)
        em.wait("sync", sAC, sAC.n)
        em.wait("sync", sGP, sGP.n)
        em.wait("gpsimd", dVsp, dVsp.n)
        em.wait("gpsimd", dVsp2[1], dVsp2[1].n)
        em.play("p3")
        ph.close()

    if PHASES >= 4:
        ph = ExitStack()
        sbl = lambda name, shape, dt: ph.enter_context(nc.sbuf_tensor(name, shape, dt))
        psl = lambda name, shape, dt: ph.enter_context(nc.psum_tensor(name, shape, dt))
        B0 = psl("p4_B0", [128, 512], f)
        B1 = psl("p4_B1", [128, 512], f)
        B2 = psl("p4_B2", [128, 512], f)
        B3 = psl("p4_B3", [128, 512], bf)
        B4 = psl("p4_B4", [128, 512], f)
        B5 = psl("p4_B5", [128, 512], f)
        B6 = psl("p4_B6", [128, 512], f)
        B7 = psl("p4_B7", [128, 512], f)
        s_qc = [sbl(f"s_qc{i}", [128, 16, 128], bf) for i in range(2)]
        s_kc = [sbl(f"s_kc{i}", [128, 16, 128], bf) for i in range(2)]
        s_kt = [sbl(f"s_kt{i}", [128, 2048], bf) for i in range(2)]
        s_vk = [sbl(f"s_vk{i}", [128, 2048], bf) for i in range(2)]
        s_oa = [sbl(f"s_oa{i}", [128, 2048], bf) for i in range(2)]
        s_gT = [sbl(f"s_gT{i}", [32, 128], f) for i in range(2)]
        s_cgT = [sbl(f"s_cgT{i}", [32, 128], f) for i in range(2)]
        s_cgtok = [sbl(f"s_cgtok{i}", [128, 32], f) for i in range(2)]
        s_negcg = [sbl(f"s_negcg{i}", [128, 32], f) for i in range(2)]
        s_ecg = [sbl(f"s_ecg{i}", [128, 32], f) for i in range(2)]
        s_cgf = [sbl(f"s_cgf{i}", [1, 4096], f) for i in range(2)]
        s_blr = [sbl(f"s_blr{i}", [1, 32], f) for i in range(2)]
        s_BL = [sbl(f"s_BL{i}", [128, 32], f) for i in range(2)]
        s_kkr = [sbl(f"s_kkr{i}", [128, 512], bf) for i in range(2)]
        s_qkr = [sbl(f"s_qkr{i}", [128, 512], bf) for i in range(2)]
        s_dtr = [sbl(f"s_dtr{i}", [128, 512], bf) for i in range(2)]
        s_dt4 = [sbl(f"s_dt4{i}", [128, 512], bf) for i in range(2)]
        s_D4 = [sbl(f"s_D4{i}", [128, 512], bf) for i in range(2)]
        s_bn4 = [sbl(f"s_bn4{i}", [128, 512], bf) for i in range(2)]
        s_bt4 = [sbl(f"s_bt4{i}", [128, 512], bf) for i in range(2)]
        s_pw = [[sbl(f"s_pw{i}_{j}", [128, 512], bf) for j in range(2)] for i in range(2)]
        s_tp = [[sbl(f"s_tp{i}_{j}", [128, 512], bf) for j in range(2)] for i in range(2)]
        s_hh = [[sbl(f"s_hh{i}_{j}", [128, 512], bf) for j in range(2)] for i in range(2)]
        s_U4 = [sbl(f"s_U4{i}", [128, 256], bf) for i in range(2)]
        s_Ub4 = [sbl(f"s_Ub4{i}", [128, 256], bf) for i in range(2)]
        s_R4 = [sbl(f"s_R4{i}", [128, 256], bf) for i in range(2)]
        s_r14 = [sbl(f"s_r14{i}", [128, 256], f) for i in range(2)]
        s_o14 = [sbl(f"s_o14{i}", [128, 256], f) for i in range(2)]
        s_w4 = [sbl(f"s_w4_{i}", [128, 4], f) for i in range(2)]
        s_o24 = [sbl(f"s_o24{i}", [128, 256], f) for i in range(2)]

        for b_ in range(BPC):
            em.add("vector", MEMSET(p_S[b_][0][:], 0.0))
            em.add("vector", MEMSET(p_S[b_][1][:], 0.0))
            em.add("vector", MEMSET(p_Sbf[b_][:], 0.0), inc=(sDV, 1))
        s_zero_dv = sDV.n

        NCHUNKS = int(os.environ.get("MK_P4_CHUNKS", 2 * NCH))
        sched = [(ci % 2, ci // 2) for ci in range(NCHUNKS)]
        prep_dma = [0] * NCHUNKS
        prep_pr = [0] * NCHUNKS
        prep_cgf = [0] * NCHUNKS
        prep_dv = [0] * NCHUNKS
        prep_ac = [0] * NCHUNKS
        ch_lastuse_te = [0] * NCHUNKS
        ch_o_spill = [0] * NCHUNKS
        ch_blend_ac = [0] * NCHUNKS
        ch_o_dv = [0] * NCHUNKS
        st = {"rb_free_ac": 0, "B2_free": 0, "b6_free": 0, "b6_free_ac": 0}

        def emit_prep(ci):
            b, c = sched[ci]
            par = ci % 2
            i_tile = b * NCH + c
            row0 = b * T + c * CH
            if ci >= 2:
                em.wait("sync", sTE, ch_lastuse_te[ci - 2])
                em.wait("sync", sDV, ch_o_dv[ci - 2])
            em.dma("sync", s_qc[par][:],
                   qT_d.rearrange("(hp p) t -> p hp t", hp=16, p=128)[:, :, row0:row0 + 128],
                   dT[par])
            em.dma("sync", s_kc[par][:],
                   kT_d.rearrange("(hp p) t -> p hp t", hp=16, p=128)[:, :, row0:row0 + 128],
                   dT[par])
            em.dma("sync", s_kt[par][:], kT_d[:, row0:row0 + 128], dT[par], transpose=True)
            d1 = em.dma("sync", s_vk[par][:], v_d[row0:row0 + 128, :], dT[par])
            em.wait("tensor", sDV, st["b6_free"])
            em.wait("tensor", sAC, st["b6_free_ac"])
            em.add("tensor", TR(B6[0:32, 0:128], p_g[:, i_tile * 32:(i_tile + 1) * 32],
                                c_idf[:]), inc=(sTE, 1))
            gT_te = sTE.n
            em.wait("vector", sTE, gT_te)
            em.add("vector", CP(s_gT[par][:], B6[0:32, 0:128]))
            em.add("vector", MEMSET(s_cgT[par][:], 0.0))
            em.add("vector", SCAN(s_cgT[par][:], s_gT[par][:], s_gT[par][:], 0.0,
                                  ALU.add, ALU.bypass), inc=(sDV, 1))
            scan_dv = sDV.n
            em.wait("sync", sDV, scan_dv)
            em.dma("sync", bounce_cg[par], s_cgT[par][:], dPr[par])
            em.wait("sync", dPr[par], dPr[par].n)
            em.dma("sync", s_cgf[par][0:1, :],
                   bounce_cg[par].rearrange("h t -> (h t)"), dPr[par])
            cgf_dma = dPr[par].n
            em.wait("tensor", sDV, scan_dv)
            em.add("tensor", TR(B6[0:128, 128:160], s_cgT[par][:], c_idf[0:32, 0:32]), inc=(sTE, 1))
            cgtok_te = sTE.n
            em.wait("vector", sTE, cgtok_te)
            em.add("vector", CP(s_cgtok[par][:], B6[:, 128:160]))
            em.add("vector", TS(s_negcg[par][:], B6[:, 128:160], -1.0, 0.0, ALU.mult),
                   inc=(sDV, 1))
            cg_dv = sDV.n
            em.wait("scalar", sDV, cg_dv)
            em.add("scalar", ACT(s_ecg[par][:], s_cgtok[par][:], AF.Exp), inc=(sAC, 1))
            ecg_ac = sAC.n
            em.wait("sync", sAC, ecg_ac)
            em.dma("sync", bounce_bl[par], s_ecg[par][127:128, 0:32], dPr[par])
            em.wait("sync", dPr[par], dPr[par].n)
            em.dma("sync", s_blr[par][0:1, :], bounce_bl[par], dPr[par])
            blr_dma = dPr[par].n
            em.wait("tensor", dPr[par], blr_dma)
            em.wait("tensor", sDV, cg_dv)
            em.add("tensor", MM(B6[:, 160:192], c_ones[0:1, :], s_blr[par][0:1, :]),
                   inc=(sTE, 1))
            bl_te = sTE.n
            em.wait("vector", sTE, bl_te)
            em.add("vector", CP(s_BL[par][:], B6[:, 160:192]), inc=(sDV, 1))
            prep_dv[ci] = sDV.n
            prep_ac[ci] = ecg_ac
            prep_dma[ci] = d1
            prep_pr[ci] = blr_dma
            prep_cgf[ci] = cgf_dma
            st["b6_free"] = sDV.n

        NG = NCHUNKS * 8

        def ginfo(g):
            ci = g // 8
            b, c = sched[ci]
            return ci, g % 8, b, c, ci % 2, g % 2

        GS = {-1: {"kq_dv": 0, "t1_dv": 0, "bt_dv": 0, "blend_dv": 0, "o_dv": 0,
                   "exp_ac": 0},
              -2: {"blend_dv": 0, "o_dv": 0}}

        def hsl(buf, h):
            return buf[64 * (h % 2):64 * (h % 2) + 64, h // 2, :]

        def stile(buf, h):
            return buf[64 * (h % 2):64 * (h % 2) + 64,
                       (h // 2) * 64:(h // 2) * 64 + 64]

        def emit_front(g):
            ci, grp, b, c, par, gpar = ginfo(g)
            i_tile = b * NCH + c
            heads = [4 * grp + j for j in range(4)]
            gs = GS[g] = {}
            Sbf = p_Sbf[b]
            em.wait("tensor", sDV, GS[g - 2]["blend_dv"])
            em.wait("tensor", sAC, st["rb_free_ac"])
            em.wait("tensor", dT[par], prep_dma[ci])
            em.wait("tensor", dPr[par], prep_cgf[ci])
            em.add("tensor", MM(B7[:], c_ones[0:1, :],
                                s_cgf[par][0:1, grp * 512:(grp + 1) * 512]), inc=(sTE, 1))
            rb_te = sTE.n
            em.wait("tensor", sDV, GS[g - 1]["kq_dv"])
            em.wait("tensor", sDV, s_zero_dv)
            em.wait("tensor", sDV, GS[g - 1]["t1_dv"])
            if ci >= 2:
                em.wait("tensor", sAC, ch_blend_ac[ci - 2])
            for j, h in enumerate(heads):
                em.add("tensor", MM(([B0, B1][j % 2])[:, (j // 2) * 128:(j // 2) * 128 + 128],
                                    hsl(s_kc[par], h), hsl(s_kc[par], h), start=(j < 2)))
            for j, h in enumerate(heads):
                em.add("tensor",
                       MM(([B0, B1][j % 2])[:, 256 + (j // 2) * 128:256 + (j // 2) * 128 + 128],
                          hsl(s_kc[par], h), hsl(s_qc[par], h), start=False))
            for j, h in enumerate(heads):
                em.add("tensor", MM(([B4, B5][j % 2])[:, (j // 2) * 64:(j // 2) * 64 + 64],
                                    hsl(s_kc[par], h), stile(Sbf, h), start=(j < 2)))
            for j, h in enumerate(heads):
                em.add("tensor",
                       MM(([B4, B5][j % 2])[:, 128 + (j // 2) * 64:128 + (j // 2) * 64 + 64],
                          hsl(s_qc[par], h), stile(Sbf, h), start=False),
                       inc=(sTE, 1) if j == 3 else None)
            l0_te = sTE.n
            yield
            em.wait("scalar", sTE, rb_te)
            em.wait("scalar", sDV, prep_dv[ci])
            em.wait("scalar", sDV, GS[g - 2]["o_dv"])
            for j, h in enumerate(heads):
                em.add("scalar", ACT(s_dtr[gpar][:, j * 128:(j + 1) * 128],
                                     B7[:, j * 128:(j + 1) * 128], AF.Exp,
                                     bias=s_negcg[par][:, h:h + 1]),
                       inc=(sAC, 1) if j == 3 else None)
            gs["exp_ac"] = sAC.n
            st["rb_free_ac"] = gs["exp_ac"]
            em.wait("vector", sAC, gs["exp_ac"])
            em.wait("vector", sDV, GS[g - 2]["o_dv"])
            for j in range(4):
                em.add("vector", TT(s_dt4[gpar][:, j * 128:(j + 1) * 128],
                                    s_dtr[gpar][:, j * 128:(j + 1) * 128],
                                    c_umask[:], ALU.min),
                       inc=(sDV, 1) if j == 3 else None)
            dtm_dv = gs["dtm_dv"] = sDV.n
            rv = em.relay("gpsimd", sDV, dtm_dv, rlyG)
            em.wait("vector", sTE, l0_te)
            em.add("vector", CP(s_kkr[gpar][:], B0[:]))
            em.add("vector", CP(s_qkr[gpar][:], B1[:]), inc=(sDV, 1))
            kq_dv = gs["kq_dv"] = sDV.n
            em.wait("vector", rlyG, rv)
            for j in range(4):
                em.add("vector", CP(s_w4[gpar][:, j:j + 1],
                                    s_dt4[gpar][:, j * 128 + 127:j * 128 + 128]))
            em.wait("vector", sDV, prep_dv[ci])
            em.wait("vector", sAC, prep_ac[ci])
            for j, h in enumerate(heads):
                em.add("vector", STT(s_r14[gpar][:, j * 64:(j + 1) * 64],
                                     [B4, B5][j % 2][:, (j // 2) * 64:(j // 2) * 64 + 64],
                                     s_ecg[par][:, h:h + 1],
                                     s_vk[par][:, h * 64:(h + 1) * 64],
                                     ALU.mult, ALU.subtract),
                       inc=(sDV, 1) if j == 3 else None)
            r1_dv = sDV.n
            rv = em.relay("gpsimd", sDV, r1_dv, rlyG)
            em.wait("vector", rlyG, rv)
            for j, h in enumerate(heads):
                em.add("vector", TS(s_R4[gpar][:, j * 64:(j + 1) * 64],
                                    s_r14[gpar][:, j * 64:(j + 1) * 64],
                                    p_negb[:, i_tile * 32 + h:i_tile * 32 + h + 1], 0.0,
                                    ALU.mult), inc=(sDV, 1) if j == 3 else None)
            gs["r_dv"] = sDV.n
            for j, h in enumerate(heads):
                em.add("vector", TS(s_o14[gpar][:, j * 64:(j + 1) * 64],
                                    [B4, B5][j % 2][:, 128 + (j // 2) * 64:
                                                    128 + (j // 2) * 64 + 64],
                                    s_ecg[par][:, h:h + 1], 0.0, ALU.mult),
                       inc=(sDV, 1) if j == 3 else None)
            gs["t1_dv"] = sDV.n
            yield
            em.wait("tensor", sDV, dtm_dv)
            em.wait("tensor", sDV, kq_dv)
            em.wait("tensor", sDV, GS[g - 1]["bt_dv"])
            for j in range(4):
                em.add("tensor", TR(B3[:, j * 128:(j + 1) * 128],
                                    s_dt4[gpar][:, j * 128:(j + 1) * 128], c_idbf[:]),
                       inc=(sTE, 1) if j == 3 else None)
            dtT_te = sTE.n
            em.wait("vector", sTE, dtT_te)
            em.add("vector", TT(s_D4[gpar][:], B3[:], c_negi4[:], ALU.add),
                   inc=(sDV, 1))
            d_dv = sDV.n
            rv = em.relay("gpsimd", sDV, d_dv, rlyG)
            em.wait("vector", rlyG, rv)
            for j, h in enumerate(heads):
                em.add("vector", STT(s_bn4[gpar][:, j * 128:(j + 1) * 128],
                                     [s_kkr, s_qkr][j % 2][gpar]
                                     [:, (j // 2) * 128:(j // 2) * 128 + 128],
                                     p_negb[:, i_tile * 32 + h:i_tile * 32 + h + 1],
                                     s_D4[gpar][:, j * 128:(j + 1) * 128],
                                     ALU.mult, ALU.mult),
                       inc=(sDV, 1) if j == 3 else None)
            gs["bn_dv"] = sDV.n
            em.wait("gpsimd", sDV, kq_dv)
            em.wait("gpsimd", sDV, dtm_dv)
            for j in range(4):
                em.add("gpsimd", TT(s_dtr[gpar][:, j * 128:(j + 1) * 128],
                                    [s_kkr, s_qkr][j % 2][gpar]
                                    [:, 256 + (j // 2) * 128:256 + (j // 2) * 128 + 128],
                                    s_dt4[gpar][:, j * 128:(j + 1) * 128], ALU.mult),
                       inc=(sGP, 1) if j == 3 else None)
            gs["p2_gp"] = sGP.n
            yield
            em.wait("tensor", sDV, gs["bn_dv"])
            for j in range(4):
                em.add("tensor", TR(B3[:, j * 128:(j + 1) * 128],
                                    s_bn4[gpar][:, j * 128:(j + 1) * 128], c_idbf[:]),
                       inc=(sTE, 1) if j == 3 else None)
            bnT_te = sTE.n
            em.wait("vector", sTE, bnT_te)
            em.add("vector", CP(s_bt4[gpar][:], B3[:]))
            em.add("vector", TT(s_hh[gpar][0][:], B3[:], c_i4[:], ALU.add),
                   inc=(sDV, 1))
            gs["bt_dv"] = sDV.n

        def emit_back(g):
            ci, grp, b, c, par, gpar = ginfo(g)
            i_tile = b * NCH + c
            heads = [4 * grp + j for j in range(4)]
            gs = GS[g]
            Scur = p_S[b][c % 2]
            Snext = p_S[b][(c + 1) % 2]
            Sbf = p_Sbf[b]
            tpow_prev, pow_prev = s_bt4[gpar], s_bn4[gpar]
            h_prev = s_hh[gpar][0]
            ev_prev = gs["bt_dv"]
            tp_ac = 0
            h_ev = 0
            for lvl in range(1, 7):
                lp = lvl % 2
                em.wait("tensor", sDV, ev_prev)
                em.wait("tensor", sDV, st["B2_free"])
                if tp_ac:
                    em.wait("tensor", sAC, tp_ac)
                if lvl < 6:
                    em.wait("tensor", sAC, st["b6_free_ac"])
                    em.wait("tensor", sDV, st["b6_free"])
                    for j in range(4):
                        em.add("tensor", MM(B6[:, j * 128:(j + 1) * 128],
                                            pow_prev[:, j * 128:(j + 1) * 128],
                                            tpow_prev[:, j * 128:(j + 1) * 128],
                                            start=(j == 0)))
                for j in range(4):
                    em.add("tensor", MM(B2[:, j * 128:(j + 1) * 128],
                                        tpow_prev[:, j * 128:(j + 1) * 128],
                                        pow_prev[:, j * 128:(j + 1) * 128],
                                        start=(j == 0)),
                           inc=(sTE, 1) if j == 3 else None)
                pw_te = sTE.n
                em.wait("vector", sTE, pw_te)
                em.add("vector", CP(s_pw[gpar][lp][:], B2[:]), inc=(sDV, 1))
                pow_dv = sDV.n
                st["B2_free"] = pow_dv
                if lvl < 6:
                    em.wait("scalar", sTE, pw_te)
                    em.add("scalar", lambda e, a=s_tp[gpar][lp], b2=B6: e.copy(
                        a[:], b2[:]), inc=(sAC, 1))
                    tp_ac = sAC.n
                    st["b6_free_ac"] = tp_ac
                em.wait("tensor", sDV, pow_dv)
                if h_ev:
                    em.wait("tensor", sDV, h_ev)
                for j in range(4):
                    em.add("tensor", MM(B2[:, j * 128:(j + 1) * 128], c_idbf[:],
                                        h_prev[:, j * 128:(j + 1) * 128],
                                        start=(j == 0), stop=False))
                    em.add("tensor", MM(B2[:, j * 128:(j + 1) * 128],
                                        s_pw[gpar][lp][:, j * 128:(j + 1) * 128],
                                        h_prev[:, j * 128:(j + 1) * 128],
                                        start=False, stop=True),
                           inc=(sTE, 1) if j == 3 else None)
                h_te = sTE.n
                em.wait("vector", sTE, h_te)
                em.add("vector", CP(s_hh[gpar][lp][:], B2[:]), inc=(sDV, 1))
                h_ev = sDV.n
                st["B2_free"] = h_ev
                h_prev = s_hh[gpar][lp]
                tpow_prev, pow_prev = s_tp[gpar][lp], s_pw[gpar][lp]
                ev_prev = h_ev
                yield
            em.wait("tensor", sDV, gs["r_dv"])
            em.wait("tensor", sDV, st["B2_free"])
            for j in range(4):
                em.add("tensor", MM(B2[:, j * 64:(j + 1) * 64],
                                    h_prev[:, j * 128:(j + 1) * 128],
                                    s_R4[gpar][:, j * 64:(j + 1) * 64],
                                    start=(j == 0)),
                       inc=(sTE, 1) if j == 3 else None)
            u_te = sTE.n
            em.wait("vector", sTE, u_te)
            em.add("vector", CP(s_U4[gpar][:], B2[:, 0:256]))
            for j, h in enumerate(heads):
                em.add("vector", TS(s_Ub4[gpar][:, j * 64:(j + 1) * 64],
                                    B2[:, j * 64:(j + 1) * 64],
                                    s_w4[gpar][:, j:j + 1], 0.0,
                                    ALU.mult), inc=(sDV, 1) if j == 3 else None)
            u_dv = sDV.n
            yield
            em.wait("tensor", sGP, gs["p2_gp"])
            em.wait("tensor", sDV, u_dv)
            for j in range(4):
                em.add("tensor", MM(B2[:, 256 + j * 64:256 + (j + 1) * 64],
                                    s_dtr[gpar][:, j * 128:(j + 1) * 128],
                                    s_U4[gpar][:, j * 64:(j + 1) * 64],
                                    start=False))
            if g + 1 < NG:
                em.wait("tensor", sAC, GS[g + 1]["exp_ac"])
            for j, h in enumerate(heads):
                em.add("tensor", MM(B7[64 * (h % 2):64 * (h % 2) + 64,
                                       j * 64:(j + 1) * 64],
                                    s_kt[par][:, h * 64:(h + 1) * 64],
                                    s_Ub4[gpar][:, j * 64:(j + 1) * 64],
                                    start=(j < 2),
                                    tile_position=(0, 64 * (h % 2))),
                       inc=(sTE, 1) if j == 3 else None)
            oi_te = sTE.n
            ch_lastuse_te[ci] = oi_te
            em.wait("vector", sTE, oi_te)
            if ci >= 2:
                em.wait("vector", dOsp[par], ch_o_spill[ci - 2])
            for j, h in enumerate(heads):
                em.add("vector", STT(s_o24[gpar][:, j * 64:(j + 1) * 64],
                                     B2[:, 256 + j * 64:256 + (j + 1) * 64], 1.0,
                                     s_o14[gpar][:, j * 64:(j + 1) * 64],
                                     ALU.mult, ALU.add),
                       inc=(sDV, 1) if j == 3 else None)
            o1_dv = sDV.n
            rv = em.relay("gpsimd", sDV, o1_dv, rlyG)
            em.wait("vector", rlyG, rv)
            for j, h in enumerate(heads):
                em.add("vector", STT(s_oa[par][:, h * 64:(h + 1) * 64],
                                     s_vk[par][:, h * 64:(h + 1) * 64],
                                     c_dD[:, h:h + 1],
                                     s_o24[gpar][:, j * 64:(j + 1) * 64],
                                     ALU.mult, ALU.add),
                       inc=(sDV, 1) if j == 3 else None)
            gs["o_dv"] = sDV.n
            st["B2_free"] = o1_dv
            yield
            for j, h in enumerate(heads):
                em.add("vector", STT(stile(Snext, h), stile(Scur, h),
                                     s_BL[par][64 * (h % 2):64 * (h % 2) + 64, h:h + 1],
                                     B7[64 * (h % 2):64 * (h % 2) + 64,
                                        j * 64:(j + 1) * 64],
                                     ALU.mult, ALU.add),
                       inc=(sDV, 1) if j == 3 else None)
            gs["blend_dv"] = sDV.n
            em.wait("scalar", sDV, gs["blend_dv"])
            for j, h in enumerate(heads):
                em.add("scalar",
                       lambda e, a=stile(Sbf, h), bb2=stile(Snext, h): e.copy(a, bb2),
                       inc=(sAC, 1) if j == 3 else None)
            if grp == 7:
                ch_blend_ac[ci] = sAC.n
                ch_o_dv[ci] = gs["o_dv"]

        emit_prep(0)
        if NCHUNKS > 1:
            emit_prep(1)
        for dst_, src_ in [(wg_bf, wgT), (wo_bf, woT)]:
            em.dma("gpsimd", dst_, src_, dCgw)
        for e_ in range(E):
            em.dma("gpsimd", wf_bf[e_], wffn[e_], dCff)
        cast_gwo = dCgw.n
        cast_all = dCff.n
        for g in range(NG + 1):
            fgen = emit_front(g) if g < NG else None
            bgen = emit_back(g - 1) if g >= 1 else None

            def nf():
                if fgen is not None:
                    next(fgen, None)

            def nb():
                if bgen is not None:
                    next(bgen, None)
            nf()
            nb()
            nf()
            nb()
            nf()
            nb()
            nf()
            nb()
            nb()
            nb()
            nb()
            nb()
            nb()
            if g >= 8 and g % 8 == 0:
                ci_done = g // 8 - 1
                bsp, csp = sched[ci_done]
                parsp = ci_done % 2
                row0 = bsp * T + csp * CH
                em.wait("sync", sDV, ch_o_dv[ci_done])
                ch_o_spill[ci_done] = em.dma("sync", o_d[row0:row0 + 128, :],
                                             s_oa[parsp][:], dOsp[parsp])
                if ci_done + 2 < NCHUNKS:
                    emit_prep(ci_done + 2)
        for sm_ in (dT[0], dT[1], dPr[0], dPr[1], dOsp[0], dOsp[1]):
            em.wait("sync", sm_, sm_.n)
        em.wait("sync", sDV, sDV.n)
        em.wait("sync", sAC, sAC.n)
        em.wait("sync", sGP, sGP.n)
        em.play("p4")
        ph.close()
    stack2.close()

    if PHASES >= 5:
        ph = ExitStack()
        sbl = lambda name, shape, dt: ph.enter_context(nc.sbuf_tensor(name, shape, dt))
        psl = lambda name, shape, dt: ph.enter_context(nc.psum_tensor(name, shape, dt))
        SP5 = 1024
        NSP5 = TOK // SP5
        TPS5 = SP5 // 128
        PG = [[psl(f"p5_G{i}{j}", [128, 512], f) for j in range(2)] for i in range(2)]
        PSS = psl("p5_SS", [128, 512], f)
        PRB = psl("p5_RB", [128, 512], f)
        PW = [psl(f"p5_W{i}", [128, 512], f) for i in range(2)]
        s_x5 = sbl("s_x5", [128, 16, SP5], bf)
        s_oT5 = sbl("s_oT5", [128, 16, SP5], bf)
        s_o55 = sbl("s_o55", [128, 16, SP5], bf)
        s_sq5 = sbl("s_sq5", [128, SP5], bf)
        s_r2b = sbl("s_r2b", [2, 16, SP5], bf)
        s_r25 = sbl("s_r25", [2, 512], f)
        s_wg5 = [sbl(f"s_wg5_{i}", [128, 16, 128], bf) for i in range(2)]
        s_wo5 = [sbl(f"s_wo5_{i}", [128, 16, 512], bf) for i in range(2)]
        s_a5 = sbl("s_a5", [128, SP5], f)
        s_sg5 = sbl("s_sg5", [128, SP5], f)
        s_t25 = sbl("s_t25", [128, 512], f)
        s_ev5 = [sbl(f"s_ev5_{i}", [128, 512], f) for i in range(2)]
        dO5 = [em.sem("dO5a"), em.sem("dO5b")]
        dWg5 = [em.sem("dWg5a"), em.sem("dWg5b")]
        dWoQ = [em.sem("dWoQa"), em.sem("dWoQb")]
        dX5 = em.sem("dX5")
        dSd = [em.sem("dSda"), em.sem("dSdb")]
        em.wait("sync", dCgw, cast_gwo)

        sq_ac = [0] * 16
        ss_te = [[0, 0] for _ in range(16)]
        rstd_ac = [[0, 0] for _ in range(16)]
        gate_te = [[0, 0] for _ in range(16)]
        rb_te5 = [[0, 0] for _ in range(16)]
        a_dv = [[0, 0] for _ in range(16)]
        o5_dv = [[0, 0] for _ in range(16)]
        prev_rbte = [0] * 16
        prev_adv = [0] * 16
        prev_gate_all = 0
        wo_q_te = [0, 0, 0, 0]
        prev_wo_q_te = [0, 0, 0, 0]
        wg_ld5 = [0, 0]
        o_ld5 = [0, 0]
        woq_ld = [0, 0, 0, 0]
        ev_dv5 = [0, 0]
        sd_dma = [0, 0]
        for sp in range(NSP5):
            rows0 = sp * SP5
            em.wait("sync", sTE, prev_gate_all)
            x5_ld = em.dma("sync", s_x5[:], xnT_d[:, :, rows0:rows0 + SP5], dX5)
            for it in range(18):
                d2 = it
                e2 = it - 2
                par = d2 % 2
                ep = e2 % 2
                if d2 < 16:
                    if d2 >= 2:
                        em.wait("sync", sAC, sq_ac[d2 - 2])
                    if sp > 0:
                        em.wait("sync", sDV, prev_adv[d2])
                    o_ld5[par] = em.dma("sync", s_oT5[:, d2, :],
                                        o_d[rows0:rows0 + SP5, d2 * 128:(d2 + 1) * 128],
                                        dO5[par], transpose=True)
                if 0 <= e2 < 16:
                    if e2 >= 2:
                        em.wait("sync", sTE, gate_te[e2 - 2][1])
                    q5, c0 = e2 // 4, (e2 % 4) * 128
                    wg_ld5[ep] = em.dma(
                        "sync", s_wg5[ep][:],
                        wg_bf[q5].rearrange("kb p c -> p kb c")[:, :, c0:c0 + 128],
                        dWg5[ep])
                if d2 in (8, 12):
                    qq = (d2 - 8) // 4
                    em.wait("sync", sTE, prev_wo_q_te[qq + 2])
                    woq_ld[qq] = em.dma("sync", s_wo5[qq % 2][:],
                                        wo_bf[qq].rearrange("kb p c -> p kb c"),
                                        dWoQ[qq % 2])
                if 1 <= d2 <= 16:
                    dd = d2 - 1
                    em.wait("scalar", sTE, ss_te[dd][1])
                    em.wait("scalar", sTE, prev_rbte[dd])
                    em.add("scalar", ACT(s_r25[:], PSS[0:2, 0:512], AF.Ln,
                                         bias=c_eps6r[0:2, 0:1]))
                    em.add("scalar", lambda e, dd=dd: e.activation(
                        out=s_r2b[:, dd, 512:1024], in_=s_r25[:], func=AF.Exp,
                        scale=-0.5), inc=(sAC, 1))
                    rstd_ac[dd][1] = sAC.n
                if d2 < 16:
                    em.wait("scalar", dO5[par], o_ld5[par])
                    if d2 >= 1:
                        em.wait("scalar", sTE, ss_te[d2 - 1][1])
                    em.add("scalar", ACT(s_sq5[:], s_oT5[:, d2, :], AF.Square),
                           inc=(sAC, 1))
                    sq_ac[d2] = sAC.n
                if 0 <= e2 < 16:
                    em.wait("tensor", sAC, rstd_ac[e2][0])
                    em.wait("tensor", sDV, a_dv[e2 - 1][1] if e2 >= 1 else prev_adv[15])
                    em.add("tensor", MM(PRB[:], c_obd64T[:], s_r2b[:, e2, 0:512]),
                           inc=(sTE, 1))
                    rb_te5[e2][0] = sTE.n
                    em.wait("tensor", dX5, x5_ld)
                    em.wait("tensor", dWg5[ep], wg_ld5[ep])
                    em.wait("tensor", sDV, o5_dv[e2 - 2][0])
                    for kb in range(16):
                        em.add("tensor", MM(PG[ep][0][:], s_wg5[ep][:, kb, :],
                                            s_x5[:, kb, 0:512],
                                            start=(kb == 0), stop=(kb == 15)),
                               inc=(sTE, 1) if kb == 15 else None)
                    gate_te[e2][0] = sTE.n
                if d2 < 16:
                    em.wait("tensor", sAC, sq_ac[d2])
                    em.wait("tensor", sAC, rstd_ac[d2 - 1][1])
                    em.add("tensor", MM(PSS[0:2, 0:512], c_obd64[:], s_sq5[:, 0:512]),
                           inc=(sTE, 1))
                    ss_te[d2][0] = sTE.n
                if 0 <= e2 < 16:
                    em.wait("vector", sTE, rb_te5[e2][0])
                    em.add("vector", STT(s_a5[:, 0:512], PRB[:], c_onc[:, 0:1],
                                         s_oT5[:, e2, 0:512], ALU.mult, ALU.mult),
                           inc=(sDV, 1))
                    a_dv[e2][0] = sDV.n
                if 0 <= e2 < 16:
                    em.wait("scalar", sDV, a_dv[e2][0])
                    em.add("scalar", ACT(s_sg5[:, 0:512], s_a5[:, 0:512], AF.Exp,
                                         scale=-1.0))
                    em.add("scalar", ACT(s_sg5[:, 0:512], s_sg5[:, 0:512], AF.Ln,
                                         bias=c_one[:, 0:1]))
                    em.add("scalar", ACT(s_sg5[:, 0:512], s_sg5[:, 0:512], AF.Exp,
                                         scale=-1.0), inc=(sAC, 1))
                    sg0_ac = sAC.n
                if d2 < 16:
                    em.wait("scalar", sTE, ss_te[d2][0])
                    em.add("scalar", ACT(s_r25[:], PSS[0:2, 0:512], AF.Ln,
                                         bias=c_eps6r[0:2, 0:1]))
                    em.add("scalar", lambda e, d2=d2: e.activation(
                        out=s_r2b[:, d2, 0:512], in_=s_r25[:], func=AF.Exp,
                        scale=-0.5), inc=(sAC, 1))
                    rstd_ac[d2][0] = sAC.n
                if 0 <= e2 < 16:
                    em.wait("tensor", sAC, rstd_ac[e2][1])
                    em.wait("tensor", sDV, a_dv[e2][0])
                    em.add("tensor", MM(PRB[:], c_obd64T[:], s_r2b[:, e2, 512:1024]),
                           inc=(sTE, 1))
                    rb_te5[e2][1] = sTE.n
                    em.wait("tensor", sDV, o5_dv[e2 - 2][1])
                    for kb in range(16):
                        em.add("tensor", MM(PG[ep][1][:], s_wg5[ep][:, kb, :],
                                            s_x5[:, kb, 512:1024],
                                            start=(kb == 0), stop=(kb == 15)),
                               inc=(sTE, 1) if kb == 15 else None)
                    gate_te[e2][1] = sTE.n
                if d2 < 16:
                    em.wait("tensor", sAC, rstd_ac[d2][0])
                    em.add("tensor", MM(PSS[0:2, 0:512], c_obd64[:], s_sq5[:, 512:1024]),
                           inc=(sTE, 1))
                    ss_te[d2][1] = sTE.n
                if 0 <= e2 < 16:
                    em.wait("vector", sAC, sg0_ac)
                    em.add("vector", TT(s_t25[:], s_a5[:, 0:512], s_sg5[:, 0:512],
                                        ALU.mult))
                    em.wait("vector", sTE, gate_te[e2][0])
                    em.add("vector", TT(s_o55[:, e2, 0:512], PG[ep][0][:], s_t25[:],
                                        ALU.mult), inc=(sDV, 1))
                    o5_dv[e2][0] = sDV.n
                    em.wait("vector", sTE, rb_te5[e2][1])
                    em.add("vector", STT(s_a5[:, 512:1024], PRB[:], c_onc[:, 0:1],
                                         s_oT5[:, e2, 512:1024], ALU.mult, ALU.mult),
                           inc=(sDV, 1))
                    a_dv[e2][1] = sDV.n
                    em.wait("scalar", sDV, a_dv[e2][1])
                    em.add("scalar", ACT(s_sg5[:, 512:1024], s_a5[:, 512:1024], AF.Exp,
                                         scale=-1.0))
                    em.add("scalar", ACT(s_sg5[:, 512:1024], s_sg5[:, 512:1024], AF.Ln,
                                         bias=c_one[:, 0:1]))
                    em.add("scalar", ACT(s_sg5[:, 512:1024], s_sg5[:, 512:1024], AF.Exp,
                                         scale=-1.0), inc=(sAC, 1))
                    sg1_ac = sAC.n
                    em.wait("vector", sAC, sg1_ac)
                    em.add("vector", TT(s_t25[:], s_a5[:, 512:1024], s_sg5[:, 512:1024],
                                        ALU.mult))
                    em.wait("vector", sTE, gate_te[e2][1])
                    em.add("vector", TT(s_o55[:, e2, 512:1024], PG[ep][1][:], s_t25[:],
                                        ALU.mult), inc=(sDV, 1))
                    o5_dv[e2][1] = sDV.n
            prev_gate_all = sTE.n
            prev_rbte = [rb_te5[i][1] for i in range(16)]
            prev_adv = [a_dv[i][1] for i in range(16)]
            em.wait("tensor", sDV, o5_dv[15][1])
            for q in range(4):
                qp = q % 2
                em.wait("tensor", dWoQ[qp], woq_ld[q])
                for tt in range(TPS5):
                    bank = PW[tt % 2]
                    em.wait("tensor", sDV, ev_dv5[tt % 2])
                    for d2 in range(16):
                        em.add("tensor", MM(bank[:], s_o55[:, d2, tt * 128:(tt + 1) * 128],
                                            s_wo5[qp][:, d2, :],
                                            start=(d2 == 0), stop=(d2 == 15)),
                               inc=(sTE, 1) if d2 == 15 else None)
                    rte = sTE.n
                    em.wait("vector", sTE, rte)
                    em.wait("vector", dSd[tt % 2], sd_dma[tt % 2])
                    em.add("vector", CP(s_ev5[tt % 2][:], bank[:]), inc=(sDV, 1))
                    ev_dv5[tt % 2] = sDV.n
                    em.wait("sync", sDV, ev_dv5[tt % 2])
                    sd_dma[tt % 2] = em.dma(
                        "sync", s_d[rows0 + tt * 128:rows0 + (tt + 1) * 128,
                                    q * 512:(q + 1) * 512],
                        s_ev5[tt % 2][:], dSd[tt % 2])
                wo_q_te[q] = sTE.n
                if q + 2 < 4:
                    em.wait("sync", sTE, wo_q_te[q])
                    woq_ld[q + 2] = em.dma("sync", s_wo5[q % 2][:],
                                           wo_bf[q + 2].rearrange("kb p c -> p kb c"),
                                           dWoQ[q % 2])
            prev_wo_q_te = list(wo_q_te)
        for sm_ in (dO5[0], dO5[1], dWg5[0], dWg5[1], dWoQ[0], dWoQ[1], dX5,
                    dSd[0], dSd[1]):
            em.wait("sync", sm_, sm_.n)
        em.wait("sync", sDV, sDV.n)
        em.wait("sync", sAC, sAC.n)
        em.wait("sync", sTE, sTE.n)
        em.play("p5")
        ph.close()

    if PHASES >= 6:
        ph = ExitStack()
        sbl = lambda name, shape, dt: ph.enter_context(nc.sbuf_tensor(name, shape, dt))
        psl = lambda name, shape, dt: ph.enter_context(nc.psum_tensor(name, shape, dt))
        SP6 = 1024
        NSP6 = TOK // SP6
        TPS6 = SP6 // 128
        PB = [psl(f"p6_B{i}", [128, 512], f) for i in range(8)]
        s_x6 = sbl("s_x6", [128, 16, SP6], bf)
        s_w6 = [sbl(f"s_w6_{i}", [128, 16, 512], bf) for i in range(2)]
        s_acc = sbl("s_acc", [128, TPS6 * 2048], f)
        s_cbT = sbl("s_cbT", [8, TPS6 * 128], f)
        c_fb6 = sbl("c_fb6", [8, 2048], f)
        s_sg6 = sbl("s_sg6", [128, 2048], f)
        s_xr6 = [sbl(f"s_xr6_{i}", [128, 2048], f) for i in range(2)]
        s_sr6 = [sbl(f"s_sr6_{i}", [128, 2048], f) for i in range(2)]
        s_ob6 = [sbl(f"s_ob6_{i}", [128, 2048], f) for i in range(2)]
        dX6 = em.sem("dX6")
        dW6 = [em.sem("dW6a"), em.sem("dW6b")]
        dRes = [em.sem("dResa"), em.sem("dResb")]
        dOut = [em.sem("dOuta"), em.sem("dOutb")]
        em.wait("sync", dCff, cast_all)
        fb_ld = em.dma("sync", c_fb6[:], fb, dX6)

        def accv(tt, q):
            return s_acc[:, tt * 2048 + q * 512:tt * 2048 + (q + 1) * 512]

        x6war_te = 0
        prev_ev_all = 0
        w6_ld = [0, 0]
        round_te = [0] * 32
        res_ld = [0, 0]
        out_dma6 = [0, 0]
        res_war = [0, 0]
        gtt_dv = 0
        for sp in range(NSP6):
            rows0 = sp * SP6
            i0 = sp * TPS6
            em.wait("sync", sTE, x6war_te)
            x6_ld = em.dma("sync", s_x6[:], xnT_d[:, :, rows0:rows0 + SP6], dX6)
            for r0 in range(2):
                e0, q0 = r0 // 4, r0 % 4
                w6_ld[r0 % 2] = em.dma("sync", s_w6[r0 % 2][:],
                                       wf_bf[e0, q0].rearrange("kb p c -> p kb c"),
                                       dW6[r0 % 2])
            em.wait("tensor", sDV, prev_ev_all)
            for tt in range(TPS6):
                em.add("tensor", TR(PB[tt][0:8, 0:128],
                                    p_comb[:, (i0 + tt) * 8:(i0 + tt + 1) * 8], c_idf[:]),
                       inc=(sTE, 1) if tt == TPS6 - 1 else None)
            cbt_te = sTE.n
            em.wait("vector", sTE, cbt_te)
            for tt in range(TPS6):
                em.add("vector", CP(s_cbT[:, tt * 128:(tt + 1) * 128], PB[tt][0:8, 0:128]),
                       inc=(sDV, 1) if tt == TPS6 - 1 else None)
            cbt_dv = sDV.n
            em.wait("tensor", sDV, cbt_dv)
            em.wait("tensor", dX6, fb_ld)
            seed_dv = 0
            for q in range(4):
                if seed_dv:
                    em.wait("tensor", sDV, seed_dv)
                for tt in range(TPS6):
                    em.add("tensor", MM(PB[tt][:], s_cbT[:, tt * 128:(tt + 1) * 128],
                                        c_fb6[:, q * 512:(q + 1) * 512]),
                           inc=(sTE, 1) if tt == TPS6 - 1 else None)
                seed_te = sTE.n
                em.wait("vector", sTE, seed_te)
                for tt in range(TPS6):
                    em.add("vector", CP(accv(tt, q), PB[tt][:]),
                           inc=(sDV, 1) if tt == TPS6 - 1 else None)
                seed_dv = sDV.n
            ev_half = [seed_dv, seed_dv]
            for e in range(E):
                for q in range(4):
                    r = e * 4 + q
                    wp = r % 2
                    em.wait("tensor", dW6[wp], w6_ld[wp])
                    em.wait("tensor", dX6, x6_ld)
                    for half in range(2):
                        em.wait("tensor", sDV, ev_half[half])
                        for kb in range(16):
                            for tt in range(4 * half, 4 * half + 4):
                                em.add("tensor",
                                       MM(PB[tt][:], s_x6[:, kb, tt * 128:(tt + 1) * 128],
                                          s_w6[wp][:, kb, :],
                                          start=(kb == 0), stop=(kb == 15)),
                                       inc=(sTE, 1) if (kb == 15 and tt == 4 * half + 3)
                                       else None)
                        hte = sTE.n
                        em.wait("vector", sTE, hte)
                        for tt in range(4 * half, 4 * half + 4):
                            em.add("vector",
                                   STT(accv(tt, q), PB[tt][:],
                                       p_comb[:, (i0 + tt) * 8 + e:(i0 + tt) * 8 + e + 1],
                                       accv(tt, q), ALU.mult, ALU.add),
                                   inc=(sDV, 1) if tt == 4 * half + 3 else None)
                        ev_half[half] = sDV.n
                    round_te[r] = sTE.n
                    if r + 2 < 32:
                        e2_, q2_ = (r + 2) // 4, (r + 2) % 4
                        em.wait("sync", sTE, round_te[r])
                        w6_ld[wp] = em.dma("sync", s_w6[wp][:],
                                           wf_bf[e2_, q2_].rearrange("kb p c -> p kb c"),
                                           dW6[wp])
            prev_ev_all = ev_half[1]
            x6war_te = round_te[31]
            prev_ob_dv = 0
            for tt in range(TPS6):
                par = tt % 2
                acct = s_acc[:, tt * 2048:(tt + 1) * 2048]
                em.wait("scalar", sDV, ev_half[1] if tt == 0 else prev_ob_dv)
                em.add("scalar", ACT(s_sg6[:], acct, AF.Exp, scale=-1.0))
                em.add("scalar", ACT(s_sg6[:], s_sg6[:], AF.Ln, bias=c_one[:, 0:1]))
                em.add("scalar", ACT(s_sg6[:], s_sg6[:], AF.Exp, scale=-1.0),
                       inc=(sAC, 1))
                sg_ac6 = sAC.n
                em.wait("sync", sDV, res_war[par])
                em.dma("sync", s_xr6[par][:], x_in[rows0 + tt * 128:rows0 + (tt + 1) * 128, :],
                       dRes[par])
                res_ld[par] = em.dma(
                    "sync", s_sr6[par][:],
                    s_d[rows0 + tt * 128:rows0 + (tt + 1) * 128, :], dRes[par])
                em.wait("vector", sAC, sg_ac6)
                em.add("vector", TT(s_sg6[:], acct, s_sg6[:], ALU.mult), inc=(sDV, 1))
                gtt_dv = sDV.n
                em.wait("vector", dRes[par], res_ld[par])
                em.wait("vector", dOut[par], out_dma6[par])
                em.add("vector", TT(s_ob6[par][:], s_sg6[:], s_sr6[par][:], ALU.mult))
                em.add("vector", TT(s_ob6[par][:], s_ob6[par][:], s_xr6[par][:], ALU.add),
                       inc=(sDV, 1))
                ob_dv = sDV.n
                res_war[par] = ob_dv
                prev_ob_dv = ob_dv
                em.wait("sync", sDV, ob_dv)
                out_dma6[par] = em.dma(
                    "sync", out_d[rows0 + tt * 128:rows0 + (tt + 1) * 128, :],
                    s_ob6[par][:], dOut[par])
        for sm_ in (dX6, dW6[0], dW6[1], dRes[0], dRes[1], dOut[0], dOut[1]):
            em.wait("sync", sm_, sm_.n)
        em.wait("sync", sDV, sDV.n)
        em.wait("sync", sAC, sAC.n)
        em.wait("sync", sTE, sTE.n)
        em.wait("sync", sGP, sGP.n)
        em.play("p6")
        ph.close()

    if PHASES < 6:
        ph = ExitStack()
        s_z = ph.enter_context(nc.sbuf_tensor("s_z", [128, D], F32))
        em.add("vector", MEMSET(s_z[:], 0.0), inc=(sDV, 1))
        em.wait("sync", sDV, sDV.n)
        for i in range(NT):
            em.dma("sync", out_d[i * 128:(i + 1) * 128, :], s_z[:], dS)
        em.wait("sync", dS, dS.n)
        em.play("stub")
        ph.close()

    stack.close()
    return nc


def kernel(**inputs):
    x = np.asarray(inputs["L0_attn_x"], np.float32)
    shared = host_prepare(inputs)
    nc = build_program()
    ncores = DEVCORES or NCORES
    in_maps = []
    for c in range(ncores):
        m = dict(shared)
        m["x"] = np.ascontiguousarray(x[c * BPC:(c + 1) * BPC].reshape(TOK, D))
        in_maps.append(m)
    res = run_bass_kernel_spmd(nc, in_maps, core_ids=list(range(ncores)))
    out = np.concatenate([np.asarray(r["out"]).reshape(BPC, T, D) for r in res.results], axis=0)
    if DEVCORES:
        out = np.concatenate([out] + [out[:BPC]] * (NCORES - DEVCORES), axis=0)
    if DEBUG:
        kernel.last_results = res.results
    return out


if __name__ == "__main__":
    nc = build_program()
    print("BUILD OK")


# version 05
# speedup vs baseline: 1.4523x; 1.0520x over previous
import os
import numpy as np
import ml_dtypes
import concourse.bass as bass
import concourse.mybir as mybir
from concourse.bass_utils import run_bass_kernel_spmd
from concourse.hw_specs import get_activation_tables
from contextlib import ExitStack

F32 = mybir.dt.float32
BF16 = mybir.dt.bfloat16
AF = mybir.ActivationFunctionType
ALU = mybir.AluOpType
AX = mybir.AxisListType.X

NCORES = 8
BPC = 2
T = 2048
D = 2048
H = 32
DH = 64
TOK = BPC * T
NT = TOK // 128
CH = 128
NCH = T // CH
E = 8

DEBUG = bool(os.environ.get("MK_DEBUG"))
DEVCORES = int(os.environ.get("MK_CORES", "0"))
PHASES = int(os.environ.get("MK_PHASES", "6"))


def _strip_order():
    sio = np.zeros(D, np.int64)
    for d in range(16):
        h0, h1 = 2 * d, 2 * d + 1
        rows = np.concatenate([
            h0 * 64 + np.arange(0, 64, 2), h1 * 64 + np.arange(0, 64, 2),
            h0 * 64 + 1 + np.arange(0, 64, 2), h1 * 64 + 1 + np.arange(0, 64, 2)])
        sio[d * 128:(d + 1) * 128] = rows
    return sio


def _yarn_tables():
    ROPE_BASE, SCALING, BETA_FAST, BETA_SLOW = 10000.0, 32.0, 1.0, 32.0
    half = (np.arange(0, DH, 2, dtype=np.float32) / DH).astype(np.float32)
    inv_freq = (1.0 / ROPE_BASE ** half).astype(np.float32)
    wavelen = (2.0 * np.pi / inv_freq).astype(np.float32)
    ramp = np.clip((wavelen - BETA_FAST) / (BETA_SLOW - BETA_FAST), 0.0, 1.0).astype(np.float32)
    scale = (1.0 + (SCALING - 1.0) * ramp).astype(np.float32)
    t = np.arange(T, dtype=np.float32)
    freqs = (t[:, None] / scale[None, :] * inv_freq[None, :]).astype(np.float32)
    idx = (2 * np.arange(DH // 2)) % (DH // 2)
    ctab = np.cos(freqs[:, idx]).T.astype(np.float32)
    stab = np.sin(freqs[:, idx]).T.astype(np.float32)
    return ctab, stab


def host_prepare(inp):
    f32 = np.float32
    bfl = ml_dtypes.bfloat16
    sio = _strip_order()
    Wq = np.asarray(inp["L0_attn_Wq"], f32)
    Wk = np.asarray(inp["L0_attn_Wk"], f32)
    Wv = np.asarray(inp["L0_attn_Wv"], f32)
    Wg = np.asarray(inp["L0_attn_Wg"], f32)
    Wo = np.asarray(inp["L0_attn_Wo"], f32)
    d = {}

    def blkT(W, order=None):
        WT = W.T if order is None else np.ascontiguousarray(W.T[:, order])
        return np.ascontiguousarray(WT.reshape(D, 16, 128).transpose(1, 0, 2))

    d["wqT"] = blkT(Wq, sio)
    d["wkT"] = blkT(Wk, sio)
    d["wvT"] = blkT(Wv)

    def blk512(WT):
        return np.ascontiguousarray(WT.reshape(16, 128, 4, 512).transpose(2, 0, 1, 3))

    d["wgT"] = blk512(np.ascontiguousarray(Wg.T))
    d["woT"] = blk512(np.ascontiguousarray(Wo.T))
    Wf = np.asarray(inp["L0_ffn_W"], f32)
    d["wffn"] = np.ascontiguousarray(Wf.reshape(8, 16, 128, 4, 512).transpose(0, 3, 1, 2, 4))

    def small(WT):
        return np.ascontiguousarray(np.asarray(WT, f32).reshape(16, 128, -1))

    d["wbT"] = small(np.asarray(inp["L0_attn_Wb"], f32).T)
    d["wgkT"] = small(np.asarray(inp["L0_attn_Wgk"], f32).T)
    d["wgf"] = small(np.asarray(inp["L0_ffn_Wg"], f32))
    d["fb"] = np.ascontiguousarray(np.asarray(inp["L0_ffn_b"], f32))
    for nm, order in (("q", sio), ("k", sio), ("v", None)):
        w = np.asarray(inp[f"L0_attn_w{nm}_conv"], f32)
        b = np.asarray(inp[f"L0_attn_b{nm}_conv"], f32)
        if order is not None:
            w, b = w[order], b[order]
        d[f"cw_{nm}"] = np.ascontiguousarray(w.reshape(16, 128, 4).transpose(1, 0, 2))
        d[f"cb_{nm}"] = np.ascontiguousarray(b.reshape(16, 128).T)

    def rep(v):
        v = np.asarray(v, f32).ravel()
        return np.ascontiguousarray(np.broadcast_to(v[None, :], (128, v.size)))

    d["bb_rep"] = rep(inp["L0_attn_bb"])
    d["bgk_rep"] = rep(inp["L0_attn_bgk"])
    d["dt_rep"] = rep(inp["L0_attn_dt_bias"])
    d["alog_rep"] = rep(inp["L0_attn_A_log"])
    d["dD_rep"] = rep(inp["L0_attn_D"])
    d["onorm_rep"] = np.ascontiguousarray(np.broadcast_to(
        np.tile(np.asarray(inp["L0_attn_onorm_w"], f32), H)[None, :], (128, D)))
    ctab, stab = _yarn_tables()
    d["cs_c"] = np.ascontiguousarray(np.tile(ctab, (4, 1)))
    d["cs_s"] = np.ascontiguousarray(np.tile(stab, (4, 1)))
    I = np.eye(128, dtype=f32)
    d["id_bf"] = I.astype(bfl)
    d["id_f32"] = I
    d["i4_bf"] = np.ascontiguousarray(np.tile(I, (1, 4)).astype(bfl))
    d["negi4_bf"] = np.ascontiguousarray(np.tile(-I, (1, 4)).astype(bfl))
    d["umask_bf"] = np.triu(np.full((128, 128), 3.3e38, f32)).astype(bfl)
    d["ones_f32"] = np.ones((1, 128), f32)
    obd = np.zeros((128, 2), f32)
    for r in range(128):
        obd[r, (r // 32) % 2] = 1.0
    d["obd_bf"] = obd.astype(bfl)
    d["obdT_bf"] = np.ascontiguousarray(obd.T.astype(bfl))
    obd64 = np.zeros((128, 2), f32)
    for r in range(128):
        obd64[r, r // 64] = 1.0
    d["obd64_bf"] = obd64.astype(bfl)
    d["obd64T_bf"] = np.ascontiguousarray(obd64.T.astype(bfl))
    d["onc"] = np.ascontiguousarray(
        (np.tile(np.asarray(inp["L0_attn_onorm_w"], f32), 2) * 8.0)[:, None])
    return d


class Sem:
    __slots__ = ("h", "n")

    def __init__(self, h):
        self.h = h
        self.n = 0


class Em:
    ENGS = ("sync", "tensor", "vector", "scalar", "gpsimd")

    def __init__(self, nc, stack):
        self.nc = nc
        self.stack = stack
        self.cmds = {e: [] for e in self.ENGS}
        self.sems = {}

    def sem(self, name):
        if name not in self.sems:
            self.sems[name] = Sem(self.stack.enter_context(self.nc.semaphore(name)))
        return self.sems[name]

    def add(self, eng, fn, inc=None):
        if inc is not None:
            sem, k = inc
            sem.n += k
            self.cmds[eng].append(lambda e, fn=fn, h=sem.h, k=k: fn(e).then_inc(h, k))
        else:
            self.cmds[eng].append(fn)

    def wait(self, eng, sem, val=None):
        v = sem.n if val is None else val
        if v > 0:
            self.cmds[eng].append(lambda e, h=sem.h, v=v: e.wait_ge(h, v))

    def relay(self, eng, waitsem, val, incsem):
        self.wait(eng, waitsem, val)
        incsem.n += 1
        self.cmds[eng].append(lambda e, h=incsem.h: e.sem_inc(h, 1))
        return incsem.n

    def dma(self, eng, out, in_, sem, transpose=False):
        if transpose:
            self.add(eng, lambda e, o=out, i=in_: e.dma_start_transpose(o, i), inc=(sem, 16))
        else:
            self.add(eng, lambda e, o=out, i=in_: e.dma_start(out=o, in_=i), inc=(sem, 16))
        return sem.n

    def play(self, name):
        with self.nc.Block(name) as block:
            for eng in self.ENGS:
                dec = getattr(block, eng)
                cl = self.cmds[eng]

                def body(e, cl=cl):
                    for fn in cl:
                        fn(e)
                dec(body)
        self.cmds = {e: [] for e in self.ENGS}


def MM(out, lhsT, rhs, start=True, stop=True, tile_position=None):
    return lambda e: e.matmul(out, lhsT=lhsT, rhs=rhs, start=start, stop=stop,
                              skip_group_check=True, tile_position=tile_position)


def TR(out, in_, ident):
    return lambda e: e.transpose(out, in_, ident)


def CP(out, in_):
    return lambda e: e.tensor_copy(out, in_)


def TT(out, a, b, op):
    return lambda e: e.tensor_tensor(out=out, in0=a, in1=b, op=op)


def TS(out, a, s1, s2, op0, op1=ALU.bypass):
    return lambda e: e.tensor_scalar(out=out, in0=a, scalar1=s1, scalar2=s2, op0=op0, op1=op1)


def STT(out, a, s, b, op0, op1):
    return lambda e: e.scalar_tensor_tensor(out=out, in0=a, scalar=s, in1=b, op0=op0, op1=op1)


def ACT(out, in_, func, bias=None, scale=None):
    kw = {}
    if bias is not None:
        kw["bias"] = bias
    if scale is not None:
        kw["scale"] = scale
    return lambda e: e.activation(out=out, in_=in_, func=func, **kw)


def MEMSET(ap, v):
    return lambda e: e.memset(ap, v)


def RED(out, in_, op, negate=False):
    return lambda e: e.tensor_reduce(out=out, in_=in_, axis=AX, op=op, negate=negate)


def SCAN(out, d0, d1, init, op0, op1):
    return lambda e: e.tensor_tensor_scan(out=out, data0=d0, data1=d1, initial=init,
                                          op0=op0, op1=op1)


def build_program():
    nc = bass.Bass("TRN2", target_bir_lowering=False)
    stack = ExitStack()
    em = Em(nc, stack)
    f, bf = F32, BF16

    din = lambda name, shape, dt: nc.dram_tensor(name, shape, dt, kind="ExternalInput").ap()
    dout_ = lambda name, shape, dt: nc.dram_tensor(name, shape, dt, kind="ExternalOutput").ap()
    dint = lambda name, shape, dt: nc.dram_tensor(
        name, shape, dt, kind=("ExternalOutput" if DEBUG else "Internal")).ap()

    x_in = din("x", [TOK, D], f)
    wqT = din("wqT", [16, D, 128], f)
    wkT = din("wkT", [16, D, 128], f)
    wvT = din("wvT", [16, D, 128], f)
    wgT = din("wgT", [4, 16, 128, 512], f)
    woT = din("woT", [4, 16, 128, 512], f)
    wffn = din("wffn", [8, 4, 16, 128, 512], f)
    wbT = din("wbT", [16, 128, 32], f)
    wgkT = din("wgkT", [16, 128, 32], f)
    wgf = din("wgf", [16, 128, 8], f)
    fb = din("fb", [8, 2048], f)
    cw = {p: din(f"cw_{p}", [128, 16, 4], f) for p in "qkv"}
    cb = {p: din(f"cb_{p}", [128, 16], f) for p in "qkv"}
    bb_rep = din("bb_rep", [128, 32], f)
    bgk_rep = din("bgk_rep", [128, 32], f)
    dt_rep = din("dt_rep", [128, 32], f)
    alog_rep = din("alog_rep", [128, 32], f)
    dD_rep = din("dD_rep", [128, 32], f)
    onorm_rep = din("onorm_rep", [128, D], f)
    cs_c = din("cs_c", [128, T], f)
    cs_s = din("cs_s", [128, T], f)
    id_bf = din("id_bf", [128, 128], bf)
    id_f32 = din("id_f32", [128, 128], f)
    i4_bf = din("i4_bf", [128, 512], bf)
    negi4_bf = din("negi4_bf", [128, 512], bf)
    umask_bf = din("umask_bf", [128, 128], bf)
    ones_f32 = din("ones_f32", [1, 128], f)
    obd_bf = din("obd_bf", [128, 2], bf)
    obdT_bf = din("obdT_bf", [2, 128], bf)
    obd64_bf = din("obd64_bf", [128, 2], bf)
    obd64T_bf = din("obd64T_bf", [2, 128], bf)
    onc = din("onc", [128, 1], f)

    out_d = dout_("out", [TOK, D], f)
    xnT_d = dint("xnT_d", [128, 16, TOK], bf)
    qT_d = dint("qT_d", [D, TOK], bf)
    kT_d = dint("kT_d", [D, TOK], bf)
    v_d = dint("v_d", [TOK, D], bf)
    o_d = dint("o_d", [TOK, D], bf)
    s_d = dint("s_d", [TOK, D], f)
    dbg_d = dint("dbg_d", [128, NT * 72], f)
    wq_bf = dint("wq_bf", [16, D, 128], bf)
    wk_bf = dint("wk_bf", [16, D, 128], bf)
    wv_bf = dint("wv_bf", [16, D, 128], bf)
    wg_bf = dint("wg_bf", [4, 16, 128, 512], bf)
    wo_bf = dint("wo_bf", [4, 16, 128, 512], bf)
    wf_bf = dint("wf_bf", [8, 4, 16, 128, 512], bf)
    bounce_cg = dint("bounce_cg", [2, 32, 128], f)
    bounce_bl = dint("bounce_bl", [2, 1, 32], f)

    dS = em.sem("dS")
    dG = em.sem("dG")
    dConst = em.sem("dConst")
    dX = [em.sem("dX0"), em.sem("dX1")]
    dSpX = [em.sem("dSpX0"), em.sem("dSpX1")]
    dCqkv = em.sem("dCqkv")
    dCgw = em.sem("dCgw")
    dCff = em.sem("dCff")
    dXns = em.sem("dXns")
    dW = [em.sem("dW0"), em.sem("dW1")]
    dFin = em.sem("dFin")
    dVsp = em.sem("dVsp")
    dT = [em.sem("dT0"), em.sem("dT1")]
    dPr = [em.sem("dPr0"), em.sem("dPr1")]
    dOsp = [em.sem("dOsp0"), em.sem("dOsp1")]
    sTE = em.sem("sTE")
    rlyA = em.sem("rlyA")
    rlyG = em.sem("rlyG")
    sDV = em.sem("sDV")
    sAC = em.sem("sAC")
    sGP = em.sem("sGP")

    sb = lambda name, shape, dt: stack.enter_context(nc.sbuf_tensor(name, shape, dt))
    stack2 = ExitStack()
    sb2 = lambda name, shape, dt: stack2.enter_context(nc.sbuf_tensor(name, shape, dt))
    c_dD = sb("c_dD", [128, 32], f)
    c_idbf = sb("c_idbf", [128, 128], bf)
    c_idf = sb("c_idf", [128, 128], f)
    c_i4 = sb("c_i4", [128, 512], bf)
    c_negi4 = sb("c_negi4", [128, 512], bf)
    c_umask = sb("c_umask", [128, 128], bf)
    c_ones = sb("c_ones", [1, 128], f)
    c_obd = sb("c_obd", [128, 2], bf)
    c_obdT = sb("c_obdT", [2, 128], bf)
    c_eps5 = sb("c_eps5", [128, 1], f)
    c_eps6 = sb("c_eps6", [128, 1], f)
    c_eps6r = sb("c_eps6r", [128, 1], f)
    c_one = sb("c_one", [128, 1], f)
    c_cw = {p: sb(f"c_cw_{p}", [128, 16, 4], f) for p in "qkv"}
    c_cb = {p: sb(f"c_cb_{p}", [128, 16], f) for p in "qkv"}
    p_comb = sb("p_comb", [128, NT * 8], f)
    c_obd64 = sb("c_obd64", [128, 2], bf)
    c_obd64T = sb("c_obd64T", [2, 128], bf)
    c_onc = sb("c_onc", [128, 1], f)
    p_beta = sb2("p_beta", [128, NT * 32], f)
    p_negb = sb2("p_negb", [128, NT * 32], f)
    p_g = sb2("p_g", [128, NT * 32], f)
    p_S = [[sb2(f"p_S_{b}_{pp}", [128, 16 * 64], f) for pp in range(2)] for b in range(BPC)]
    p_Sbf = [sb2(f"p_Sbf_{b}", [128, 16 * 64], bf) for b in range(BPC)]
    c_csc = sb2("c_csc", [128, T], f)
    c_css = sb2("c_css", [128, T], f)

    names = list(get_activation_tables(nc.m.arch).keys())
    set_id = names.index("natural_log_exp_and_others")

    ph = ExitStack()
    sbl = lambda name, shape, dt: ph.enter_context(nc.sbuf_tensor(name, shape, dt))
    psl = lambda name, shape, dt: ph.enter_context(nc.psum_tensor(name, shape, dt))

    c_wbT = sbl("c_wbT", [128, 16, 32], f)
    c_wgkT = sbl("c_wgkT", [128, 16, 32], f)
    c_wgf = sbl("c_wgf", [128, 16, 8], f)
    c_bb = sbl("c_bb", [128, 32], f)
    c_bgkdt = sbl("c_bgkdt", [128, 32], f)
    c_nexpA = sbl("c_nexpA", [128, 32], f)
    s_bgk = sbl("s_bgk", [128, 32], f)
    s_dt = sbl("s_dt", [128, 32], f)
    s_alog = sbl("s_alog", [128, 32], f)
    s_x = [sbl(f"s_x{i}", [128, D], f) for i in range(2)]

    x_ld = [0] * NT
    x_ld[0] = em.dma("sync", s_x[0][:], x_in[0:128, :], dX[0])
    x_ld[1] = em.dma("sync", s_x[1][:], x_in[128:256, :], dX[1])
    for dst, src in [(c_idbf[:], id_bf), (c_idf[:], id_f32),
                     (c_wbT[:], wbT.rearrange("kb p n -> p kb n")),
                     (c_wgkT[:], wgkT.rearrange("kb p n -> p kb n")),
                     (c_wgf[:], wgf.rearrange("kb p n -> p kb n")),
                     (c_bb[:], bb_rep), (c_dD[:], dD_rep),
                     (c_i4[:], i4_bf),
                     (c_negi4[:], negi4_bf), (c_umask[:], umask_bf), (c_ones[:], ones_f32),
                     (c_obd[:], obd_bf), (c_obdT[:], obdT_bf),
                     (c_obd64[:], obd64_bf), (c_obd64T[:], obd64T_bf), (c_onc[:], onc),
                     (s_bgk[:], bgk_rep), (s_dt[:], dt_rep), (s_alog[:], alog_rep),
                     (c_csc[:], cs_c), (c_css[:], cs_s)]:
        em.dma("sync", dst, src, dConst)
    for p in "qkv":
        em.dma("sync", c_cw[p][:], cw[p], dConst)
        em.dma("sync", c_cb[p][:], cb[p], dConst)
    n_const = dConst.n

    for dst, src in [(wq_bf, wqT), (wk_bf, wkT), (wv_bf, wvT)]:
        em.dma("gpsimd", dst, src, dCqkv)
    cast_qkv = dCqkv.n
    cast_gwo = cast_all = 0

    def _preload(e, sid=set_id):
        return e.add_instruction(mybir.InstLoadActFuncSet(
            name=nc.get_next_instruction_name(), act_func_set_id=sid, ins=[], outs=[]))
    em.add("scalar", _preload)

    em.add("vector", MEMSET(c_eps5[:], 1e-5))
    em.add("vector", MEMSET(c_eps6[:], 1e-6))
    em.add("vector", MEMSET(c_eps6r[:], 64e-6))
    em.add("vector", MEMSET(c_one[:], 1.0), inc=(sDV, 1))
    em.wait("vector", dConst, n_const)
    em.add("vector", TT(c_bgkdt[:], s_bgk[:], s_dt[:], ALU.add), inc=(sDV, 1))
    bgkdt_done = sDV.n
    em.wait("scalar", dConst, n_const)
    em.add("scalar", ACT(c_nexpA[:], s_alog[:], AF.Exp))


    s_xnbf = [sbl(f"s_xnbf{i}", [128, D], bf) for i in range(2)]
    s_xnf = [sbl(f"s_xnf{i}", [128, D], f) for i in range(2)]
    s_xTf = [sbl(f"s_xTf{i}", [128, 16 * 128], f) for i in range(2)]
    s_hi = [sbl(f"s_hi{i}", [128, 16, 1024], bf) for i in range(2)]
    s_bnst = [sbl(f"s_bnst{i}", [128, 24], f) for i in range(2)]
    s_mv = [sbl(f"s_mv{i}", [128, 2], f) for i in range(2)]
    s_rstd = [sbl(f"s_rstd{i}", [128, 1], f) for i in range(2)]
    s_mb = [sbl(f"s_mb{i}", [128, 1], f) for i in range(2)]
    s_sm = sbl("s_sm", [128, 8], f)
    s_e8 = sbl("s_e8", [128, 8], f)
    s_msk = sbl("s_msk", [128, 8], f)
    s_t32 = sbl("s_t32", [128, 32], f)
    s_t32b = sbl("s_t32b", [128, 32], f)
    s_t32c = sbl("s_t32c", [128, 32], f)
    ps_tbf = [psl(f"ps_tbf{i}", [128, 512], bf) for i in range(2)]
    ps_tf = [psl(f"ps_tf{i}", [128, 512], f) for i in range(2)]
    ps_small = psl("ps_small", [128, 512], f)

    xn_done = [0] * NT
    tile_ev = [0] * NT
    small_done = [0] * NT
    epi_done = [0] * NT
    spill = {}

    for i in range(NT):
        par = i % 2
        seg = i // 8
        if i >= 2:
            em.wait("sync", sAC, xn_done[i - 2])
            x_ld[i] = em.dma("sync", s_x[par][:], x_in[i * 128:(i + 1) * 128, :], dX[par])
        em.wait("vector", dX[par], x_ld[i])
        for j in range(4):
            em.add("vector", lambda e, j=j, par=par: e.bn_stats(
                s_bnst[par][:, j * 6:(j + 1) * 6], s_x[par][:, j * 512:(j + 1) * 512]),
                   inc=(sDV, 1) if j == 3 else None)
        stats_dv = sDV.n
        rv = em.relay("scalar", sDV, stats_dv, rlyA)
        em.wait("vector", rlyA, rv)
        em.add("vector", lambda e, par=par: e.bn_aggr(s_mv[par][:], s_bnst[par][:]),
               inc=(sDV, 1))
        bn_done = sDV.n
        em.wait("scalar", sDV, bn_done)
        em.add("scalar", ACT(s_rstd[par][:], s_mv[par][:, 1:2], AF.Ln, bias=c_eps5[:, 0:1]))
        em.add("scalar", lambda e, par=par: e.activation(
            out=s_rstd[par][:], in_=s_rstd[par][:], func=AF.Exp, scale=-0.5), inc=(sAC, 1))
        rstd_done = sAC.n
        em.wait("vector", sAC, rstd_done)
        em.add("vector", lambda e, par=par: e.scalar_tensor_tensor(
            out=s_mb[par][:], in0=s_mv[par][:, 0:1], scalar=-1.0, in1=s_rstd[par][:],
            op0=ALU.mult, op1=ALU.mult), inc=(sDV, 1))
        mb_done = sDV.n
        em.wait("scalar", sDV, mb_done)
        if i >= 2:
            em.wait("scalar", sTE, small_done[i - 2])
        em.add("scalar", lambda e, par=par: e.activation(
            out=s_xnbf[par][:], in_=s_x[par][:], func=AF.Identity,
            bias=s_mb[par][:, 0:1], scale=s_rstd[par][:, 0:1]))
        em.add("scalar", lambda e, par=par: e.activation(
            out=s_xnf[par][:], in_=s_x[par][:], func=AF.Identity,
            bias=s_mb[par][:, 0:1], scale=s_rstd[par][:, 0:1]), inc=(sAC, 1))
        xn_done[i] = sAC.n
        em.wait("tensor", sAC, xn_done[i])
        if i >= 1:
            em.wait("tensor", sDV, tile_ev[i - 1])
        segcol = (i % 8) * 128
        if i % 8 == 0 and seg >= 2:
            em.wait("vector", dSpX[seg % 2], spill[seg - 2])
        if i >= 2:
            em.wait("vector", sTE, small_done[i - 2])
        for half in range(2):
            for j8 in range(8):
                j = half * 8 + j8
                bank = ps_tbf[j8 // 4]
                em.add("tensor", TR(bank[:, (j8 % 4) * 128:(j8 % 4 + 1) * 128],
                                    s_xnbf[par][:, j * 128:(j + 1) * 128], c_idbf[:]),
                       inc=(sTE, 1) if j8 == 7 else None)
            tb = sTE.n
            for j8 in range(8):
                j = half * 8 + j8
                bank = ps_tf[j8 // 4]
                em.add("tensor", TR(bank[:, (j8 % 4) * 128:(j8 % 4 + 1) * 128],
                                    s_xnf[par][:, j * 128:(j + 1) * 128], c_idf[:]),
                       inc=(sTE, 1) if j8 == 7 else None)
            tf_ = sTE.n
            em.wait("vector", sTE, tb)
            for q in range(2):
                j0 = half * 8 + q * 4
                em.add("vector", lambda e, q=q, j0=j0, seg=seg, segcol=segcol: e.tensor_copy(
                    s_hi[seg % 2][:, j0:j0 + 4, segcol:segcol + 128], ps_tbf[q][:]))
            em.wait("vector", sTE, tf_)
            for q in range(2):
                j0 = half * 8 + q * 4
                em.add("vector", lambda e, q=q, j0=j0, par=par: e.tensor_copy(
                    s_xTf[par][:, j0 * 128:(j0 + 4) * 128], ps_tf[q][:]),
                       inc=(sDV, 1) if q == 1 else None)
            if half == 0:
                em.wait("tensor", sDV, sDV.n)
        tile_ev[i] = sDV.n
        if i % 8 == 7:
            em.wait("sync", sDV, tile_ev[i])
            spill[seg] = em.dma("sync", xnT_d[:, :, seg * 1024:(seg + 1) * 1024],
                                s_hi[seg % 2][:], dSpX[seg % 2])
        em.wait("tensor", sDV, tile_ev[i])
        if i >= 1:
            em.wait("tensor", sDV, epi_done[i - 1])
        for j in range(16):
            em.add("tensor", MM(ps_small[:, 0:32], s_xTf[par][:, j * 128:(j + 1) * 128],
                                c_wbT[:, j, :], start=(j == 0), stop=(j == 15)))
        for j in range(16):
            em.add("tensor", MM(ps_small[:, 32:64], s_xTf[par][:, j * 128:(j + 1) * 128],
                                c_wgkT[:, j, :], start=False, stop=(j == 15)))
        for j in range(16):
            em.add("tensor", MM(ps_small[:, 64:72], s_xTf[par][:, j * 128:(j + 1) * 128],
                                c_wgf[:, j, :], start=False, stop=(j == 15)),
                   inc=(sTE, 1) if j == 15 else None)
        small_done[i] = sTE.n
        em.wait("vector", sTE, small_done[i])
        em.wait("vector", sDV, bgkdt_done)
        em.add("vector", TT(s_t32[:], ps_small[:, 0:32], c_bb[:], ALU.add))
        em.add("vector", TT(s_t32b[:], ps_small[:, 32:64], c_bgkdt[:], ALU.add),
               inc=(sDV, 1))
        raw_done = sDV.n
        em.wait("scalar", sDV, raw_done)
        em.add("scalar", ACT(s_t32c[:], s_t32[:], AF.Exp, scale=-1.0))
        em.add("scalar", ACT(s_t32c[:], s_t32c[:], AF.Ln, bias=c_one[:, 0:1]))
        em.add("scalar", ACT(p_beta[:, i * 32:(i + 1) * 32], s_t32c[:], AF.Exp, scale=-1.0),
               inc=(sAC, 1))
        beta_done = sAC.n
        em.add("scalar", ACT(s_t32[:], s_t32b[:], AF.Abs))
        em.add("scalar", ACT(s_t32[:], s_t32[:], AF.Exp, scale=-1.0))
        em.add("scalar", ACT(s_t32[:], s_t32[:], AF.Ln, bias=c_one[:, 0:1]), inc=(sAC, 1))
        spl_done = sAC.n
        em.wait("vector", sAC, spl_done)
        em.add("vector", TS(s_t32c[:], s_t32b[:], 0.0, 0.0, ALU.max))
        em.add("vector", RED(s_sm[:, 0:1], ps_small[:, 64:72], ALU.max, negate=True))
        em.add("vector", TS(p_negb[:, i * 32:(i + 1) * 32], p_beta[:, i * 32:(i + 1) * 32],
                            -1.0, 0.0, ALU.mult), inc=(sDV, 1))
        red_dv = sDV.n
        rv = em.relay("scalar", sDV, red_dv, rlyA)
        em.wait("vector", rlyA, rv)
        em.add("vector", TT(p_g[:, i * 32:(i + 1) * 32], s_t32[:], s_t32c[:], ALU.add))
        em.add("vector", TS(s_msk[:], ps_small[:, 64:72], s_sm[:, 0:1], 0.0,
                            ALU.add, ALU.is_ge), inc=(sDV, 1))
        nm_done = sDV.n
        em.wait("scalar", sDV, nm_done)
        em.add("scalar", ACT(s_e8[:], ps_small[:, 64:72], AF.Exp, bias=s_sm[:, 0:1]),
               inc=(sAC, 1))
        e8_done = sAC.n
        em.wait("vector", sAC, e8_done)
        em.add("vector", RED(s_sm[:, 1:2], s_e8[:], ALU.add), inc=(sDV, 1))
        ssum_done = sDV.n
        em.wait("scalar", sDV, ssum_done)
        em.add("scalar", ACT(s_sm[:, 2:3], s_sm[:, 1:2], AF.Ln))
        em.add("scalar", ACT(s_sm[:, 3:4], s_sm[:, 2:3], AF.Exp, scale=-1.0), inc=(sAC, 1))
        rs_done = sAC.n
        em.wait("vector", sAC, rs_done)
        em.add("vector", STT(p_comb[:, i * 8:(i + 1) * 8], s_e8[:], s_sm[:, 3:4], s_msk[:],
                             ALU.mult, ALU.mult), inc=(sDV, 1))
        epi_done[i] = sDV.n

    em.add("scalar", ACT(c_nexpA[:], s_alog[:], AF.Exp))
    em.add("scalar", lambda e: e.activation(out=c_nexpA[:], in_=c_nexpA[:],
                                            func=AF.Identity, scale=-1.0), inc=(sAC, 1))
    nexpA_done = sAC.n
    em.wait("vector", sAC, nexpA_done)
    for i in range(NT):
        em.add("vector", TT(p_g[:, i * 32:(i + 1) * 32], p_g[:, i * 32:(i + 1) * 32],
                            c_nexpA[:], ALU.mult), inc=(sDV, 1) if i == NT - 1 else None)
    if DEBUG:
        em.wait("sync", sDV, sDV.n)
        em.wait("sync", sAC, sAC.n)
        for nm, src, w in (("beta", p_beta, 32), ("g", p_g, 32), ("comb", p_comb, 8)):
            off = {"beta": 0, "g": NT * 32, "comb": NT * 64}[nm]
            em.dma("sync", dbg_d[:, off:off + NT * w], src[:], dS)
    em.wait("sync", dS, dS.n)
    for sm_ in (dConst, dX[0], dX[1], dSpX[0], dSpX[1]):
        em.wait("sync", sm_, sm_.n)
    em.wait("sync", sDV, sDV.n)
    em.wait("sync", sAC, sAC.n)
    em.wait("sync", sTE, sTE.n)
    em.play("p1")
    ph.close()

    if PHASES >= 3:
        ph = ExitStack()
        sbl = lambda name, shape, dt: ph.enter_context(nc.sbuf_tensor(name, shape, dt))
        psl = lambda name, shape, dt: ph.enter_context(nc.psum_tensor(name, shape, dt))
        HT = T // 2
        s_xns = sbl("s_xns", [128, 16, HT], bf)
        s_wt = [sbl(f"s_wt{i}", [128, 16, 128], bf) for i in range(2)]
        s_strip = [sbl(f"s_strip{i}", [128, 3 + HT], f) for i in range(2)]
        s_hist = sbl("s_hist", [128, 48, 3], f)
        s_conv = [[sbl(f"s_conv{i}_{j}", [128, HT], f) for j in range(2)] for i in range(2)]
        s_act = [sbl(f"s_act{i}", [128, HT], bf) for i in range(2)]
        s_u = [sbl(f"s_u{i}", [128, HT], f) for i in range(2)]
        s_v = [sbl(f"s_v{i}", [128, HT], f) for i in range(2)]
        s_rope = [sbl(f"s_rope{i}", [128, HT], bf) for i in range(2)]
        s_sq = [sbl(f"s_sq{i}", [128, HT], bf) for i in range(2)]
        s_rsq = sbl("s_rsq", [2, 512], f)
        s_rsqb = [sbl(f"s_rsqb{i}", [2, HT], bf) for i in range(2)]
        s_fin = [sbl(f"s_fin{i}", [128, HT], bf) for i in range(2)]
        s_vt = [sbl(f"s_vt{i}", [128, 8, 128], bf) for i in range(2)]
        ps_pr = [psl(f"ps_pr{i}", [128, 512], f) for i in range(4)]
        ps_ss = psl("ps_ss", [128, 512], f)
        ps_rb = [psl(f"ps_rb{i}", [128, 512], f) for i in range(2)]
        ps_vt = psl("ps_vt", [128, 512], bf)
        dFin2 = [dFin, em.sem("dFin2")]
        dVsp2 = [dVsp, em.sem("dVsp2")]

        em.wait("gpsimd", dCqkv, cast_qkv)
        em.wait("sync", dCqkv, cast_qkv)
        strips = []
        for sq_ in range(2):
            for hf in range(2):
                for ip, p in enumerate("qkv"):
                    for d2 in range(16):
                        strips.append((sq_, hf, p, ip * 16 + d2, d2))
        NS3 = len(strips)
        proj_te = [0] * NS3
        evict_dv = [0] * NS3
        sg_ac = [0] * NS3
        act_gp = [0] * NS3
        sq_ac3 = [0] * NS3
        cte = [0] * NS3
        rsqb_ac = [0] * NS3
        fin_dv = [0] * NS3
        w_ld3 = [0, 0]
        seq_ld = 0
        fin_dma = [0, 0]
        v_dma = [0, 0]
        last_ss_ac = 0
        last_vt_dv = 0
        last_rb_fin = 0
        for it in range(NS3 + 3):
            if it < NS3:
                i = it
                sq_, hf, p, hidx, d2 = strips[i]
                par = i % 2
                col0 = sq_ * T + hf * HT
                if i % 48 == 0:
                    if i:
                        em.wait("sync", sTE, proj_te[i - 1])
                    for qq in range(4):
                        seq_ld = em.dma("sync", s_xns[:, qq * 4:(qq + 1) * 4, :],
                                        xnT_d[:, qq * 4:(qq + 1) * 4, col0:col0 + HT],
                                        dXns)
                wbf = {"q": wq_bf, "k": wk_bf, "v": wv_bf}[p]
                if i >= 2:
                    em.wait("sync", sTE, proj_te[i - 2])
                w_ld3[par] = em.dma("sync", s_wt[par][:],
                                    wbf[d2].rearrange("(kb p) m -> p kb m", kb=16, p=128),
                                    dW[par])
                em.wait("tensor", dW[par], w_ld3[par])
                em.wait("tensor", dXns, seq_ld)
                if i >= 2:
                    em.wait("tensor", sDV, evict_dv[i - 2])
                for tn in range(2):
                    bank = ps_pr[2 * par + tn]
                    for kb in range(16):
                        em.add("tensor", MM(bank[:], s_wt[par][:, kb, :],
                                            s_xns[:, kb, tn * 512:(tn + 1) * 512],
                                            start=(kb == 0), stop=(kb == 15)),
                               inc=(sTE, 1) if (tn == 1 and kb == 15) else None)
                proj_te[i] = sTE.n
            if it >= 2 and it - 2 < NS3:
                i = it - 2
                sq_, hf, p, hidx, d2 = strips[i]
                par = i % 2
                if p in "qk":
                    for tn in range(2):
                        em.wait("tensor", sAC, last_ss_ac)
                        if tn == 0:
                            em.wait("tensor", sAC, sq_ac3[i])
                        em.add("tensor", MM(ps_ss[0:2, 0:512], c_obd[:],
                                            s_sq[par][:, tn * 512:(tn + 1) * 512]),
                               inc=(sTE, 1))
                        te3 = sTE.n
                        em.wait("scalar", sTE, te3)
                        em.add("scalar", ACT(s_rsq[:], ps_ss[0:2, 0:512], AF.Ln,
                                             bias=c_eps6[0:2, 0:1]))
                        em.add("scalar", lambda e, par=par, tn=tn: e.activation(
                            out=s_rsqb[par][:, tn * 512:(tn + 1) * 512], in_=s_rsq[:],
                            func=AF.Exp, scale=-0.5), inc=(sAC, 1))
                        last_ss_ac = sAC.n
                    rsqb_ac[i] = sAC.n
                    cte[i] = sTE.n
                else:
                    em.wait("tensor", sGP, act_gp[i])
                    for rnd in range(2):
                        em.wait("tensor", sDV, last_vt_dv)
                        for j4 in range(4):
                            j8 = rnd * 4 + j4
                            em.add("tensor", TR(ps_vt[:, j4 * 128:(j4 + 1) * 128],
                                                s_act[par][:, j8 * 128:(j8 + 1) * 128],
                                                c_idbf[:]),
                                   inc=(sTE, 1) if j4 == 3 else None)
                        te3 = sTE.n
                        em.wait("vector", sTE, te3)
                        if rnd == 0:
                            em.wait("vector", dVsp2[par], v_dma[par])
                        em.add("vector", lambda e, par=par, rnd=rnd: e.tensor_copy(
                            s_vt[par][:, rnd * 4:rnd * 4 + 4, :], ps_vt[:]),
                               inc=(sDV, 1))
                        last_vt_dv = sDV.n
                    cte[i] = sTE.n
                    em.wait("gpsimd", sDV, last_vt_dv)
                    v_dma[par] = em.dma(
                        "gpsimd",
                        v_d.rearrange("(u j p) c -> u p j c", u=4, j=8, p=128)[
                            sq_ * 2 + hf, :, :, d2 * 128:(d2 + 1) * 128],
                        s_vt[par][:], dVsp2[par])
            if it >= 3 and it - 3 < NS3:
                i = it - 3
                sq_, hf, p, hidx, d2 = strips[i]
                par = i % 2
                if p in "qk":
                    col0 = sq_ * T + hf * HT
                    em.wait("tensor", sAC, rsqb_ac[i])
                    em.wait("tensor", sDV, last_rb_fin)
                    for tn in range(2):
                        em.add("tensor", MM(ps_rb[tn][:], c_obdT[:],
                                            s_rsqb[par][:, tn * 512:(tn + 1) * 512]),
                               inc=(sTE, 1) if tn == 1 else None)
                    rbte = sTE.n
                    em.wait("vector", sTE, rbte)
                    em.wait("vector", dFin2[par], fin_dma[par])
                    for tn in range(2):
                        em.add("vector", STT(s_fin[par][:, tn * 512:(tn + 1) * 512],
                                             ps_rb[tn][:], 1.0,
                                             s_rope[par][:, tn * 512:(tn + 1) * 512],
                                             ALU.mult, ALU.mult),
                               inc=(sDV, 1) if tn == 1 else None)
                    fin_dv[i] = sDV.n
                    last_rb_fin = fin_dv[i]
                    em.wait("sync", sDV, fin_dv[i])
                    tgt = (qT_d if p == "q" else kT_d)
                    dst = tgt.rearrange("(d b a r) t -> d a b r t", d=16, b=2, a=2, r=32)
                    em.dma("sync", dst[d2, 0, :, :, col0:col0 + HT],
                           s_fin[par][0:64, :], dFin2[par])
                    fin_dma[par] = em.dma("sync", dst[d2, 1, :, :, col0:col0 + HT],
                                          s_fin[par][64:128, :], dFin2[par])
            if it >= 1 and it - 1 < NS3:
                i = it - 1
                sq_, hf, p, hidx, d2 = strips[i]
                par = i % 2
                cwp, cbp = c_cw[p], c_cb[p]
                em.wait("vector", sTE, proj_te[i])
                if i >= 2:
                    em.wait("vector", sAC, sg_ac[i - 2])
                if hf == 0:
                    em.add("vector", MEMSET(s_strip[par][:, 0:3], 0.0))
                else:
                    em.add("vector", CP(s_strip[par][:, 0:3], s_hist[:, hidx, :]))
                for tn in range(2):
                    em.add("vector", CP(s_strip[par][:, 3 + tn * 512:3 + (tn + 1) * 512],
                                        ps_pr[2 * par + tn][:]),
                           inc=(sDV, 1) if tn == 1 else None)
                evict_dv[i] = sDV.n
                rv = em.relay("scalar", sDV, evict_dv[i], rlyA)
                em.wait("vector", rlyA, rv)
                em.add("vector", TS(s_conv[par][0][:], s_strip[par][:, 0:HT],
                                    cwp[:, d2, 0:1], cbp[:, d2:d2 + 1], ALU.mult, ALU.add))
                em.add("vector", STT(s_conv[par][1][:], s_strip[par][:, 1:1 + HT],
                                     cwp[:, d2, 1:2], s_conv[par][0][:],
                                     ALU.mult, ALU.add))
                em.add("vector", STT(s_conv[par][0][:], s_strip[par][:, 2:2 + HT],
                                     cwp[:, d2, 2:3], s_conv[par][1][:],
                                     ALU.mult, ALU.add))
                em.add("vector", STT(s_conv[par][1][:], s_strip[par][:, 3:3 + HT],
                                     cwp[:, d2, 3:4], s_conv[par][0][:],
                                     ALU.mult, ALU.add), inc=(sDV, 1))
                conv_done = sDV.n
                if hf == 0:
                    em.add("vector", CP(s_hist[:, hidx, :], s_strip[par][:, HT:HT + 3]))
                em.wait("scalar", sDV, conv_done)
                em.add("scalar", ACT(s_conv[par][0][:], s_conv[par][1][:], AF.Exp,
                                     scale=-1.0))
                em.add("scalar", ACT(s_strip[par][:, 0:HT], s_conv[par][0][:], AF.Ln,
                                     bias=c_one[:, 0:1]))
                em.add("scalar", ACT(s_conv[par][0][:], s_strip[par][:, 0:HT], AF.Exp,
                                     scale=-1.0), inc=(sAC, 1))
                sg_ac[i] = sAC.n
                em.wait("gpsimd", sAC, sg_ac[i])
                if i >= 2:
                    em.wait("gpsimd", sTE, cte[i - 2])
                em.add("gpsimd", TT(s_act[par][:], s_conv[par][1][:], s_conv[par][0][:],
                                    ALU.mult), inc=(sGP, 1))
                act_gp[i] = sGP.n
                if p in "qk":
                    csl = slice(hf * HT, (hf + 1) * HT)
                    em.add("gpsimd", TT(s_u[par][:], s_act[par][:], c_csc[:, csl],
                                        ALU.mult))
                    em.add("gpsimd", TT(s_v[par][0:64, :], s_act[par][64:128, :],
                                        c_css[64:128, csl], ALU.mult))
                    em.add("gpsimd", TT(s_v[par][64:128, :], s_act[par][0:64, :],
                                        c_css[0:64, csl], ALU.mult), inc=(sGP, 1))
                    uv_gp = sGP.n
                    em.wait("vector", sGP, uv_gp)
                    if i >= 2:
                        em.wait("vector", sDV, fin_dv[i - 2])
                    em.add("vector", TT(s_rope[par][0:64, :], s_u[par][0:64, :],
                                        s_v[par][0:64, :], ALU.subtract))
                    em.add("vector", TT(s_rope[par][64:128, :], s_u[par][64:128, :],
                                        s_v[par][64:128, :], ALU.add), inc=(sDV, 1))
                    rope_done = sDV.n
                    em.wait("scalar", sDV, rope_done)
                    if i >= 2:
                        em.wait("scalar", sTE, cte[i - 2])
                    em.add("scalar", ACT(s_sq[par][:], s_rope[par][:], AF.Square),
                           inc=(sAC, 1))
                    sq_ac3[i] = sAC.n
        for sm_ in (dXns, dW[0], dW[1], dFin, dFin2[1]):
            em.wait("sync", sm_, sm_.n)
        em.wait("sync", sDV, sDV.n)
        em.wait("sync", sAC, sAC.n)
        em.wait("sync", sGP, sGP.n)
        em.wait("gpsimd", dVsp, dVsp.n)
        em.wait("gpsimd", dVsp2[1], dVsp2[1].n)
        em.play("p3")
        ph.close()

    if PHASES >= 4:
        ph = ExitStack()
        sbl = lambda name, shape, dt: ph.enter_context(nc.sbuf_tensor(name, shape, dt))
        psl = lambda name, shape, dt: ph.enter_context(nc.psum_tensor(name, shape, dt))
        B0 = psl("p4_B0", [128, 512], f)
        B1 = psl("p4_B1", [128, 512], f)
        B2 = psl("p4_B2", [128, 512], f)
        B3 = psl("p4_B3", [128, 512], bf)
        B4 = psl("p4_B4", [128, 512], f)
        B5 = psl("p4_B5", [128, 512], f)
        B6 = psl("p4_B6", [128, 512], f)
        B7 = psl("p4_B7", [128, 512], f)
        s_qc = [sbl(f"s_qc{i}", [128, 16, 128], bf) for i in range(2)]
        s_kc = [sbl(f"s_kc{i}", [128, 16, 128], bf) for i in range(2)]
        s_kt = [sbl(f"s_kt{i}", [128, 2048], bf) for i in range(2)]
        s_vk = [sbl(f"s_vk{i}", [128, 2048], bf) for i in range(2)]
        s_oa = [sbl(f"s_oa{i}", [128, 2048], bf) for i in range(2)]
        s_gT = [sbl(f"s_gT{i}", [32, 128], f) for i in range(2)]
        s_cgT = [sbl(f"s_cgT{i}", [32, 128], f) for i in range(2)]
        s_cgtok = [sbl(f"s_cgtok{i}", [128, 32], f) for i in range(2)]
        s_negcg = [sbl(f"s_negcg{i}", [128, 32], f) for i in range(2)]
        s_ecg = [sbl(f"s_ecg{i}", [128, 32], f) for i in range(2)]
        s_cgf = [sbl(f"s_cgf{i}", [1, 4096], f) for i in range(2)]
        s_blr = [sbl(f"s_blr{i}", [1, 32], f) for i in range(2)]
        s_BL = [sbl(f"s_BL{i}", [128, 32], f) for i in range(2)]
        s_kkr = [sbl(f"s_kkr{i}", [128, 512], bf) for i in range(2)]
        s_qkr = [sbl(f"s_qkr{i}", [128, 512], bf) for i in range(2)]
        s_dtr = [sbl(f"s_dtr{i}", [128, 512], bf) for i in range(2)]
        s_dt4 = [sbl(f"s_dt4{i}", [128, 512], bf) for i in range(2)]
        s_D4 = [sbl(f"s_D4{i}", [128, 512], bf) for i in range(2)]
        s_bn4 = [sbl(f"s_bn4{i}", [128, 512], bf) for i in range(2)]
        s_bt4 = [sbl(f"s_bt4{i}", [128, 512], bf) for i in range(2)]
        s_pw = [[sbl(f"s_pw{i}_{j}", [128, 512], bf) for j in range(2)] for i in range(2)]
        s_tp = [[sbl(f"s_tp{i}_{j}", [128, 512], bf) for j in range(2)] for i in range(2)]
        s_hh = [[sbl(f"s_hh{i}_{j}", [128, 512], bf) for j in range(2)] for i in range(2)]
        s_U4 = [sbl(f"s_U4{i}", [128, 256], bf) for i in range(2)]
        s_Ub4 = [sbl(f"s_Ub4{i}", [128, 256], bf) for i in range(2)]
        s_R4 = [sbl(f"s_R4{i}", [128, 256], bf) for i in range(2)]
        s_r14 = [sbl(f"s_r14{i}", [128, 256], f) for i in range(2)]
        s_o14 = [sbl(f"s_o14{i}", [128, 256], f) for i in range(2)]
        s_w4 = [sbl(f"s_w4_{i}", [128, 4], f) for i in range(2)]
        s_o24 = [sbl(f"s_o24{i}", [128, 256], f) for i in range(2)]

        for b_ in range(BPC):
            em.add("vector", MEMSET(p_S[b_][0][:], 0.0))
            em.add("vector", MEMSET(p_S[b_][1][:], 0.0))
            em.add("vector", MEMSET(p_Sbf[b_][:], 0.0), inc=(sDV, 1))
        s_zero_dv = sDV.n

        NCHUNKS = int(os.environ.get("MK_P4_CHUNKS", 2 * NCH))
        sched = [(ci % 2, ci // 2) for ci in range(NCHUNKS)]
        prep_dma = [0] * NCHUNKS
        prep_pr = [0] * NCHUNKS
        prep_cgf = [0] * NCHUNKS
        prep_dv = [0] * NCHUNKS
        prep_ac = [0] * NCHUNKS
        ch_lastuse_te = [0] * NCHUNKS
        ch_o_spill = [0] * NCHUNKS
        ch_blend_ac = [0] * NCHUNKS
        ch_o_dv = [0] * NCHUNKS
        st = {"rb_free_ac": 0, "B2_free": 0, "B2_free_ac": 0, "b6_free": 0,
              "b6_free_ac": 0}

        def emit_prep(ci):
            b, c = sched[ci]
            par = ci % 2
            i_tile = b * NCH + c
            row0 = b * T + c * CH
            if ci >= 2:
                em.wait("sync", sTE, ch_lastuse_te[ci - 2])
                em.wait("sync", sDV, ch_o_dv[ci - 2])
            em.dma("sync", s_qc[par][:],
                   qT_d.rearrange("(hp p) t -> p hp t", hp=16, p=128)[:, :, row0:row0 + 128],
                   dT[par])
            em.dma("sync", s_kc[par][:],
                   kT_d.rearrange("(hp p) t -> p hp t", hp=16, p=128)[:, :, row0:row0 + 128],
                   dT[par])
            em.dma("sync", s_kt[par][:], kT_d[:, row0:row0 + 128], dT[par], transpose=True)
            d1 = em.dma("sync", s_vk[par][:], v_d[row0:row0 + 128, :], dT[par])
            em.wait("tensor", sDV, st["b6_free"])
            em.wait("tensor", sAC, st["b6_free_ac"])
            em.add("tensor", TR(B6[0:32, 0:128], p_g[:, i_tile * 32:(i_tile + 1) * 32],
                                c_idf[:]), inc=(sTE, 1))
            gT_te = sTE.n
            em.wait("vector", sTE, gT_te)
            em.add("vector", CP(s_gT[par][:], B6[0:32, 0:128]))
            em.add("vector", MEMSET(s_cgT[par][:], 0.0))
            em.add("vector", SCAN(s_cgT[par][:], s_gT[par][:], s_gT[par][:], 0.0,
                                  ALU.add, ALU.bypass), inc=(sDV, 1))
            scan_dv = sDV.n
            em.wait("sync", sDV, scan_dv)
            em.dma("sync", bounce_cg[par], s_cgT[par][:], dPr[par])
            em.wait("sync", dPr[par], dPr[par].n)
            em.dma("sync", s_cgf[par][0:1, :],
                   bounce_cg[par].rearrange("h t -> (h t)"), dPr[par])
            cgf_dma = dPr[par].n
            em.wait("tensor", sDV, scan_dv)
            em.add("tensor", TR(B6[0:128, 128:160], s_cgT[par][:], c_idf[0:32, 0:32]), inc=(sTE, 1))
            cgtok_te = sTE.n
            em.wait("vector", sTE, cgtok_te)
            em.add("vector", CP(s_cgtok[par][:], B6[:, 128:160]))
            em.add("vector", TS(s_negcg[par][:], B6[:, 128:160], -1.0, 0.0, ALU.mult),
                   inc=(sDV, 1))
            cg_dv = sDV.n
            em.wait("scalar", sDV, cg_dv)
            em.add("scalar", ACT(s_ecg[par][:], s_cgtok[par][:], AF.Exp), inc=(sAC, 1))
            ecg_ac = sAC.n
            em.wait("sync", sAC, ecg_ac)
            em.dma("sync", bounce_bl[par], s_ecg[par][127:128, 0:32], dPr[par])
            em.wait("sync", dPr[par], dPr[par].n)
            em.dma("sync", s_blr[par][0:1, :], bounce_bl[par], dPr[par])
            blr_dma = dPr[par].n
            em.wait("tensor", dPr[par], blr_dma)
            em.wait("tensor", sDV, cg_dv)
            em.add("tensor", MM(B6[:, 160:192], c_ones[0:1, :], s_blr[par][0:1, :]),
                   inc=(sTE, 1))
            bl_te = sTE.n
            em.wait("vector", sTE, bl_te)
            em.add("vector", CP(s_BL[par][:], B6[:, 160:192]), inc=(sDV, 1))
            prep_dv[ci] = sDV.n
            prep_ac[ci] = ecg_ac
            prep_dma[ci] = d1
            prep_pr[ci] = blr_dma
            prep_cgf[ci] = cgf_dma
            st["b6_free"] = sDV.n

        NG = NCHUNKS * 8

        def ginfo(g):
            ci = g // 8
            b, c = sched[ci]
            return ci, g % 8, b, c, ci % 2, g % 2

        GS = {-1: {"kq_dv": 0, "t1_dv": 0, "bt_dv": 0, "blend_dv": 0, "o_dv": 0,
                   "exp_ac": 0},
              -2: {"blend_dv": 0, "o_dv": 0}}

        def hsl(buf, h):
            return buf[64 * (h % 2):64 * (h % 2) + 64, h // 2, :]

        def stile(buf, h):
            return buf[64 * (h % 2):64 * (h % 2) + 64,
                       (h // 2) * 64:(h // 2) * 64 + 64]

        def emit_front(g):
            ci, grp, b, c, par, gpar = ginfo(g)
            i_tile = b * NCH + c
            heads = [4 * grp + j for j in range(4)]
            gs = GS[g] = {}
            Sbf = p_Sbf[b]
            em.wait("tensor", sDV, GS[g - 2]["blend_dv"])
            em.wait("tensor", sAC, st["rb_free_ac"])
            em.wait("tensor", dT[par], prep_dma[ci])
            em.wait("tensor", dPr[par], prep_cgf[ci])
            em.add("tensor", MM(B7[:], c_ones[0:1, :],
                                s_cgf[par][0:1, grp * 512:(grp + 1) * 512]), inc=(sTE, 1))
            rb_te = sTE.n
            em.wait("tensor", sDV, GS[g - 1]["kq_dv"])
            em.wait("tensor", sDV, s_zero_dv)
            em.wait("tensor", sDV, GS[g - 1]["t1_dv"])
            if ci >= 2:
                em.wait("tensor", sAC, ch_blend_ac[ci - 2])
            for j, h in enumerate(heads):
                em.add("tensor", MM(([B0, B1][j % 2])[:, (j // 2) * 128:(j // 2) * 128 + 128],
                                    hsl(s_kc[par], h), hsl(s_kc[par], h), start=(j < 2)))
            for j, h in enumerate(heads):
                em.add("tensor",
                       MM(([B0, B1][j % 2])[:, 256 + (j // 2) * 128:256 + (j // 2) * 128 + 128],
                          hsl(s_kc[par], h), hsl(s_qc[par], h), start=False))
            for j, h in enumerate(heads):
                em.add("tensor", MM(([B4, B5][j % 2])[:, (j // 2) * 64:(j // 2) * 64 + 64],
                                    hsl(s_kc[par], h), stile(Sbf, h), start=(j < 2)))
            for j, h in enumerate(heads):
                em.add("tensor",
                       MM(([B4, B5][j % 2])[:, 128 + (j // 2) * 64:128 + (j // 2) * 64 + 64],
                          hsl(s_qc[par], h), stile(Sbf, h), start=False),
                       inc=(sTE, 1) if j == 3 else None)
            l0_te = sTE.n
            yield
            em.wait("scalar", sTE, rb_te)
            em.wait("scalar", sDV, prep_dv[ci])
            em.wait("scalar", sDV, GS[g - 2]["o_dv"])
            for j, h in enumerate(heads):
                em.add("scalar", ACT(s_dtr[gpar][:, j * 128:(j + 1) * 128],
                                     B7[:, j * 128:(j + 1) * 128], AF.Exp,
                                     bias=s_negcg[par][:, h:h + 1]),
                       inc=(sAC, 1) if j == 3 else None)
            gs["exp_ac"] = sAC.n
            st["rb_free_ac"] = gs["exp_ac"]
            em.wait("vector", sAC, gs["exp_ac"])
            em.wait("vector", sDV, GS[g - 2]["o_dv"])
            for j in range(4):
                em.add("vector", TT(s_dt4[gpar][:, j * 128:(j + 1) * 128],
                                    s_dtr[gpar][:, j * 128:(j + 1) * 128],
                                    c_umask[:], ALU.min),
                       inc=(sDV, 1) if j == 3 else None)
            dtm_dv = gs["dtm_dv"] = sDV.n
            rv = em.relay("gpsimd", sDV, dtm_dv, rlyG)
            em.wait("vector", sTE, l0_te)
            em.add("vector", CP(s_kkr[gpar][:], B0[:]))
            em.add("vector", CP(s_qkr[gpar][:], B1[:]), inc=(sDV, 1))
            kq_dv = gs["kq_dv"] = sDV.n
            em.wait("vector", rlyG, rv)
            for j in range(4):
                em.add("vector", CP(s_w4[gpar][:, j:j + 1],
                                    s_dt4[gpar][:, j * 128 + 127:j * 128 + 128]))
            em.wait("vector", sDV, prep_dv[ci])
            em.wait("vector", sAC, prep_ac[ci])
            for j, h in enumerate(heads):
                em.add("vector", STT(s_r14[gpar][:, j * 64:(j + 1) * 64],
                                     [B4, B5][j % 2][:, (j // 2) * 64:(j // 2) * 64 + 64],
                                     s_ecg[par][:, h:h + 1],
                                     s_vk[par][:, h * 64:(h + 1) * 64],
                                     ALU.mult, ALU.subtract),
                       inc=(sDV, 1) if j == 3 else None)
            r1_dv = sDV.n
            rv = em.relay("gpsimd", sDV, r1_dv, rlyG)
            em.wait("vector", rlyG, rv)
            for j, h in enumerate(heads):
                em.add("vector", TS(s_R4[gpar][:, j * 64:(j + 1) * 64],
                                    s_r14[gpar][:, j * 64:(j + 1) * 64],
                                    p_negb[:, i_tile * 32 + h:i_tile * 32 + h + 1], 0.0,
                                    ALU.mult), inc=(sDV, 1) if j == 3 else None)
            gs["r_dv"] = sDV.n
            for j, h in enumerate(heads):
                em.add("vector", TS(s_o14[gpar][:, j * 64:(j + 1) * 64],
                                    [B4, B5][j % 2][:, 128 + (j // 2) * 64:
                                                    128 + (j // 2) * 64 + 64],
                                    s_ecg[par][:, h:h + 1], 0.0, ALU.mult),
                       inc=(sDV, 1) if j == 3 else None)
            gs["t1_dv"] = sDV.n
            yield
            em.wait("tensor", sDV, dtm_dv)
            em.wait("tensor", sDV, kq_dv)
            em.wait("tensor", sDV, GS[g - 1]["bt_dv"])
            for j in range(4):
                em.add("tensor", TR(B3[:, j * 128:(j + 1) * 128],
                                    s_dt4[gpar][:, j * 128:(j + 1) * 128], c_idbf[:]),
                       inc=(sTE, 1) if j == 3 else None)
            dtT_te = sTE.n
            em.wait("vector", sTE, dtT_te)
            em.add("vector", TT(s_D4[gpar][:], B3[:], c_negi4[:], ALU.add),
                   inc=(sDV, 1))
            d_dv = sDV.n
            rv = em.relay("gpsimd", sDV, d_dv, rlyG)
            em.wait("vector", rlyG, rv)
            for j, h in enumerate(heads):
                em.add("vector", STT(s_bn4[gpar][:, j * 128:(j + 1) * 128],
                                     [s_kkr, s_qkr][j % 2][gpar]
                                     [:, (j // 2) * 128:(j // 2) * 128 + 128],
                                     p_negb[:, i_tile * 32 + h:i_tile * 32 + h + 1],
                                     s_D4[gpar][:, j * 128:(j + 1) * 128],
                                     ALU.mult, ALU.mult),
                       inc=(sDV, 1) if j == 3 else None)
            gs["bn_dv"] = sDV.n
            em.wait("gpsimd", sDV, kq_dv)
            em.wait("gpsimd", sDV, dtm_dv)
            for j in range(4):
                em.add("gpsimd", TT(s_dtr[gpar][:, j * 128:(j + 1) * 128],
                                    [s_kkr, s_qkr][j % 2][gpar]
                                    [:, 256 + (j // 2) * 128:256 + (j // 2) * 128 + 128],
                                    s_dt4[gpar][:, j * 128:(j + 1) * 128], ALU.mult),
                       inc=(sGP, 1) if j == 3 else None)
            gs["p2_gp"] = sGP.n
            yield
            em.wait("tensor", sDV, gs["bn_dv"])
            for j in range(4):
                em.add("tensor", TR(B3[:, j * 128:(j + 1) * 128],
                                    s_bn4[gpar][:, j * 128:(j + 1) * 128], c_idbf[:]),
                       inc=(sTE, 1) if j == 3 else None)
            bnT_te = sTE.n
            em.wait("vector", sTE, bnT_te)
            em.add("vector", CP(s_bt4[gpar][:], B3[:]))
            em.add("vector", TT(s_hh[gpar][0][:], B3[:], c_i4[:], ALU.add),
                   inc=(sDV, 1))
            gs["bt_dv"] = sDV.n

        def emit_back(g):
            ci, grp, b, c, par, gpar = ginfo(g)
            i_tile = b * NCH + c
            heads = [4 * grp + j for j in range(4)]
            gs = GS[g]
            Scur = p_S[b][c % 2]
            Snext = p_S[b][(c + 1) % 2]
            Sbf = p_Sbf[b]
            tpow_prev, pow_prev = s_bt4[gpar], s_bn4[gpar]
            h_prev = s_hh[gpar][0]
            ev_prev = ("vector", gs["bt_dv"])
            tp_ac = 0
            for lvl in range(1, 7):
                lp = lvl % 2
                em.wait("tensor", sDV if ev_prev[0] == "vector" else sAC, ev_prev[1])
                em.wait("tensor", sDV, st["B2_free"])
                em.wait("tensor", sAC, st["B2_free_ac"])
                if tp_ac:
                    em.wait("tensor", sAC, tp_ac)
                if lvl < 6:
                    em.wait("tensor", sAC, st["b6_free_ac"])
                    em.wait("tensor", sDV, st["b6_free"])
                    for j in range(4):
                        em.add("tensor", MM(B6[:, j * 128:(j + 1) * 128],
                                            pow_prev[:, j * 128:(j + 1) * 128],
                                            tpow_prev[:, j * 128:(j + 1) * 128],
                                            start=(j == 0)))
                for j in range(4):
                    em.add("tensor", MM(B2[:, j * 128:(j + 1) * 128],
                                        tpow_prev[:, j * 128:(j + 1) * 128],
                                        pow_prev[:, j * 128:(j + 1) * 128],
                                        start=(j == 0)),
                           inc=(sTE, 1) if j == 3 else None)
                pw_te = sTE.n
                em.wait("scalar", sTE, pw_te)
                em.add("scalar", lambda e, a=s_pw[gpar][lp], b2=B2: e.copy(
                    a[:], b2[:]), inc=(sAC, 1))
                pow_ac = sAC.n
                if lvl < 6:
                    em.add("scalar", lambda e, a=s_tp[gpar][lp], b2=B6: e.copy(
                        a[:], b2[:]), inc=(sAC, 1))
                    tp_ac = sAC.n
                    st["b6_free_ac"] = tp_ac
                em.wait("tensor", sAC, pow_ac)
                for j in range(4):
                    em.add("tensor", MM(B2[:, j * 128:(j + 1) * 128], c_idbf[:],
                                        h_prev[:, j * 128:(j + 1) * 128],
                                        start=(j == 0), stop=False))
                    em.add("tensor", MM(B2[:, j * 128:(j + 1) * 128],
                                        s_pw[gpar][lp][:, j * 128:(j + 1) * 128],
                                        h_prev[:, j * 128:(j + 1) * 128],
                                        start=False, stop=True),
                           inc=(sTE, 1) if j == 3 else None)
                h_te = sTE.n
                em.wait("scalar", sTE, h_te)
                em.add("scalar", lambda e, a=s_hh[gpar][lp], b2=B2: e.copy(
                    a[:], b2[:]), inc=(sAC, 1))
                h_ac = sAC.n
                st["B2_free_ac"] = h_ac
                h_prev = s_hh[gpar][lp]
                tpow_prev, pow_prev = s_tp[gpar][lp], s_pw[gpar][lp]
                ev_prev = ("scalar", h_ac)
                yield
            em.wait("tensor", sDV, gs["r_dv"])
            em.wait("tensor", sAC, ev_prev[1])
            em.wait("tensor", sDV, st["B2_free"])
            em.wait("tensor", sAC, st["B2_free_ac"])
            for j in range(4):
                em.add("tensor", MM(B2[:, j * 64:(j + 1) * 64],
                                    h_prev[:, j * 128:(j + 1) * 128],
                                    s_R4[gpar][:, j * 64:(j + 1) * 64],
                                    start=(j == 0)),
                       inc=(sTE, 1) if j == 3 else None)
            u_te = sTE.n
            em.wait("vector", sTE, u_te)
            em.add("vector", CP(s_U4[gpar][:], B2[:, 0:256]))
            for j, h in enumerate(heads):
                em.add("vector", TS(s_Ub4[gpar][:, j * 64:(j + 1) * 64],
                                    B2[:, j * 64:(j + 1) * 64],
                                    s_w4[gpar][:, j:j + 1], 0.0,
                                    ALU.mult), inc=(sDV, 1) if j == 3 else None)
            u_dv = sDV.n
            yield
            em.wait("tensor", sGP, gs["p2_gp"])
            em.wait("tensor", sDV, u_dv)
            for j in range(4):
                em.add("tensor", MM(B2[:, 256 + j * 64:256 + (j + 1) * 64],
                                    s_dtr[gpar][:, j * 128:(j + 1) * 128],
                                    s_U4[gpar][:, j * 64:(j + 1) * 64],
                                    start=False))
            if g + 1 < NG:
                em.wait("tensor", sAC, GS[g + 1]["exp_ac"])
            for j, h in enumerate(heads):
                em.add("tensor", MM(B7[64 * (h % 2):64 * (h % 2) + 64,
                                       j * 64:(j + 1) * 64],
                                    s_kt[par][:, h * 64:(h + 1) * 64],
                                    s_Ub4[gpar][:, j * 64:(j + 1) * 64],
                                    start=(j < 2),
                                    tile_position=(0, 64 * (h % 2))),
                       inc=(sTE, 1) if j == 3 else None)
            oi_te = sTE.n
            ch_lastuse_te[ci] = oi_te
            em.wait("vector", sTE, oi_te)
            if ci >= 2:
                em.wait("vector", dOsp[par], ch_o_spill[ci - 2])
            for j, h in enumerate(heads):
                em.add("vector", STT(s_o24[gpar][:, j * 64:(j + 1) * 64],
                                     B2[:, 256 + j * 64:256 + (j + 1) * 64], 1.0,
                                     s_o14[gpar][:, j * 64:(j + 1) * 64],
                                     ALU.mult, ALU.add),
                       inc=(sDV, 1) if j == 3 else None)
            o1_dv = sDV.n
            rv = em.relay("gpsimd", sDV, o1_dv, rlyG)
            em.wait("vector", rlyG, rv)
            for j, h in enumerate(heads):
                em.add("vector", STT(s_oa[par][:, h * 64:(h + 1) * 64],
                                     s_vk[par][:, h * 64:(h + 1) * 64],
                                     c_dD[:, h:h + 1],
                                     s_o24[gpar][:, j * 64:(j + 1) * 64],
                                     ALU.mult, ALU.add),
                       inc=(sDV, 1) if j == 3 else None)
            gs["o_dv"] = sDV.n
            st["B2_free"] = o1_dv
            yield
            for j, h in enumerate(heads):
                em.add("vector", STT(stile(Snext, h), stile(Scur, h),
                                     s_BL[par][64 * (h % 2):64 * (h % 2) + 64, h:h + 1],
                                     B7[64 * (h % 2):64 * (h % 2) + 64,
                                        j * 64:(j + 1) * 64],
                                     ALU.mult, ALU.add),
                       inc=(sDV, 1) if j == 3 else None)
            gs["blend_dv"] = sDV.n
            em.wait("scalar", sDV, gs["blend_dv"])
            for j, h in enumerate(heads):
                em.add("scalar",
                       lambda e, a=stile(Sbf, h), bb2=stile(Snext, h): e.copy(a, bb2),
                       inc=(sAC, 1) if j == 3 else None)
            if grp == 7:
                ch_blend_ac[ci] = sAC.n
                ch_o_dv[ci] = gs["o_dv"]

        emit_prep(0)
        if NCHUNKS > 1:
            emit_prep(1)
        for dst_, src_ in [(wg_bf, wgT), (wo_bf, woT)]:
            em.dma("gpsimd", dst_, src_, dCgw)
        for e_ in range(E):
            em.dma("gpsimd", wf_bf[e_], wffn[e_], dCff)
        cast_gwo = dCgw.n
        cast_all = dCff.n
        for g in range(NG + 1):
            fgen = emit_front(g) if g < NG else None
            bgen = emit_back(g - 1) if g >= 1 else None

            def nf():
                if fgen is not None:
                    next(fgen, None)

            def nb():
                if bgen is not None:
                    next(bgen, None)
            nf()
            nb()
            nf()
            nb()
            nf()
            nb()
            nf()
            nb()
            nb()
            nb()
            nb()
            nb()
            nb()
            if g >= 8 and g % 8 == 0:
                ci_done = g // 8 - 1
                bsp, csp = sched[ci_done]
                parsp = ci_done % 2
                row0 = bsp * T + csp * CH
                em.wait("sync", sDV, ch_o_dv[ci_done])
                ch_o_spill[ci_done] = em.dma("sync", o_d[row0:row0 + 128, :],
                                             s_oa[parsp][:], dOsp[parsp])
                if ci_done + 2 < NCHUNKS:
                    emit_prep(ci_done + 2)
        for sm_ in (dT[0], dT[1], dPr[0], dPr[1], dOsp[0], dOsp[1]):
            em.wait("sync", sm_, sm_.n)
        em.wait("sync", sDV, sDV.n)
        em.wait("sync", sAC, sAC.n)
        em.wait("sync", sGP, sGP.n)
        em.play("p4")
        ph.close()
    stack2.close()

    if PHASES >= 5:
        ph = ExitStack()
        sbl = lambda name, shape, dt: ph.enter_context(nc.sbuf_tensor(name, shape, dt))
        psl = lambda name, shape, dt: ph.enter_context(nc.psum_tensor(name, shape, dt))
        SP5 = 1024
        NSP5 = TOK // SP5
        TPS5 = SP5 // 128
        PG = [[psl(f"p5_G{i}{j}", [128, 512], f) for j in range(2)] for i in range(2)]
        PSS = psl("p5_SS", [128, 512], f)
        PRB = psl("p5_RB", [128, 512], f)
        PW = [psl(f"p5_W{i}", [128, 512], f) for i in range(2)]
        s_x5 = sbl("s_x5", [128, 16, SP5], bf)
        s_oT5 = sbl("s_oT5", [128, 16, SP5], bf)
        s_o55 = sbl("s_o55", [128, 16, SP5], bf)
        s_sq5 = sbl("s_sq5", [128, SP5], bf)
        s_r2b = sbl("s_r2b", [2, 16, SP5], bf)
        s_r25 = sbl("s_r25", [2, 512], f)
        s_wg5 = [sbl(f"s_wg5_{i}", [128, 16, 128], bf) for i in range(2)]
        s_wo5 = [sbl(f"s_wo5_{i}", [128, 16, 512], bf) for i in range(2)]
        s_a5 = sbl("s_a5", [128, SP5], f)
        s_sg5 = sbl("s_sg5", [128, SP5], f)
        s_t25 = sbl("s_t25", [128, 512], f)
        s_ev5 = [sbl(f"s_ev5_{i}", [128, 512], f) for i in range(2)]
        dO5 = [em.sem("dO5a"), em.sem("dO5b")]
        dWg5 = [em.sem("dWg5a"), em.sem("dWg5b")]
        dWoQ = [em.sem("dWoQa"), em.sem("dWoQb")]
        dX5 = em.sem("dX5")
        dSd = [em.sem("dSda"), em.sem("dSdb")]
        em.wait("sync", dCgw, cast_gwo)

        sq_ac = [0] * 16
        ss_te = [[0, 0] for _ in range(16)]
        rstd_ac = [[0, 0] for _ in range(16)]
        gate_te = [[0, 0] for _ in range(16)]
        rb_te5 = [[0, 0] for _ in range(16)]
        a_dv = [[0, 0] for _ in range(16)]
        o5_dv = [[0, 0] for _ in range(16)]
        prev_rbte = [0] * 16
        prev_adv = [0] * 16
        prev_gate_all = 0
        wo_q_te = [0, 0, 0, 0]
        prev_wo_q_te = [0, 0, 0, 0]
        wg_ld5 = [0, 0]
        o_ld5 = [0, 0]
        woq_ld = [0, 0, 0, 0]
        ev_dv5 = [0, 0]
        sd_dma = [0, 0]
        for sp in range(NSP5):
            rows0 = sp * SP5
            em.wait("sync", sTE, prev_gate_all)
            x5_ld = em.dma("sync", s_x5[:], xnT_d[:, :, rows0:rows0 + SP5], dX5)
            for it in range(18):
                d2 = it
                e2 = it - 2
                par = d2 % 2
                ep = e2 % 2
                if d2 < 16:
                    if d2 >= 2:
                        em.wait("sync", sAC, sq_ac[d2 - 2])
                    if sp > 0:
                        em.wait("sync", sDV, prev_adv[d2])
                    o_ld5[par] = em.dma("sync", s_oT5[:, d2, :],
                                        o_d[rows0:rows0 + SP5, d2 * 128:(d2 + 1) * 128],
                                        dO5[par], transpose=True)
                if 0 <= e2 < 16:
                    if e2 >= 2:
                        em.wait("sync", sTE, gate_te[e2 - 2][1])
                    q5, c0 = e2 // 4, (e2 % 4) * 128
                    wg_ld5[ep] = em.dma(
                        "sync", s_wg5[ep][:],
                        wg_bf[q5].rearrange("kb p c -> p kb c")[:, :, c0:c0 + 128],
                        dWg5[ep])
                if d2 in (8, 12):
                    qq = (d2 - 8) // 4
                    em.wait("sync", sTE, prev_wo_q_te[qq + 2])
                    woq_ld[qq] = em.dma("sync", s_wo5[qq % 2][:],
                                        wo_bf[qq].rearrange("kb p c -> p kb c"),
                                        dWoQ[qq % 2])
                if 1 <= d2 <= 16:
                    dd = d2 - 1
                    em.wait("scalar", sTE, ss_te[dd][1])
                    em.wait("scalar", sTE, prev_rbte[dd])
                    em.add("scalar", ACT(s_r25[:], PSS[0:2, 0:512], AF.Ln,
                                         bias=c_eps6r[0:2, 0:1]))
                    em.add("scalar", lambda e, dd=dd: e.activation(
                        out=s_r2b[:, dd, 512:1024], in_=s_r25[:], func=AF.Exp,
                        scale=-0.5), inc=(sAC, 1))
                    rstd_ac[dd][1] = sAC.n
                if d2 < 16:
                    em.wait("scalar", dO5[par], o_ld5[par])
                    if d2 >= 1:
                        em.wait("scalar", sTE, ss_te[d2 - 1][1])
                    em.add("scalar", ACT(s_sq5[:], s_oT5[:, d2, :], AF.Square),
                           inc=(sAC, 1))
                    sq_ac[d2] = sAC.n
                if 0 <= e2 < 16:
                    em.wait("tensor", sAC, rstd_ac[e2][0])
                    em.wait("tensor", sDV, a_dv[e2 - 1][1] if e2 >= 1 else prev_adv[15])
                    em.add("tensor", MM(PRB[:], c_obd64T[:], s_r2b[:, e2, 0:512]),
                           inc=(sTE, 1))
                    rb_te5[e2][0] = sTE.n
                    em.wait("tensor", dX5, x5_ld)
                    em.wait("tensor", dWg5[ep], wg_ld5[ep])
                    em.wait("tensor", sDV, o5_dv[e2 - 2][0])
                    for kb in range(16):
                        em.add("tensor", MM(PG[ep][0][:], s_wg5[ep][:, kb, :],
                                            s_x5[:, kb, 0:512],
                                            start=(kb == 0), stop=(kb == 15)),
                               inc=(sTE, 1) if kb == 15 else None)
                    gate_te[e2][0] = sTE.n
                if d2 < 16:
                    em.wait("tensor", sAC, sq_ac[d2])
                    em.wait("tensor", sAC, rstd_ac[d2 - 1][1])
                    em.add("tensor", MM(PSS[0:2, 0:512], c_obd64[:], s_sq5[:, 0:512]),
                           inc=(sTE, 1))
                    ss_te[d2][0] = sTE.n
                if 0 <= e2 < 16:
                    em.wait("vector", sTE, rb_te5[e2][0])
                    em.add("vector", STT(s_a5[:, 0:512], PRB[:], c_onc[:, 0:1],
                                         s_oT5[:, e2, 0:512], ALU.mult, ALU.mult),
                           inc=(sDV, 1))
                    a_dv[e2][0] = sDV.n
                if 0 <= e2 < 16:
                    em.wait("scalar", sDV, a_dv[e2][0])
                    em.add("scalar", ACT(s_sg5[:, 0:512], s_a5[:, 0:512], AF.Exp,
                                         scale=-1.0))
                    em.add("scalar", ACT(s_sg5[:, 0:512], s_sg5[:, 0:512], AF.Ln,
                                         bias=c_one[:, 0:1]))
                    em.add("scalar", ACT(s_sg5[:, 0:512], s_sg5[:, 0:512], AF.Exp,
                                         scale=-1.0), inc=(sAC, 1))
                    sg0_ac = sAC.n
                if d2 < 16:
                    em.wait("scalar", sTE, ss_te[d2][0])
                    em.add("scalar", ACT(s_r25[:], PSS[0:2, 0:512], AF.Ln,
                                         bias=c_eps6r[0:2, 0:1]))
                    em.add("scalar", lambda e, d2=d2: e.activation(
                        out=s_r2b[:, d2, 0:512], in_=s_r25[:], func=AF.Exp,
                        scale=-0.5), inc=(sAC, 1))
                    rstd_ac[d2][0] = sAC.n
                if 0 <= e2 < 16:
                    em.wait("tensor", sAC, rstd_ac[e2][1])
                    em.wait("tensor", sDV, a_dv[e2][0])
                    em.add("tensor", MM(PRB[:], c_obd64T[:], s_r2b[:, e2, 512:1024]),
                           inc=(sTE, 1))
                    rb_te5[e2][1] = sTE.n
                    em.wait("tensor", sDV, o5_dv[e2 - 2][1])
                    for kb in range(16):
                        em.add("tensor", MM(PG[ep][1][:], s_wg5[ep][:, kb, :],
                                            s_x5[:, kb, 512:1024],
                                            start=(kb == 0), stop=(kb == 15)),
                               inc=(sTE, 1) if kb == 15 else None)
                    gate_te[e2][1] = sTE.n
                if d2 < 16:
                    em.wait("tensor", sAC, rstd_ac[d2][0])
                    em.add("tensor", MM(PSS[0:2, 0:512], c_obd64[:], s_sq5[:, 512:1024]),
                           inc=(sTE, 1))
                    ss_te[d2][1] = sTE.n
                if 0 <= e2 < 16:
                    em.wait("vector", sAC, sg0_ac)
                    em.add("vector", TT(s_t25[:], s_a5[:, 0:512], s_sg5[:, 0:512],
                                        ALU.mult))
                    em.wait("vector", sTE, gate_te[e2][0])
                    em.add("vector", TT(s_o55[:, e2, 0:512], PG[ep][0][:], s_t25[:],
                                        ALU.mult), inc=(sDV, 1))
                    o5_dv[e2][0] = sDV.n
                    em.wait("vector", sTE, rb_te5[e2][1])
                    em.add("vector", STT(s_a5[:, 512:1024], PRB[:], c_onc[:, 0:1],
                                         s_oT5[:, e2, 512:1024], ALU.mult, ALU.mult),
                           inc=(sDV, 1))
                    a_dv[e2][1] = sDV.n
                    em.wait("scalar", sDV, a_dv[e2][1])
                    em.add("scalar", ACT(s_sg5[:, 512:1024], s_a5[:, 512:1024], AF.Exp,
                                         scale=-1.0))
                    em.add("scalar", ACT(s_sg5[:, 512:1024], s_sg5[:, 512:1024], AF.Ln,
                                         bias=c_one[:, 0:1]))
                    em.add("scalar", ACT(s_sg5[:, 512:1024], s_sg5[:, 512:1024], AF.Exp,
                                         scale=-1.0), inc=(sAC, 1))
                    sg1_ac = sAC.n
                    em.wait("vector", sAC, sg1_ac)
                    em.add("vector", TT(s_t25[:], s_a5[:, 512:1024], s_sg5[:, 512:1024],
                                        ALU.mult))
                    em.wait("vector", sTE, gate_te[e2][1])
                    em.add("vector", TT(s_o55[:, e2, 512:1024], PG[ep][1][:], s_t25[:],
                                        ALU.mult), inc=(sDV, 1))
                    o5_dv[e2][1] = sDV.n
            prev_gate_all = sTE.n
            prev_rbte = [rb_te5[i][1] for i in range(16)]
            prev_adv = [a_dv[i][1] for i in range(16)]
            em.wait("tensor", sDV, o5_dv[15][1])
            for q in range(4):
                qp = q % 2
                em.wait("tensor", dWoQ[qp], woq_ld[q])
                for tt in range(TPS5):
                    bank = PW[tt % 2]
                    em.wait("tensor", sDV, ev_dv5[tt % 2])
                    for d2 in range(16):
                        em.add("tensor", MM(bank[:], s_o55[:, d2, tt * 128:(tt + 1) * 128],
                                            s_wo5[qp][:, d2, :],
                                            start=(d2 == 0), stop=(d2 == 15)),
                               inc=(sTE, 1) if d2 == 15 else None)
                    rte = sTE.n
                    em.wait("vector", sTE, rte)
                    em.wait("vector", dSd[tt % 2], sd_dma[tt % 2])
                    em.add("vector", CP(s_ev5[tt % 2][:], bank[:]), inc=(sDV, 1))
                    ev_dv5[tt % 2] = sDV.n
                    em.wait("sync", sDV, ev_dv5[tt % 2])
                    sd_dma[tt % 2] = em.dma(
                        "sync", s_d[rows0 + tt * 128:rows0 + (tt + 1) * 128,
                                    q * 512:(q + 1) * 512],
                        s_ev5[tt % 2][:], dSd[tt % 2])
                wo_q_te[q] = sTE.n
                if q + 2 < 4:
                    em.wait("sync", sTE, wo_q_te[q])
                    woq_ld[q + 2] = em.dma("sync", s_wo5[q % 2][:],
                                           wo_bf[q + 2].rearrange("kb p c -> p kb c"),
                                           dWoQ[q % 2])
            prev_wo_q_te = list(wo_q_te)
        for sm_ in (dO5[0], dO5[1], dWg5[0], dWg5[1], dWoQ[0], dWoQ[1], dX5,
                    dSd[0], dSd[1]):
            em.wait("sync", sm_, sm_.n)
        em.wait("sync", sDV, sDV.n)
        em.wait("sync", sAC, sAC.n)
        em.wait("sync", sTE, sTE.n)
        em.play("p5")
        ph.close()

    if PHASES >= 6:
        ph = ExitStack()
        sbl = lambda name, shape, dt: ph.enter_context(nc.sbuf_tensor(name, shape, dt))
        psl = lambda name, shape, dt: ph.enter_context(nc.psum_tensor(name, shape, dt))
        SP6 = 1024
        NSP6 = TOK // SP6
        TPS6 = SP6 // 128
        PB = [psl(f"p6_B{i}", [128, 512], f) for i in range(8)]
        s_x6 = sbl("s_x6", [128, 16, SP6], bf)
        s_w6 = [sbl(f"s_w6_{i}", [128, 16, 512], bf) for i in range(2)]
        s_acc = sbl("s_acc", [128, TPS6 * 2048], f)
        s_cbT = sbl("s_cbT", [8, TPS6 * 128], f)
        c_fb6 = sbl("c_fb6", [8, 2048], f)
        s_sg6 = sbl("s_sg6", [128, 2048], f)
        s_xr6 = [sbl(f"s_xr6_{i}", [128, 2048], f) for i in range(2)]
        s_sr6 = [sbl(f"s_sr6_{i}", [128, 2048], f) for i in range(2)]
        s_ob6 = [sbl(f"s_ob6_{i}", [128, 2048], f) for i in range(2)]
        dX6 = em.sem("dX6")
        dW6 = [em.sem("dW6a"), em.sem("dW6b")]
        dRes = [em.sem("dResa"), em.sem("dResb")]
        dOut = [em.sem("dOuta"), em.sem("dOutb")]
        em.wait("sync", dCff, cast_all)
        fb_ld = em.dma("sync", c_fb6[:], fb, dX6)

        def accv(tt, q):
            return s_acc[:, tt * 2048 + q * 512:tt * 2048 + (q + 1) * 512]

        x6war_te = 0
        prev_ev_all = 0
        w6_ld = [0, 0]
        round_te = [0] * 32
        res_ld = [0, 0]
        out_dma6 = [0, 0]
        res_war = [0, 0]
        gtt_dv = 0
        for sp in range(NSP6):
            rows0 = sp * SP6
            i0 = sp * TPS6
            em.wait("sync", sTE, x6war_te)
            x6_ld = em.dma("sync", s_x6[:], xnT_d[:, :, rows0:rows0 + SP6], dX6)
            for r0 in range(2):
                e0, q0 = r0 // 4, r0 % 4
                w6_ld[r0 % 2] = em.dma("sync", s_w6[r0 % 2][:],
                                       wf_bf[e0, q0].rearrange("kb p c -> p kb c"),
                                       dW6[r0 % 2])
            em.wait("tensor", sDV, prev_ev_all)
            for tt in range(TPS6):
                em.add("tensor", TR(PB[tt][0:8, 0:128],
                                    p_comb[:, (i0 + tt) * 8:(i0 + tt + 1) * 8], c_idf[:]),
                       inc=(sTE, 1) if tt == TPS6 - 1 else None)
            cbt_te = sTE.n
            em.wait("vector", sTE, cbt_te)
            for tt in range(TPS6):
                em.add("vector", CP(s_cbT[:, tt * 128:(tt + 1) * 128], PB[tt][0:8, 0:128]),
                       inc=(sDV, 1) if tt == TPS6 - 1 else None)
            cbt_dv = sDV.n
            em.wait("tensor", sDV, cbt_dv)
            em.wait("tensor", dX6, fb_ld)
            seed_dv = 0
            for q in range(4):
                if seed_dv:
                    em.wait("tensor", sDV, seed_dv)
                for tt in range(TPS6):
                    em.add("tensor", MM(PB[tt][:], s_cbT[:, tt * 128:(tt + 1) * 128],
                                        c_fb6[:, q * 512:(q + 1) * 512]),
                           inc=(sTE, 1) if tt == TPS6 - 1 else None)
                seed_te = sTE.n
                em.wait("vector", sTE, seed_te)
                for tt in range(TPS6):
                    em.add("vector", CP(accv(tt, q), PB[tt][:]),
                           inc=(sDV, 1) if tt == TPS6 - 1 else None)
                seed_dv = sDV.n
            ev_half = [seed_dv, seed_dv]
            for e in range(E):
                for q in range(4):
                    r = e * 4 + q
                    wp = r % 2
                    em.wait("tensor", dW6[wp], w6_ld[wp])
                    em.wait("tensor", dX6, x6_ld)
                    for half in range(2):
                        em.wait("tensor", sDV, ev_half[half])
                        for kb in range(16):
                            for tt in range(4 * half, 4 * half + 4):
                                em.add("tensor",
                                       MM(PB[tt][:], s_x6[:, kb, tt * 128:(tt + 1) * 128],
                                          s_w6[wp][:, kb, :],
                                          start=(kb == 0), stop=(kb == 15)),
                                       inc=(sTE, 1) if (kb == 15 and tt == 4 * half + 3)
                                       else None)
                        hte = sTE.n
                        em.wait("vector", sTE, hte)
                        for tt in range(4 * half, 4 * half + 4):
                            em.add("vector",
                                   STT(accv(tt, q), PB[tt][:],
                                       p_comb[:, (i0 + tt) * 8 + e:(i0 + tt) * 8 + e + 1],
                                       accv(tt, q), ALU.mult, ALU.add),
                                   inc=(sDV, 1) if tt == 4 * half + 3 else None)
                        ev_half[half] = sDV.n
                    round_te[r] = sTE.n
                    if r + 2 < 32:
                        e2_, q2_ = (r + 2) // 4, (r + 2) % 4
                        em.wait("sync", sTE, round_te[r])
                        w6_ld[wp] = em.dma("sync", s_w6[wp][:],
                                           wf_bf[e2_, q2_].rearrange("kb p c -> p kb c"),
                                           dW6[wp])
            prev_ev_all = ev_half[1]
            x6war_te = round_te[31]
            prev_ob_dv = 0
            for tt in range(TPS6):
                par = tt % 2
                acct = s_acc[:, tt * 2048:(tt + 1) * 2048]
                em.wait("scalar", sDV, ev_half[1] if tt == 0 else prev_ob_dv)
                em.add("scalar", ACT(s_sg6[:], acct, AF.Exp, scale=-1.0))
                em.add("scalar", ACT(s_sg6[:], s_sg6[:], AF.Ln, bias=c_one[:, 0:1]))
                em.add("scalar", ACT(s_sg6[:], s_sg6[:], AF.Exp, scale=-1.0),
                       inc=(sAC, 1))
                sg_ac6 = sAC.n
                em.wait("sync", sDV, res_war[par])
                em.dma("sync", s_xr6[par][:], x_in[rows0 + tt * 128:rows0 + (tt + 1) * 128, :],
                       dRes[par])
                res_ld[par] = em.dma(
                    "sync", s_sr6[par][:],
                    s_d[rows0 + tt * 128:rows0 + (tt + 1) * 128, :], dRes[par])
                em.wait("vector", sAC, sg_ac6)
                em.add("vector", TT(s_sg6[:], acct, s_sg6[:], ALU.mult), inc=(sDV, 1))
                gtt_dv = sDV.n
                em.wait("vector", dRes[par], res_ld[par])
                em.wait("vector", dOut[par], out_dma6[par])
                em.add("vector", TT(s_ob6[par][:], s_sg6[:], s_sr6[par][:], ALU.mult))
                em.add("vector", TT(s_ob6[par][:], s_ob6[par][:], s_xr6[par][:], ALU.add),
                       inc=(sDV, 1))
                ob_dv = sDV.n
                res_war[par] = ob_dv
                prev_ob_dv = ob_dv
                em.wait("sync", sDV, ob_dv)
                out_dma6[par] = em.dma(
                    "sync", out_d[rows0 + tt * 128:rows0 + (tt + 1) * 128, :],
                    s_ob6[par][:], dOut[par])
        for sm_ in (dX6, dW6[0], dW6[1], dRes[0], dRes[1], dOut[0], dOut[1]):
            em.wait("sync", sm_, sm_.n)
        em.wait("sync", sDV, sDV.n)
        em.wait("sync", sAC, sAC.n)
        em.wait("sync", sTE, sTE.n)
        em.wait("sync", sGP, sGP.n)
        em.play("p6")
        ph.close()

    if PHASES < 6:
        ph = ExitStack()
        s_z = ph.enter_context(nc.sbuf_tensor("s_z", [128, D], F32))
        em.add("vector", MEMSET(s_z[:], 0.0), inc=(sDV, 1))
        em.wait("sync", sDV, sDV.n)
        for i in range(NT):
            em.dma("sync", out_d[i * 128:(i + 1) * 128, :], s_z[:], dS)
        em.wait("sync", dS, dS.n)
        em.play("stub")
        ph.close()

    stack.close()
    return nc


def kernel(**inputs):
    x = np.asarray(inputs["L0_attn_x"], np.float32)
    shared = host_prepare(inputs)
    nc = build_program()
    ncores = DEVCORES or NCORES
    in_maps = []
    for c in range(ncores):
        m = dict(shared)
        m["x"] = np.ascontiguousarray(x[c * BPC:(c + 1) * BPC].reshape(TOK, D))
        in_maps.append(m)
    res = run_bass_kernel_spmd(nc, in_maps, core_ids=list(range(ncores)))
    out = np.concatenate([np.asarray(r["out"]).reshape(BPC, T, D) for r in res.results], axis=0)
    if DEVCORES:
        out = np.concatenate([out] + [out[:BPC]] * (NCORES - DEVCORES), axis=0)
    if DEBUG:
        kernel.last_results = res.results
    return out


if __name__ == "__main__":
    nc = build_program()
    print("BUILD OK")
